# version 15
# baseline (speedup 1.0000x reference)
import os
import numpy as np
import ml_dtypes
import concourse.bass as bass
import concourse.mybir as mybir
from concourse.bass_utils import run_bass_kernel_spmd
from contextlib import ExitStack

F32 = mybir.dt.float32
BF16 = mybir.dt.bfloat16
AF = mybir.ActivationFunctionType
ALU = mybir.AluOpType
AX = mybir.AxisListType

T = 2048
D = 1024
NCH = 16
SCLEN = 512
NSC = 4
CPS = 4
INNER = 2048
NHM = 8
PD = 256
STATE = 64
PROJ = 4232
NHA = 16
NPAIR = 8
DFF = 4096
NEXP = 4
EPS = 1e-5
NEG = -1e30

ENGS = ("t", "v", "s", "g", "y")
DMAQ = ("g", "y")
NLANE = 8
SEMS = list(ENGS) + [f"{q}{i}" for q in DMAQ for i in range(NLANE)] + [f"{e}B" for e in ENGS]


class Buf:
    __slots__ = ("w", "r")

    def __init__(self):
        self.w = None
        self.r = {}


class Prog:
    def __init__(self):
        self.cnt = dict.fromkeys(SEMS, 0)
        self.wm = {c: dict.fromkeys(SEMS, 0) for c in ENGS}
        self.items = {e: [] for e in ENGS}
        self.lane_rr = dict.fromkeys(DMAQ, 0)

    def _wait(self, e, pe, v):
        if v <= 0 or v <= self.wm[e][pe]:
            return
        self.wm[e][pe] = v
        self.items[e].append(("w", pe, v))

    def wait_tok(self, e, tok):
        if tok is not None:
            self._wait(e, tok[0], tok[1])

    def do(self, e, fns, reads=(), writes=(), inc=None):
        if callable(fns):
            fns = (fns,)
        for b in reads:
            self.wait_tok(e, b.w)
        for b in writes:
            self.wait_tok(e, b.w)
            for pe, v in b.r.items():
                self._wait(e, pe, v)
        if e in DMAQ:
            lane = f"{e}{self.lane_rr[e]}"
            self.lane_rr[e] = (self.lane_rr[e] + 1) % NLANE
            self._wait(e, lane, self.cnt[lane])
            sem_key = lane
            inc_amt = 16
        else:
            sem_key = e
            inc_amt = 1 if inc is None else inc
        for fn in fns:
            self.items[e].append(("o", fn))
        self.cnt[sem_key] += inc_amt
        self.items[e].append(("i", sem_key, inc_amt))
        tok = (sem_key, self.cnt[sem_key])
        for b in reads:
            b.r[sem_key] = max(b.r.get(sem_key, 0), tok[1])
        for b in writes:
            b.w = tok
            b.r = {}
        return tok

    def barrier(self):
        toks = {}
        for e in ENGS:
            if e in DMAQ:
                for i in range(NLANE):
                    lane = f"{e}{i}"
                    self._wait(e, lane, self.cnt[lane])
            else:
                self._wait(e, e, self.cnt[e])
            b = f"{e}B"
            self.cnt[b] += 1
            self.items[e].append(("b", b, 1))
            toks[b] = self.cnt[b]
        for e in ENGS:
            for pe in ENGS:
                self._wait(e, f"{pe}B", toks[f"{pe}B"])


def _replay(P, e, eng, sems):
    last = None
    for it in P.items[e]:
        k = it[0]
        if k == "o":
            last = it[1](eng)
        elif k == "w":
            eng.wait_ge(sems[it[1]], it[2])
        elif k == "i":
            last.then_inc(sems[it[1]], it[2])
        elif k == "b":
            eng.nop().then_inc(sems[it[1]], it[2])


def host_consts():
    c = {}
    c["c_identf"] = np.eye(128, dtype=np.float32)
    c["c_identb"] = np.eye(128, dtype=np.float32).astype(ml_dtypes.bfloat16)
    c["c_ones1"] = np.ones((1, 128), dtype=np.float32)
    c["c_ones8"] = np.ones((8, 128), dtype=np.float32)
    hs = np.zeros((128, 2), dtype=np.float32)
    hs[0:64, 0] = 1.0
    hs[64:128, 1] = 1.0
    c["c_hsel"] = hs
    tri = np.where(np.arange(128)[:, None] <= np.arange(128)[None, :], 0.0, NEG).astype(np.float32)
    c["c_maskb8"] = np.tile(tri, (1, 8))
    mb = np.zeros((8, 1024), dtype=np.float32)
    for h in range(8):
        mb[h, 128 * h:128 * (h + 1)] = 1.0
    c["c_mask8"] = mb
    c["c_cvec4"] = np.arange(4, dtype=np.float32).reshape(4, 1)
    c["c_ones65"] = np.ones((65, 128), dtype=np.float32)
    c["c_imm"] = np.array([[EPS], [1.0]], dtype=np.float32) * np.ones((1, 1), np.float32)
    c["c_imm"] = np.concatenate([np.full((128, 1), EPS, np.float32),
                                 np.full((128, 1), 1.0, np.float32)], axis=1)
    return c


KPH = int(os.environ.get("KPH", "99"))
_LASTP = None


def build_program():
    nc = bass.Bass("TRN2", target_bir_lowering=False)

    def dram(name, shape, dtype=F32, kind="ExternalInput"):
        return nc.dram_tensor(name, list(shape), dtype, kind=kind).ap()

    x_in = dram("x_in", (T, D))
    w_in = dram("w_in", (D, PROJ))
    b_in = dram("b_in", (PROJ,))
    a_log = dram("a_log", (NHM,))
    d_skip = dram("d_skip", (NHM,))
    dt_bias = dram("dt_bias", (NHM,))
    w_out_d = dram("w_out", (INNER, D))
    b_out = dram("b_out", (D,))
    ln_g = dram("ln_g", (D,))
    ln_b = dram("ln_b", (D,))
    wq = dram("wq", (D, D))
    bq = dram("bq", (D,))
    wk = dram("wk", (D, D))
    bk = dram("bk", (D,))
    wv = dram("wv", (D, D))
    bv = dram("bv", (D,))
    wo = dram("wo", (D, D))
    bo = dram("bo", (D,))
    rw = dram("rw", (D, 1))
    rb = dram("rb", (1,))
    w1 = dram("w1", (NEXP, D, DFF))
    b1 = dram("b1", (NEXP, DFF))
    w2 = dram("w2", (NEXP, DFF, D))
    b2 = dram("b2", (NEXP, D))
    l1g = dram("l1g", (D,))
    l1b = dram("l1b", (D,))
    l2g = dram("l2g", (D,))
    l2b = dram("l2b", (D,))
    c_identf = dram("c_identf", (128, 128))
    c_identb = dram("c_identb", (128, 128), BF16)
    c_ones1 = dram("c_ones1", (1, 128))
    c_ones8 = dram("c_ones8", (8, 128))
    c_hsel = dram("c_hsel", (128, 2))
    c_maskb8 = dram("c_maskb8", (128, 1024))
    c_mask8 = dram("c_mask8", (8, 1024))
    c_cvec4 = dram("c_cvec4", (4, 1))
    c_ones65 = dram("c_ones65", (65, 128))
    c_imm = dram("c_imm", (128, 2))
    xn_scr = dram("xn_scr", (T, D), kind="Internal")
    part_scr = dram("part_scr", (T, D), kind="Internal")
    x1_scr = dram("x1_scr", (T, D), kind="Internal")
    out_hbm = dram("out", (T, D), kind="ExternalOutput")

    ARENA = 196 * 1024
    arena = nc.alloc_sbuf_tensor("arena", [128, ARENA], mybir.dt.uint8)
    arena_addr = nc.lookup_mloc(arena).addr

    def sub(name, shape, dtype, off):
        per_part = int(np.prod(shape[1:])) * mybir.dt.size(dtype)
        assert off + per_part <= ARENA, f"arena overflow: {name}"
        hnd = nc.virtual_sbuf_tensor(name, list(shape), dtype)
        nc._tensor(hnd.name, hnd.shape, hnd.dtype, type="SB")
        mloc = nc.lookup_mloc(hnd)
        mloc.allocated = True
        mloc.addr = arena_addr + off
        return hnd

    K = 1024
    CUR = [0]
    def nxt(name, shape, dtype):
        per = int(np.prod(shape[1:])) * mybir.dt.size(dtype)
        off = (CUR[0] + 31) & ~31
        CUR[0] = off + per
        return sub(name, shape, dtype, off)
    identf = nxt("identf", [128, 128], F32)
    identb = nxt("identb", [128, 128], BF16)
    ones1 = nxt("ones1", [1, 128], F32)
    ones8 = nxt("ones8", [8, 128], F32)
    hsel = nxt("hsel", [128, 2], F32)
    hselb = nxt("hselb", [128, 2], BF16)
    identD = nxt("identD", [128, 8 * 128], BF16)
    cvec4 = nxt("cvec4", [4, 1], F32)
    immt = nxt("immt", [128, 2], F32)
    ones65 = nxt("ones65", [65, 128], F32)
    maskb8 = nxt("maskb8", [128, 1024], F32)
    mask8 = nxt("mask8", [8, 1024], F32)
    b_in_col = nxt("b_in_col", [128, 33], F32)
    comb8 = nxt("comb8", [8, 1], F32)
    acol = nxt("acol", [8, 1], F32)
    tmp8a = nxt("tmp8a", [8, 1], F32)
    drow = nxt("drow", [1, 8], F32)
    dbc = nxt("dbc", [128, 8], F32)
    bqk_col = nxt("bqk_col", [128, 16], F32)
    rw_col = nxt("rw_col", [128, 8], BF16)
    rb_t = nxt("rb_t", [1, 1], F32)
    b1_col = nxt("b1_col", [128, 128], F32)
    lprevc = nxt("lprevc", [8, 1], F32)
    lprevbc = nxt("lprevbc", [128, 8], F32)
    decbc = nxt("decbc", [128, 8], F32)
    wtm = nxt("wtm", [128, 8], F32)
    lendrow = nxt("lendrow", [1, 8], F32)
    lcarry = nxt("lcarry", [8, 1], F32)
    stat = nxt("stat", [128, 8], F32)
    negm = nxt("negm", [128, 2], F32)
    nrow = nxt("nrow", [1, 2], F32)
    ltm = nxt("ltm", [128, 16], F32)
    nltm = nxt("nltm", [128, 8], F32)
    btm = nxt("btm", [128, 64], BF16)
    b2_sb = nxt("b2_sb", [4, D], BF16)
    assert CUR[0] <= 16 * K, CUR[0]
    RI = 16 * K
    rowstage = sub("rowstage", [1, 2048], F32, RI)
    gbc_g = sub("gbc_g", [128, 1024], F32, RI + 8 * K)
    gbc_b = sub("gbc_b", [128, 1024], F32, RI + 12 * K)
    gbc = sub("gbc", [128, 2048], F32, RI + 8 * K)
    wvc = [sub(f"wvc{i}", [128, 1024], BF16, RI + i * 2048) for i in range(8)]
    masksT = sub("masksT", [4, 2048], BF16, RI)
    maskbc = sub("maskbc", [128, 2048], BF16, RI + 4 * K)
    hmsk = sub("hmsk", [128, 2048], BF16, RI + 8 * K)
    RH = 32 * K
    stg = [sub(f"stg{i}", [128, 1024], F32, RH + i * 4096) for i in range(4)]
    nbf = [sub(f"nbf{i}", [128, 1024], BF16, RH + i * 4096) for i in range(2)]
    wstp6 = [sub(f"wstp6{i}", [128, 1024], BF16, RH + 8 * K + i * 2048) for i in range(4)]
    R64 = 48 * K
    szall = sub("szall", [128, 8192], BF16, R64)
    o3 = R64 + 16 * K
    def nx(name, shape, dtype):
        nonlocal o3
        per = int(np.prod(shape[1:])) * mybir.dt.size(dtype)
        off = (o3 + 31) & ~31
        o3 = off + per
        return sub(name, shape, dtype, off)
    bT_b = nx("bT_b", [64, 512], BF16)
    cT_b = nx("cT_b", [64, 512], BF16)
    dts = nx("dts", [8, 512], F32)
    lts = nx("lts", [8, 512], F32)
    ldec = nx("ldec", [8, 512], F32)
    mpre = nx("mpre", [128, 1024], F32)
    mexp = nx("mexp", [128, 1024], BF16)
    gst = nx("gst", [128, 1024], BF16)
    rmsk = nx("rmsk", [8, 1024], F32)
    cpb = nx("cpb", [64, 1024], BF16)
    bph = nx("bph", [128, 8 * 64], BF16)
    hst = nx("hst", [64, 8 * 256], F32)
    hstb = nx("hstb", [64, 8 * 256], BF16)
    xs_bf = nx("xs_bf", [128, 2048], BF16)
    dtx = nx("dtx", [128, 2048], BF16)
    yt = nx("yt", [128, 2048], BF16)
    rmat = nx("rmat", [8, 128], F32)
    btbc_sb = nx("btbc_sb", [128, 128], F32)
    assert o3 <= R64 + 65 * K, o3 - R64
    vtm = [sub(f"vtm{i}", [128, 1040], BF16, R64 + i * 2080) for i in range(16)]
    qpair = [sub(f"qpair{i}", [128, 2048], BF16, R64 + 34 * K + i * 4096) for i in range(2)]
    kpair = [sub(f"kpair{i}", [128, 2048], BF16, R64 + 42 * K + i * 4096) for i in range(2)]
    woc = [sub(f"woc{i}", [128, 1024], BF16, R64 + 50 * K + i * 2048) for i in range(7)] \
        + [sub("woc7", [128, 1024], BF16, R64 + 34 * K)]
    hsb = [sub(f"hsb{m}", [128, 2048], BF16, R64 + m * 4096) for m in range(16)]
    R32A = 113 * K
    fm = [sub(f"fm{i}", [128, 2048], BF16, R32A + i * 4096) for i in range(8)]
    R32B = 145 * K
    rdb = [sub(f"rdb{i}", [128, 1024], BF16, R32B + i * 2048) for i in range(16)]
    attnT = [sub(f"attnT{i}", [128, 2048], BF16, R32B + i * 4096) for i in range(8)]
    R16 = 177 * K
    xsfm = [sub(f"xsfm{i}", [128, 512], BF16, R16 + i * 1024) for i in range(16)]
    wqk = [sub(f"wqk{i}", [128, 1024], BF16, R16 + i * 2048) for i in range(2)]
    ptt = [sub(f"ptt{i}", [128, 1024], BF16, R16 + 4 * K + i * 2048) for i in range(2)]
    ptt2 = [sub(f"ptt2_{i}", [128, 1024], BF16, R16 + 8 * K + i * 2048) for i in range(2)]
    biasrow = sub("biasrow", [1, 1024], F32, R16 + 8 * K)
    rc65 = sub("rc65", [65, 1024], F32, R16 + 12 * K)
    q2s8 = sub("q2s", [128, 4096], BF16, RI + 8 * K)
    keyrow = sub("keyrow", [1, 2048], F32, R16)
    t512a = sub("t512a", [1, 512], F32, R16 + 8 * K)
    t512b = sub("t512b", [1, 512], F32, R16 + 10 * K)
    t512i = sub("t512i", [1, 512], mybir.dt.int32, R16 + 12 * K)
    t512c = sub("t512c", [1, 512], F32, R16 + 14 * K)

    ctx = ExitStack()
    e = ctx.enter_context
    PZ = [e(nc.psum_tensor(f"PZ{i}", [128, 512], F32)) for i in range(2)]
    PX = [e(nc.psum_tensor(f"PX{i}", [128, 512], F32)) for i in range(2)]
    PBIG = e(nc.psum_tensor("PBIG", [128, 1024], F32))
    PS6 = e(nc.psum_tensor("PS6", [128, 512], F32))
    PY = e(nc.psum_tensor("PY", [128, 512], F32))

    def sub_ps(name, shape, dtype, base_handle):
        hnd = nc.virtual_psum_tensor(name, list(shape), dtype)
        nc._tensor(hnd.name, hnd.shape, hnd.dtype, type="PSUM")
        mloc = nc.lookup_mloc(hnd)
        bm = nc.lookup_mloc(base_handle)
        mloc.allocated = True
        mloc.addr = bm.addr
        mloc.bank = bm.bank
        mloc.base = bm.base
        return hnd


    sems = {k: e(nc.semaphore(f"sem_{k}")) for k in SEMS}
    P = Prog()
    _B = {}

    def buf(name):
        if name not in _B:
            _B[name] = Buf()
        return _B[name]

    bpz = [buf("pz0"), buf("pz1")]
    bpx = [buf("px0"), buf("px1")]
    bpb = buf("pbig")
    bp6 = buf("ps6")
    bpy = buf("py")

    bxn_s = [buf(f"xns{c}") for c in range(NCH)]
    bpart_s = [buf(f"parts{c}") for c in range(NCH)]
    bx1_s = [buf(f"x1s{c}") for c in range(NCH)]

    def dma(qe, out_ap, in_ap, reads=(), writes=(), **kw):
        return P.do(qe, lambda g, o=out_ap, i=in_ap, kw=dict(kw): g.dma_start(out=o, in_=i, **kw),
                    reads=reads, writes=writes)

    def mm(out_ap, lhsT_ap, rhs_ap, start, stop, reads=(), writes=()):
        return P.do("t", lambda t, o=out_ap, l=lhsT_ap, r=rhs_ap, a=start, b=stop:
                    t.matmul(o, l, r, start=a, stop=b),
                    reads=reads, writes=writes)

    def mm_raw(parts, reads=(), writes=()):
        fns = [(lambda t, o=o, l=l, r=r, s=s, st=st: t.matmul(o, l, r, start=s, stop=st))
               for (o, l, r, s, st) in parts]
        return P.do("t", fns, reads=reads, writes=writes)

    def act(out_ap, in_ap, func, bias=0.0, scale=1.0, accum=None, reads=(), writes=()):
        return P.do("s", lambda s, o=out_ap, i=in_ap, f=func, b=bias, sc=scale, a=accum:
                    s.activation(o, i, f, bias=b, scale=sc, accum_out=a),
                    reads=reads, writes=writes)

    def vop(fn, reads=(), writes=()):
        return P.do("v", fn, reads=reads, writes=writes)

    def trans(out_ap, in_ap, ident_ap, reads=(), writes=()):
        return P.do("t", lambda t, o=out_ap, i=in_ap, d=ident_ap:
                    t.matmul(o, i, d, start=True, stop=True),
                    reads=reads, writes=writes)

    cb = buf("consts")
    vb = buf("vecs")
    bstg = [buf(f"stg{i}") for i in range(4)]
    bfm = [buf(f"fm{i}") for i in range(8)]
    bgbc = buf("gbc")
    brow = buf("rowstage")
    bstat = buf("stat")

    dma("y", identf[:, :], c_identf, writes=(cb,))
    dma("y", identb[:, :], c_identb, writes=(cb,))
    dma("y", ones1[:, :], c_ones1, writes=(cb,))
    dma("y", ones8[:, :], c_ones8, writes=(cb,))
    dma("y", hsel[:, :], c_hsel, writes=(cb,))
    dma("g", hselb[:, :], c_hsel, writes=(cb,))
    dma("y", maskb8[:, :], c_maskb8, writes=(cb,))
    dma("y", mask8[:, :], c_mask8, writes=(cb,))
    dma("y", cvec4[:, :], c_cvec4, writes=(cb,))
    dma("y", ones65[:, :], c_ones65, writes=(cb,))
    dma("y", immt[:, :], c_imm, writes=(cb,))
    nc.const_aps.register(F32, EPS, immt[:, 0:1])
    nc.const_aps.register(F32, 1.0, immt[:, 1:2])
    dma("y", b_in_col[:, :], b_in[0:4224].rearrange("(m p) -> p m", p=128), writes=(vb,),
        allow_slow_non_contiguous=True)
    dma("y", tmp8a[:, :], b_in[4224:4232].rearrange("(a b) -> a b", b=1), writes=(vb,),
        allow_slow_non_contiguous=True)
    dma("y", comb8[:, :], dt_bias.rearrange("(a b) -> a b", b=1), writes=(vb,),
        allow_slow_non_contiguous=True)
    dma("y", acol[:, :], a_log.rearrange("(a b) -> a b", b=1), writes=(vb,),
        allow_slow_non_contiguous=True)
    dma("y", drow[:, :], d_skip.rearrange("(b a) -> b a", b=1), writes=(vb,))
    dma("y", bqk_col[:, 0:8], bq.rearrange("(m p) -> p m", p=128), writes=(vb,),
        allow_slow_non_contiguous=True)
    dma("y", bqk_col[:, 8:16], bk.rearrange("(m p) -> p m", p=128), writes=(vb,),
        allow_slow_non_contiguous=True)
    dma("y", rb_t[:, :], rb.rearrange("(a b) -> a b", b=1), writes=(vb,))
    dma("y", b1_col[:, :], b1.rearrange("e (m p) -> p (e m)", p=128), writes=(vb,),
        allow_slow_non_contiguous=True)
    dma("g", b2_sb[:, :], b2[:, :], writes=(vb,))
    vop(lambda v: v.tensor_add(comb8[:, :], comb8[:, :], tmp8a[:, :]), reads=(vb,), writes=(vb,))
    act(acol[:, :], acol[:, :], AF.Exp, reads=(vb,), writes=(vb,))
    vop(lambda v: v.tensor_scalar(acol[:, :], acol[:, :], -1.0, None, ALU.mult),
        reads=(vb,), writes=(vb,))
    mm(PS6[:, 0:8], ones1[0:1, :], drow[0:1, :], True, True, reads=(cb, vb), writes=(bp6,))
    vop(lambda v: v.tensor_copy(dbc[:, :], PS6[:, 0:8]), reads=(bp6,), writes=(vb,))
    for h in range(NHM):
        vop(lambda v, h=h: v.tensor_scalar(identD[:, 128 * h:128 * (h + 1)], identb[:, :],
                                           dbc[:, h:h + 1], None, ALU.mult),
            reads=(cb, vb), writes=(cb,))

    def build_bc(dst_g, dst_b, src_g, src_b_):
        for dst, src in ((dst_g, src_g), (dst_b, src_b_)):
            dma("y", rowstage[0:1, 0:D], src.rearrange("(b a) -> b a", b=1), writes=(brow,))
            for j in range(0, D, 512):
                mm(PBIG[:, j:j + 512], ones1[0:1, :], rowstage[0:1, j:j + 512], True, True,
                   reads=(cb, brow), writes=(bpb,))
            vop(lambda v, dst=dst: v.tensor_copy(dst[:, :], PBIG[:, :]),
                reads=(bpb,), writes=(bgbc,))

    sq_scr = stg[3]
    bsq = bstg[3]

    def ln_stats(src_ap, src_buf):
        act(sq_scr[:, :], src_ap, AF.Square, accum=stat[:, 2:3], reads=(src_buf,),
            writes=(bsq, bstat))
        vop(lambda v: v.reduce_sum(stat[:, 3:4], src_ap, AX.X), reads=(src_buf,), writes=(bstat,))
        vop(lambda v: v.tensor_scalar(stat[:, 0:1], stat[:, 3:4], 1.0 / D, None, ALU.mult),
            reads=(bstat,), writes=(bstat,))
        vop(lambda v: v.tensor_scalar(stat[:, 2:3], stat[:, 2:3], 1.0 / D, None, ALU.mult),
            reads=(bstat,), writes=(bstat,))
        vop(lambda v: v.tensor_mul(stat[:, 4:5], stat[:, 0:1], stat[:, 0:1]),
            reads=(bstat,), writes=(bstat,))
        vop(lambda v: v.tensor_sub(stat[:, 4:5], stat[:, 2:3], stat[:, 4:5]),
            reads=(bstat,), writes=(bstat,))
        act(stat[:, 4:5], stat[:, 4:5], AF.Sqrt, bias=EPS, reads=(bstat,), writes=(bstat,))
        vop(lambda v: v.reciprocal(stat[:, 1:2], stat[:, 4:5]), reads=(bstat,), writes=(bstat,))

    def fm_transposes(src_bf_ap, src_buf, c):
        for j in range(8):
            pt = PS6[:, 0:128] if j % 2 == 0 else PY[:, 0:128]
            tb = bp6 if j % 2 == 0 else bpy
            trans(pt, bass.AP(src_bf_ap, 128 * j, [[1024, 128], [1, 128]]),
                  identb[:, :], reads=(src_buf, cb), writes=(tb,))
            vop(lambda v, pt=pt, j=j, c=c: v.tensor_copy(fm[j][:, 128 * c:128 * (c + 1)], pt),
                reads=(tb,), writes=(bfm[j],))

    build_bc(gbc_g, gbc_b, ln_g, ln_b)
    for c in range(NCH if KPH >= 1 else 0):
        sx, bx = stg[c % 2], bstg[c % 2]
        sn, bn = stg[2], bstg[2]
        dma("y", sx[:, :], x_in[128 * c:128 * (c + 1), :], writes=(bx,))
        ln_stats(sx[:, :], bx)
        vop(lambda v, sx=sx: v.tensor_scalar(sn[:, :], sx[:, :], stat[:, 0:1], stat[:, 1:2],
                                             ALU.subtract, ALU.mult),
            reads=(bx, bstat), writes=(bn,))
        dma("y", xn_scr[128 * c:128 * (c + 1), :], sn[:, :], reads=(bn,), writes=(bxn_s[c],))
        ln_stats(sn[:, :], bn)
        vop(lambda v: v.tensor_scalar(sn[:, :], sn[:, :], stat[:, 0:1], stat[:, 1:2],
                                      ALU.subtract, ALU.mult),
            reads=(bn, bstat), writes=(bn,))
        vop(lambda v: v.tensor_mul(sn[:, :], sn[:, :], gbc_g[:, :]), reads=(bn, bgbc), writes=(bn,))
        nb, bnb = nbf[c % 2], bstg[c % 2]
        vop(lambda v, nb=nb: v.tensor_add(nb[:, :], sn[:, :], gbc_b[:, :]),
            reads=(bn, bgbc), writes=(bnb,))
        fm_transposes(nb, bnb, c)

    P.barrier()

    dma("y", rowstage[0:1, 0:1024], b_out.rearrange("(b a) -> b a", b=1), writes=(brow,))

    bhst = buf("hst")
    bhstb = [buf(f"hstb{h}") for h in range(8)]
    blpc = buf("lpc")
    blc = buf("lcarry")
    vop(lambda v: v.memset(hst[:, :], 0.0), writes=(bhst,))
    vop(lambda v: v.memset(hstb[:, :], 0.0), writes=tuple(bhstb))
    vop(lambda v: v.memset(lprevc[:, :], 0.0), writes=(blpc,))
    vop(lambda v: v.memset(lprevbc[:, :], 0.0), writes=(blpc,))
    vop(lambda v: v.memset(lcarry[:, :], 0.0), writes=(blc,))

    MODE_B = [33, 32] + list(range(16, 32)) + list(range(16))

    brdb = [buf(f"rdb{i}") for i in range(16)]
    bxsf = [buf(f"xsfm{i}") for i in range(16)]
    bsz = [buf(f"siluz{m}") for m in range(16)]
    bbct = buf("bct")
    bdtl = buf("dtl")
    bldec = buf("ldec")
    bltm = buf("ltm")
    blend = buf("lend")
    bmpre = buf("mpre")
    bnltm = buf("nltm")
    bbtbc = buf("btbc")
    bmexp = buf("mexp")
    bgst = buf("gst")
    brm = buf("rmat")
    brmsk = buf("rmsk")
    bcpb = buf("cpb")
    bbtm = buf("btm")
    bbph = buf("bph")
    bxsb = buf("xs_bf")
    bdtx = buf("dtx")
    byt = [buf(f"yt{j}") for j in range(16)]
    bwm = buf("wdec")

    for j in range(12 if KPH >= 2 else 0):
        dma("g", rdb[j][:, :], w_out_d[128 * j:128 * (j + 1), :], writes=(brdb[j],))

    def emit_wout(c):
        for j in range(16):
            mm_raw([(PY[:, :], yt[:, 128 * j:128 * (j + 1)],
                     rdb[j][:, 0:512], j == 0, False),
                    (PX[0][:, :], yt[:, 128 * j:128 * (j + 1)],
                     rdb[j][:, 512:1024], j == 0, False)],
                   reads=(byt[j], brdb[j]), writes=(bpy, bpx[0]))
        mm_raw([(PY[:, :], ones1[0:1, :], rowstage[0:1, 0:512], False, True),
                (PX[0][:, :], ones1[0:1, :], rowstage[0:1, 512:1024], False, True)],
               reads=(brow, cb), writes=(bpy, bpx[0]))
        dma("y", stg[0][:, :], xn_scr[128 * c:128 * (c + 1), :], reads=(bxn_s[c],),
            writes=(bstg[0],))
        dma("y", stg[1][:, :], x_in[128 * c:128 * (c + 1), :], writes=(bstg[1],))
        vop(lambda v: v.scalar_tensor_tensor(stg[2][:, 0:512], PY[:, :], 1.0,
                                             stg[0][:, 0:512], ALU.mult, ALU.add),
            reads=(bpy, bstg[0]), writes=(bstg[2],))
        vop(lambda v: v.scalar_tensor_tensor(stg[2][:, 512:1024], PX[0][:, :], 1.0,
                                             stg[0][:, 512:1024], ALU.mult, ALU.add),
            reads=(bpx[0], bstg[0]), writes=(bstg[2],))
        vop(lambda v: v.tensor_add(stg[2][:, :], stg[2][:, :], stg[1][:, :]),
            reads=(bstg[2], bstg[1]), writes=(bstg[2],))
        dma("y", part_scr[128 * c:128 * (c + 1), :], stg[2][:, :], reads=(bstg[2],),
            writes=(bpart_s[c],))

    def emit_chain(c, mid=None):
        ct0 = (c % CPS) * 128
        for g in range(4):
            pz, pzb = PZ[g % 2], bpz[g % 2]
            for k in range(4):
                j = 4 * g + k
                trans(pz[:, 128 * k:128 * (k + 1)], xsfm[j][:, ct0:ct0 + 128],
                      identb[:, :], reads=(bxsf[j], cb), writes=(pzb,))
            if g % 2 == 0:
                vop(lambda v, pz=pz, g=g: v.tensor_copy(
                        xs_bf[:, 512 * g:512 * (g + 1)], pz[:, 0:512]),
                    reads=(pzb,), writes=(bxsb,))
            else:
                P.do("s", lambda s, pz=pz, g=g: s.copy(
                        xs_bf[:, 512 * g:512 * (g + 1)], pz[:, 0:512]),
                     reads=(pzb,), writes=(bxsb,))
        mm(PZ[0][:, 160:288], bT_b[:, ct0:ct0 + 128], cT_b[:, ct0:ct0 + 128], True, True,
           reads=(bbct,), writes=(bpz[0],))
        trans(PZ[0][:, 288:352], bT_b[:, ct0:ct0 + 128], identb[0:64, 0:64],
              reads=(bbct, cb), writes=(bpz[0],))
        vop(lambda v: v.tensor_copy(btbc_sb[:, :], PZ[0][:, 160:288]),
            reads=(bpz[0],), writes=(bbtbc,))
        vop(lambda v: v.tensor_copy(btm[:, :], PZ[0][:, 288:352]),
            reads=(bpz[0],), writes=(bbtm,))
        trans(PZ[0][:, 128:136], dts[:, ct0:ct0 + 128], identf[0:8, 0:8],
              reads=(bdtl, cb), writes=(bpz[0],))
        trans(PZ[0][:, 136:144], lts[:, ct0:ct0 + 128], identf[0:8, 0:8],
              reads=(bdtl, cb), writes=(bpz[0],))
        vop(lambda v: v.tensor_copy(ltm[:, :], PZ[0][:, 128:144]),
            reads=(bpz[0],), writes=(bltm,))
        trans(PZ[0][0:1, 144:152], lts[:, ct0 + 127:ct0 + 128], identf[0:8, 0:8],
              reads=(bdtl, cb), writes=(bpz[0],))
        vop(lambda v: v.tensor_copy(lendrow[:, :], PZ[0][0:1, 144:152]),
            reads=(bpz[0],), writes=(blend,))
        mm(PZ[0][:, 152:160], ones1[0:1, :], lendrow[0:1, :], True, True,
           reads=(cb, blend), writes=(bpz[0],))
        vop(lambda v: v.tensor_sub(wtm[:, :], PZ[0][:, 152:160], ltm[:, 8:16]),
            reads=(bpz[0], bltm), writes=(bwm,))
        act(wtm[:, :], wtm[:, :], AF.Exp, reads=(bwm,), writes=(bwm,))
        vop(lambda v: v.tensor_sub(decbc[:, :], PZ[0][:, 152:160], lprevbc[:, :]),
            reads=(bpz[0], blpc), writes=(bwm,))
        act(decbc[:, :], decbc[:, :], AF.Exp, reads=(bwm,), writes=(bwm,))
        vop(lambda v, ct0=ct0: v.tensor_scalar(rmat[:, :], lts[:, ct0:ct0 + 128],
                                               lprevc[:, :], None, ALU.subtract),
            reads=(bdtl, blpc), writes=(brm,))
        act(rmat[:, :], rmat[:, :], AF.Exp, reads=(brm,), writes=(brm,))
        vop(lambda v, ct0=ct0: v.tensor_copy(lprevc[:, :], lts[:, ct0 + 127:ct0 + 128]),
            reads=(bdtl,), writes=(blpc,))
        vop(lambda v: v.tensor_copy(lprevbc[:, :], PZ[0][:, 152:160]),
            reads=(bpz[0],), writes=(blpc,))
        vop(lambda v, ct0=ct0: v.tensor_mul(
                rmsk[:, :], bass.AP(lts, ct0, [[512, 8], [0, 8], [1, 128]]),
                mask8[:, :]),
            reads=(bdtl, cb), writes=(brmsk,))
        act(nltm[:, :], ltm[:, 8:16], AF.Identity, scale=-1.0,
            reads=(bltm,), writes=(bnltm,))
        for half in range(2):
            mm(PZ[1][:, 0:512], ones8[:, :], rmsk[:, 512 * half:512 * (half + 1)],
               True, True, reads=(cb, brmsk), writes=(bpz[1],))
            for hh in range(4):
                h = 4 * half + hh
                act(mpre[:, 128 * h:128 * (h + 1)], PZ[1][:, 128 * hh:128 * (hh + 1)],
                    AF.Identity, bias=nltm[:, h:h + 1],
                    reads=(bpz[1], bnltm), writes=(bmpre,))
        vop(lambda v: v.tensor_add(mpre[:, :], mpre[:, :], maskb8[:, :]),
            reads=(bmpre, cb), writes=(bmpre,))
        act(mexp[:, :], mpre[:, :], AF.Exp, reads=(bmpre,), writes=(bmexp,))
        vop(lambda v: v.tensor_mul(
                bass.AP(gst, 0, [[1024, 128], [128, 8], [1, 128]]),
                bass.AP(mexp, 0, [[1024, 128], [128, 8], [1, 128]]),
                bass.AP(btbc_sb, 0, [[128, 128], [0, 8], [1, 128]])),
            reads=(bmexp, bbtbc), writes=(bgst,))
        for h in range(NHM):
            act(bph[:, 64 * h:64 * (h + 1)], btm[:, :], AF.Identity,
                scale=wtm[:, h:h + 1], reads=(bbtm, bwm), writes=(bbph,))
        vop(lambda v: v.tensor_mul(rmsk[:, :], bass.AP(rmat, 0, [[128, 8], [0, 8], [1, 128]]),
                                   mask8[:, :]),
            reads=(brm, cb), writes=(brmsk,))
        if mid is not None:
            mid()
        for half in range(2):
            mm(PZ[1][0:64, 0:512], ones8[:, 0:64], rmsk[:, 512 * half:512 * (half + 1)],
               True, True, reads=(cb, brmsk), writes=(bpz[1],))
            vop(lambda v, half=half, ct0=ct0: v.tensor_mul(
                    bass.AP(cpb, 512 * half, [[1024, 64], [128, 4], [1, 128]]),
                    bass.AP(cT_b, ct0, [[512, 64], [0, 4], [1, 128]]),
                    bass.AP(PZ[1], 0, [[512, 64], [128, 4], [1, 128]])),
                reads=(bbct, bpz[1]), writes=(bcpb,))
        for h in range(NHM):
            act(dtx[:, 256 * h:256 * (h + 1)], xs_bf[:, 256 * h:256 * (h + 1)],
                AF.Identity, scale=ltm[:, h:h + 1],
                reads=(bxsb, bltm), writes=(bdtx,))

    YSLOT = [(PBIG, 0, bpb), (PBIG, 256, bpb), (PBIG, 512, bpb), (PBIG, 768, bpb),
             (PY, 0, bpy), (PY, 256, bpy), (PX[0], 0, bpx[0]), (PX[0], 256, bpx[0])]

    def emit_y(c):
        ct0 = (c % CPS) * 128
        for h in range(NHM):
            ypp, yc0, ypb = YSLOT[h]
            parts = []
            for q in range(2):
                o = ypp[:, yc0 + 128 * q:yc0 + 128 * (q + 1)]
                parts.append((o, dtx[:, 256 * h + 128 * q:256 * h + 128 * (q + 1)],
                              gst[:, 128 * h:128 * (h + 1)], q == 0, False))
            for q in range(2):
                o = ypp[:, yc0 + 128 * q:yc0 + 128 * (q + 1)]
                parts.append((o, xs_bf[:, 256 * h + 128 * q:256 * h + 128 * (q + 1)],
                              identD[:, 128 * h:128 * (h + 1)], False, False))
            for q in range(2):
                o = ypp[:, yc0 + 128 * q:yc0 + 128 * (q + 1)]
                parts.append((o, bass.AP(hstb, 256 * h + 128 * q, [[2048, 64], [1, 128]]),
                              cpb[:, 128 * h:128 * (h + 1)], False, q == 1))
            mm_raw(parts, reads=[bdtx, bgst, bxsb, cb, bhstb[h], bcpb], writes=(ypb,))
            hps = PS6[0:64, 0:256] if h % 2 == 0 else PX[1][0:64, 0:256]
            hpb = bp6 if h % 2 == 0 else bpx[1]
            mm(hps, bph[:, 64 * h:64 * (h + 1)], dtx[:, 256 * h:256 * (h + 1)],
               True, True, reads=(bbph, bdtx), writes=(hpb,))
            vop(lambda v, h=h, hps=hps: v.scalar_tensor_tensor(
                    bass.AP(hst, 256 * h, [[2048, 64], [1, 256]]),
                    bass.AP(hst, 256 * h, [[2048, 64], [1, 256]]),
                    decbc[0:64, h:h + 1], hps, ALU.mult, ALU.add),
                reads=(hpb, bwm, bhst), writes=(bhst,))
            vop(lambda v, h=h: v.tensor_copy(
                    bass.AP(hstb, 256 * h, [[2048, 64], [1, 256]]),
                    bass.AP(hst, 256 * h, [[2048, 64], [1, 256]])),
                reads=(bhst,), writes=(bhstb[h],))
        for (ypp, ypb, m0, nt) in ((PBIG, bpb, 0, 8), (PY, bpy, 8, 4), (PX[0], bpx[0], 12, 4)):
            vop(lambda v, ypp=ypp, m0=m0, nt=nt, ct0=ct0: v.tensor_mul(
                    bass.AP(yt, 128 * m0, [[2048, 128], [128, nt], [1, 128]]),
                    bass.AP(ypp, 0, [[128 * nt, 128], [128, nt], [1, 128]]),
                    bass.AP(szall, 512 * m0 + ct0, [[8192, 128], [512, nt], [1, 128]])),
                reads=tuple([ypb] + [bsz[m0 + i] for i in range(nt)]),
                writes=tuple(byt[m0 + i] for i in range(nt)))

    for sc in range(NSC if KPH >= 2 else 0):
        t0 = sc * SCLEN
        for mi, m in enumerate(MODE_B):
            wr, wbr = rdb[12 + mi % 4], brdb[12 + mi % 4]
            cw = 128 if m < 33 else 8
            dma("g", bass.AP(wr, 0, [[1024, 128], [128, 8], [1, cw]]),
                w_in[:, 128 * m:128 * m + cw].rearrange("(c p) n -> p c n", p=128),
                writes=(wbr,))
            pz, pzb = PZ[mi % 2], bpz[mi % 2]
            parts = []
            for fi in range(8):
                parts.append((pz[0:cw, 0:512],
                              bass.AP(wr, 128 * fi, [[1024, 128], [1, cw]]),
                              fm[fi][:, t0:t0 + 512], fi == 0, fi == 7))
            mm_raw(parts, reads=[wbr] + bfm, writes=(pzb,))
            if m < 16:
                act(bass.AP(szall, 512 * m, [[8192, 128], [1, 512]]), pz[:, :],
                    AF.Silu, bias=b_in_col[:, m:m + 1],
                    reads=(pzb, vb), writes=(bsz[m],))
            elif m < 32:
                act(xsfm[m - 16][:, :], pz[:, :], AF.Identity, bias=b_in_col[:, m:m + 1],
                    reads=(pzb, vb), writes=(bxsf[m - 16],))
            elif m == 32:
                act(bT_b[:, :], pz[0:64, :], AF.Identity, bias=b_in_col[0:64, 32:33],
                    reads=(pzb, vb), writes=(bbct,))
                act(cT_b[:, :], pz[64:128, :], AF.Identity, bias=b_in_col[64:128, 32:33],
                    reads=(pzb, vb), writes=(bbct,))
            else:
                act(dts[:, :], pz[0:8, :], AF.Identity, bias=comb8[:, :],
                    reads=(pzb, vb), writes=(bdtl,))
            if mi == 1:
                act(ldec[:, :], dts[:, :], AF.Abs, reads=(bdtl,), writes=(bldec,))
                act(ldec[:, :], ldec[:, :], AF.Exp, scale=-1.0, reads=(bldec,), writes=(bldec,))
                act(ldec[:, :], ldec[:, :], AF.Ln, bias=1.0, reads=(bldec,), writes=(bldec,))
                vop(lambda v: v.tensor_scalar(rmsk[0:8, 0:512], dts[:, :], 0.0, None, ALU.max),
                    reads=(bdtl,), writes=(brmsk,))
                vop(lambda v: v.tensor_add(dts[:, :], ldec[:, :], rmsk[0:8, 0:512]),
                    reads=(bldec, brmsk), writes=(bdtl,))
                vop(lambda v: v.tensor_scalar(ldec[:, :], dts[:, :], acol[:, :], None, ALU.mult),
                    reads=(bdtl, vb), writes=(bldec,))
                vop(lambda v: v.memset(rmsk[0:8, 0:512], 0.0), writes=(brmsk,))
                vop(lambda v: v.tensor_tensor_scan(lts[:, :], ldec[:, :], rmsk[0:8, 0:512],
                                                   lcarry[:, :], ALU.add, ALU.add),
                    reads=(bldec, brmsk, blc), writes=(bdtl,))
                vop(lambda v: v.tensor_copy(lcarry[:, :], lts[:, 511:512]),
                    reads=(bdtl,), writes=(blc,))
        for j in (12, 13, 14, 15):
            dma("g", rdb[j][:, :], w_out_d[128 * j:128 * (j + 1), :], writes=(brdb[j],))
        emit_chain(sc * CPS)

        for tc in range(CPS):
            c = sc * CPS + tc
            emit_y(c)
            if tc < CPS - 1:
                emit_chain(c + 1, mid=lambda c=c: emit_wout(c))
            else:
                emit_wout(c)

    P.barrier()

    for c in range(NCH if KPH >= 3 else 0):
        nb, bnb = nbf[c % 2], bstg[c % 2]
        dma("g", nb[:, :], xn_scr[128 * c:128 * (c + 1), :], reads=(bxn_s[c],), writes=(bnb,))
        fm_transposes(nb, bnb, c)

    bwvc = [buf(f"wvc{i}") for i in range(8)]
    bvtm = [buf(f"vtm{i}") for i in range(16)]
    bbias = buf("biasrow")
    for j in range(8 if KPH >= 3 else 0):
        dma("g", wvc[j][:, :], wv[128 * j:128 * (j + 1), :], writes=(bwvc[j],))
    dma("y", biasrow[0:1, :], bv.rearrange("(b a) -> b a", b=1), writes=(bbias,))
    for tt in range(16 if KPH >= 3 else 0):
        vop(lambda v, tt=tt: v.memset(bass.AP(vtm[tt], 64, [[1040, 128], [65, 16]]), 1.0),
            writes=(bvtm[tt],))
        parts = []
        for fi in range(8):
            for nh in range(2):
                parts.append((PBIG[:, 512 * nh:512 * (nh + 1)],
                              fm[fi][:, 128 * tt:128 * (tt + 1)],
                              wvc[fi][:, 512 * nh:512 * (nh + 1)], fi == 0, False))
        for nh in range(2):
            parts.append((PBIG[:, 512 * nh:512 * (nh + 1)], ones1[0:1, :],
                          biasrow[0:1, 512 * nh:512 * (nh + 1)], False, True))
        mm_raw(parts, reads=[cb, bbias] + bfm + bwvc, writes=(bpb,))
        vop(lambda v, tt=tt: v.tensor_copy(
                bass.AP(vtm[tt], 0, [[1040, 128], [65, 16], [1, 64]]), PBIG[:, :]),
            reads=(bpb,), writes=(bvtm[tt],))

    _bq2 = buf("q2s")
    for j in range(4, 8):
        wb_ = buf(f"wvc{j}")
        if wb_.w is not None:
            _bq2.r[wb_.w[0]] = max(_bq2.r.get(wb_.w[0], 0), wb_.w[1])
        for pe_, v_ in wb_.r.items():
            _bq2.r[pe_] = max(_bq2.r.get(pe_, 0), v_)

    bqp = [buf("qp0"), buf("qp1")]
    bkp = [buf("kp0"), buf("kp1")]
    bwqk = [buf("bwqk0"), buf("bwqk1")]
    batnT = [buf(f"atnT{i}") for i in range(8)]
    bq2 = buf("q2s")
    bnegm = buf("negm")
    bnrow = buf("nrow")
    bptt = [buf("ptt0"), buf("ptt1")]
    bptt2 = [buf("ptt2_0"), buf("ptt2_1")]
    for _b2 in bptt2:
        wb_ = buf("biasrow")
        if wb_.w is not None:
            _b2.r[wb_.w[0]] = max(_b2.r.get(wb_.w[0], 0), wb_.w[1])
        for pe_, v_ in wb_.r.items():
            _b2.r[pe_] = max(_b2.r.get(pe_, 0), v_)
    brc = buf("rc65")

    pend_norm = [None]

    def flush_norm():
        if pend_norm[0] is not None:
            pend_norm[0]()
            pend_norm[0] = None

    for p in range(NPAIR if KPH >= 3 else 0):
        qp, kp = qpair[p % 2], kpair[p % 2]
        bq_, bk_ = bqp[p % 2], bkp[p % 2]
        for (wdram, stg_t, stg_b) in ((wq, wqk[0], bwqk[0]), (wk, wqk[1], bwqk[1])):
            dma("g", bass.AP(stg_t, 0, [[1024, 128], [128, 8], [1, 128]]),
                wdram[:, 128 * p:128 * (p + 1)].rearrange("(c p) n -> p c n", p=128),
                writes=(stg_b,))
        for (wstg, bwstg, dst, bdst, bcol) in ((wqk[0], bwqk[0], qp, bq_, 0),
                                               (wqk[1], bwqk[1], kp, bk_, 8)):
            for ns in range(4):
                pz = PBIG[:, 512 * (ns % 2):512 * (ns % 2 + 1)]
                pzb = bpb
                parts = []
                for fi in range(8):
                    parts.append((pz, bass.AP(wstg, 128 * fi, [[1024, 128], [1, 128]]),
                                  fm[fi][:, 512 * ns:512 * (ns + 1)], fi == 0, fi == 7))
                mm_raw(parts, reads=[bwstg] + bfm, writes=(pzb,))
                act(dst[:, 512 * ns:512 * (ns + 1)], pz, AF.Identity,
                    bias=bqk_col[:, bcol + p:bcol + p + 1], reads=(pzb, vb), writes=(bdst,))
                if ns == 0 and wstg is wqk[0]:
                    flush_norm()
        for which, (src, bsrc) in enumerate(((qp, bq_), (kp, bk_))):
            for ns in range(4):
                s = 4 * which + ns
                act(q2s8[:, 512 * s:512 * (s + 1)], src[:, 512 * ns:512 * (ns + 1)],
                    AF.Square, reads=(bsrc,), writes=(bq2,))
        for which in range(2):
            for ns in range(4):
                s = 4 * which + ns
                mbank, mb = (PS6, bp6) if s % 2 == 0 else (PY, bpy)
                mm(mbank[0:2, 0:512], hselb[:, :], q2s8[:, 512 * s:512 * (s + 1)],
                   True, True, reads=(cb, bq2), writes=(mb,))
                if ns == 0:
                    vop(lambda v, which=which, mbank=mbank: v.reduce_max(
                            negm[0:2, which:which + 1], mbank[0:2, 0:512], AX.X),
                        reads=(mb,), writes=(bnegm,))
                else:
                    vop(lambda v, which=which, mbank=mbank: v.reduce_max(
                            stat[0:2, 6 + which:7 + which], mbank[0:2, 0:512], AX.X),
                        reads=(mb,), writes=(bstat,))
                    vop(lambda v, which=which: v.tensor_max(negm[0:2, which:which + 1],
                                                            negm[0:2, which:which + 1],
                                                            stat[0:2, 6 + which:7 + which]),
                        reads=(bnegm, bstat), writes=(bnegm,))
        vop(lambda v: v.tensor_mul(negm[0:2, 0:1], negm[0:2, 0:1], negm[0:2, 1:2]),
            reads=(bnegm,), writes=(bnegm,))
        act(negm[0:2, 0:1], negm[0:2, 0:1], AF.Sqrt, scale=1.0 / 64.0,
            reads=(bnegm,), writes=(bnegm,))
        vop(lambda v: v.tensor_scalar(negm[0:2, 0:1], negm[0:2, 0:1], -1.0, None, ALU.mult),
            reads=(bnegm,), writes=(bnegm,))
        trans(PS6[0:1, 24:26], negm[0:2, 0:1], identf[0:2, 0:2], reads=(bnegm, cb), writes=(bp6,))
        vop(lambda v: v.tensor_copy(nrow[:, :], PS6[0:1, 24:26]), reads=(bp6,), writes=(bnrow,))
        mm(PS6[:, 26:28], ones1[0:1, :], nrow[0:1, :], True, True, reads=(cb, bnrow), writes=(bp6,))
        vop(lambda v: v.tensor_copy(negm[:, :], PS6[:, 26:28]), reads=(bp6,), writes=(bnegm,))

        for qq in range(4):
            q0 = 512 * qq

            def pt_slot(kt, head):
                base = ptt if kt % 4 < 2 else ptt2
                sl = 512 * (kt % 2)
                return base[head][:, sl:sl + 512], (bptt if kt % 4 < 2 else bptt2)[head]

            def st_emit(kt):
                par = kt % 2
                pa, ba = pt_slot(kt, 0)
                pb, bb = pt_slot(kt, 1)
                mm_raw([(PZ[par][:, :], kp[0:64, 128 * kt:128 * (kt + 1)],
                         qp[0:64, q0:q0 + 512], True, True),
                        (PX[par][:, :], kp[64:128, 128 * kt:128 * (kt + 1)],
                         qp[64:128, q0:q0 + 512], True, True)],
                       reads=(bq_, bk_), writes=(bpz[par], bpx[par]))
                act(pa, PZ[par][:, :], AF.Exp, bias=negm[:, 0:1], scale=0.125,
                    reads=(bpz[par], bnegm), writes=(ba,))
                act(pb, PX[par][:, :], AF.Exp, bias=negm[:, 1:2], scale=0.125,
                    reads=(bpx[par], bnegm), writes=(bb,))

            if qq % 2 == 1:
                AVB = [(PS6[0:65, :], PS6[64:65, :], PS6[64:128, :], PS6[0:64, :], bp6),
                       (PY[0:65, :], PY[64:65, :], PY[64:128, :], PY[0:64, :], bpy)]
            else:
                AVB = [(PBIG[0:65, 0:512], PBIG[64:65, 0:512], PBIG[64:128, 0:512],
                        PBIG[0:64, 0:512], bpb),
                       (PBIG[0:65, 512:1024], PBIG[64:65, 512:1024],
                        PBIG[64:128, 512:1024], PBIG[0:64, 512:1024], bpb)]

            def av_emit(kt, AVB=AVB):
                pa, ba = pt_slot(kt, 0)
                pb, bb = pt_slot(kt, 1)
                mm_raw([(AVB[0][0], vtm[kt][:, 65 * (2 * p):65 * (2 * p) + 65],
                         pa, kt == 0, kt == 15)],
                       reads=(bvtm[kt], ba), writes=(AVB[0][4],))
                mm_raw([(AVB[1][0], vtm[kt][:, 65 * (2 * p + 1):65 * (2 * p + 1) + 65],
                         pb, kt == 0, kt == 15)],
                       reads=(bvtm[kt], bb), writes=(AVB[1][4],))

            st_emit(0)
            st_emit(1)
            for kt in range(2, 16):
                st_emit(kt)
                av_emit(kt - 2)
                if kt == 7:
                    flush_norm()
            av_emit(14)
            av_emit(15)

            def norm_emit(AVB=AVB, qq=qq, p=p, q0=q0):
                rcsl = rc65[64:65, 512 * (qq % 2):512 * (qq % 2) + 512]
                rcbc = rc65[0:64, 512 * (qq % 2):512 * (qq % 2) + 512]
                for hh, (avfull, avsum, avbc, avdat, avb) in enumerate(AVB):
                    vop(lambda v, avsum=avsum, rcsl=rcsl: v.reciprocal(rcsl, avsum),
                        reads=(avb,), writes=(brc,))
                    mm_raw([(avbc, ones65[64:65, 0:64], rcsl, True, True)],
                           reads=(brc, cb), writes=(avb,))
                    vop(lambda v, avbc=avbc, rcbc=rcbc: v.tensor_copy(rcbc, avbc),
                        reads=(avb,), writes=(brc,))
                    vop(lambda v, hh=hh, p=p, q0=q0, avdat=avdat, rcbc=rcbc: v.tensor_mul(
                            attnT[p][64 * hh:64 * (hh + 1), q0:q0 + 512],
                            avdat, rcbc),
                        reads=(avb, brc), writes=(batnT[p],))

            pend_norm[0] = norm_emit

    flush_norm()
    P.barrier()

    rwbc = sub("rwbc", [128, 1024], F32, R16 + 12 * K)
    brwbc = buf("rwbc")
    for j in range(8 if KPH >= 4 else 0):
        dma("g", woc[j][:, :], wo[128 * j:128 * (j + 1), :], writes=(buf(f"woc{j}"),))
    bwoc = [buf(f"woc{j}") for j in range(8)]
    dma("y", biasrow[0:1, :], bo.rearrange("(b a) -> b a", b=1), writes=(bbias,))
    build_bc(gbc_g, gbc_b, l1g, l1b)
    if KPH >= 4:
        dma("y", rowstage[0:1, 0:1024], rw.rearrange("(b a) 1 -> b a", b=1), writes=(brow,))
        for j in range(0, D, 512):
            mm(PBIG[:, j:j + 512], ones1[0:1, :], rowstage[0:1, j:j + 512], True, True,
               reads=(cb, brow), writes=(bpb,))
        vop(lambda v: v.tensor_copy(rwbc[:, :], PBIG[:, :]), reads=(bpb,), writes=(brwbc,))
    bkey = buf("keyrow")
    pend = []
    for c in range(NCH if KPH >= 4 else 0):
        WOP = [PBIG[:, 0:512], PBIG[:, 512:1024]] if c % 2 == 0 else [PZ[0][:, :], PZ[1][:, :]]
        WOB = (bpb,) if c % 2 == 0 else (bpz[0], bpz[1])
        parts = []
        for j in range(8):
            for nh in range(2):
                parts.append((WOP[nh], attnT[j][:, 128 * c:128 * (c + 1)],
                              woc[j][:, 512 * nh:512 * (nh + 1)], j == 0, False))
        for nh in range(2):
            parts.append((WOP[nh], ones1[0:1, :],
                          biasrow[0:1, 512 * nh:512 * (nh + 1)], False, True))
        mm_raw(parts, reads=[cb, bbias] + batnT + bwoc, writes=WOB)
        for fn in pend:
            fn()
        pend = []
        dma("y", stg[1][:, :], xn_scr[128 * c:128 * (c + 1), :], reads=(bxn_s[c],),
            writes=(bstg[1],))
        if c % 2 == 0:
            vop(lambda v: v.scalar_tensor_tensor(stg[2][:, :], PBIG[:, :], 1.0, stg[1][:, :],
                                                 ALU.mult, ALU.add),
                reads=(bpb, bstg[1]), writes=(bstg[2],))
        else:
            vop(lambda v: v.scalar_tensor_tensor(stg[2][:, 0:512], PZ[0][:, :], 1.0,
                                                 stg[1][:, 0:512], ALU.mult, ALU.add),
                reads=(bpz[0], bstg[1]), writes=(bstg[2],))
            vop(lambda v: v.scalar_tensor_tensor(stg[2][:, 512:1024], PZ[1][:, :], 1.0,
                                                 stg[1][:, 512:1024], ALU.mult, ALU.add),
                reads=(bpz[1], bstg[1]), writes=(bstg[2],))
        ln_stats(stg[2][:, :], bstg[2])
        vop(lambda v: v.tensor_scalar(stg[2][:, :], stg[2][:, :], stat[:, 0:1], stat[:, 1:2],
                                      ALU.subtract, ALU.mult),
            reads=(bstg[2], bstat), writes=(bstg[2],))
        vop(lambda v: v.tensor_mul(stg[2][:, :], stg[2][:, :], gbc_g[:, :]),
            reads=(bstg[2], bgbc), writes=(bstg[2],))
        vop(lambda v: v.tensor_add(stg[2][:, :], stg[2][:, :], gbc_b[:, :]),
            reads=(bstg[2], bgbc), writes=(bstg[2],))
        dma("y", x1_scr[128 * c:128 * (c + 1), :], stg[2][:, :], reads=(bstg[2],),
            writes=(bx1_s[c],))
        vop(lambda v: v.tensor_mul(sq_scr[:, :], stg[2][:, :], rwbc[:, :]),
            reads=(bstg[2], brwbc), writes=(bsq,))
        vop(lambda v: v.reduce_sum(stat[:, 5:6], sq_scr[:, :], AX.X),
            reads=(bsq,), writes=(bstat,))
        act(nbf[0][:, :], stg[2][:, :], AF.Copy, reads=(bstg[2],), writes=(bstg[0],))

        def mk_pend(c):
            def fl():
                trans(PS6[0:1, 32:160], stat[:, 5:6], identf[:, :], reads=(bstat, cb),
                      writes=(bp6,))
                vop(lambda v, c=c: v.tensor_copy(keyrow[0:1, 128 * c:128 * (c + 1)],
                                                 PS6[0:1, 32:160]),
                    reads=(bp6,), writes=(bkey,))
                fm_transposes(nbf[0], bstg[0], c)
            return [fl]

        pend = mk_pend(c)
    for fn in pend:
        fn()

    P.barrier()

    bt5a = buf("t512a")
    bt5b = buf("t512b")
    bmT = buf("masksT")
    bmbc = buf("maskbc")
    bhm = buf("hmsk")
    bhs = [buf(f"hsb{m}") for m in range(16)]
    bw6 = [buf(f"wstp6{i}") for i in range(4)]
    for ns in range(4 if KPH >= 5 else 0):
        bt5c = buf("t512c")
        vop(lambda v, ns=ns: v.tensor_scalar(t512a[0:1, :], keyrow[0:1, 512 * ns:512 * (ns + 1)],
                                             rb_t[0:1, 0:1], 64.0, ALU.add, ALU.add),
            reads=(bkey, vb), writes=(bt5a,))
        for dstep in (64.0, 32.0, 16.0, 8.0, 4.0):
            vop(lambda v, d=dstep: v.tensor_scalar(t512b[0:1, :], t512a[0:1, :], d, -d,
                                                   ALU.is_ge, ALU.mult),
                reads=(bt5a,), writes=(bt5b,))
            vop(lambda v: v.tensor_add(t512a[0:1, :], t512a[0:1, :], t512b[0:1, :]),
                reads=(bt5a, bt5b), writes=(bt5a,))
        vop(lambda v: v.tensor_scalar(t512b[0:1, :], t512a[0:1, :], 1.0, None, ALU.is_ge),
            reads=(bt5a,), writes=(bt5b,))
        vop(lambda v: v.tensor_scalar(t512c[0:1, :], t512a[0:1, :], 2.0, None, ALU.is_ge),
            reads=(bt5a,), writes=(bt5c,))
        vop(lambda v: v.tensor_add(t512b[0:1, :], t512b[0:1, :], t512c[0:1, :]),
            reads=(bt5b, bt5c), writes=(bt5b,))
        vop(lambda v: v.tensor_scalar(t512c[0:1, :], t512a[0:1, :], 3.0, None, ALU.is_ge),
            reads=(bt5a,), writes=(bt5c,))
        vop(lambda v, ns=ns: v.tensor_add(keyrow[0:1, 512 * ns:512 * (ns + 1)], t512b[0:1, :],
                                          t512c[0:1, :]),
            reads=(bt5b, bt5c), writes=(bkey,))
    for ns in range(4 if KPH >= 5 else 0):
        mm(PZ[1][0:4, :], ones1[0:1, 0:4], keyrow[0:1, 512 * ns:512 * (ns + 1)], True, True,
           reads=(cb, bkey), writes=(bpz[1],))
        vop(lambda v, ns=ns: v.tensor_scalar(masksT[:, 512 * ns:512 * (ns + 1)], PZ[1][0:4, :],
                                             cvec4[:, :], None, ALU.is_equal),
            reads=(bpz[1], cb), writes=(bmT,))

    for ex in range(NEXP if KPH >= 5 else 0):
        for ns in range(4):
            vop(lambda v, ns=ns, ex=ex: v.tensor_scalar(
                    t512a[0:1, :], keyrow[0:1, 512 * ns:512 * (ns + 1)], float(ex), None,
                    ALU.is_equal),
                reads=(bkey,), writes=(bt5a,))
            mm(PZ[1][:, :], ones1[0:1, :], t512a[0:1, :], True, True,
               reads=(cb, bt5a), writes=(bpz[1],))
            vop(lambda v, ns=ns: v.tensor_copy(maskbc[:, 512 * ns:512 * (ns + 1)], PZ[1][:, :]),
                reads=(bpz[1],), writes=(bmbc,))
        for half in range(2):
            for mj in range(16):
                wr, wbr = wstp6[mj % 4], bw6[mj % 4]
                ffc = 2048 * half + 128 * mj
                dma("g", bass.AP(wr, 0, [[1024, 128], [128, 8], [1, 128]]),
                    w1[ex, :, ffc:ffc + 128].rearrange("(c p) n -> p c n", p=128), writes=(wbr,))
                if mj % 2 == 0:
                    pset = [PBIG[:, 0:512], PBIG[:, 512:1024], PZ[0][:, :], PZ[1][:, :]]
                    pbufs = [bpb, bpb, bpz[0], bpz[1]]
                else:
                    pset = [PX[0][:, :], PX[1][:, :], PS6[:, :], PY[:, :]]
                    pbufs = [bpx[0], bpx[1], bp6, bpy]
                parts = []
                for fi in range(8):
                    for ns in range(4):
                        parts.append((pset[ns], bass.AP(wr, 128 * fi, [[1024, 128], [1, 128]]),
                                      fm[fi][:, 512 * ns:512 * (ns + 1)], fi == 0, fi == 7))
                mm_raw(parts, reads=[wbr] + bfm, writes=tuple(set(pbufs)))
                bcol = b1_col[:, 32 * ex + 16 * half + mj:32 * ex + 16 * half + mj + 1]
                for ns in range(4):
                    act(hmsk[:, 512 * ns:512 * (ns + 1)], pset[ns], AF.Relu, bias=bcol,
                        reads=(pbufs[ns], vb), writes=(bhm,))
                vop(lambda v, mj=mj: v.tensor_mul(hsb[mj][:, :], hmsk[:, :], maskbc[:, :]),
                    reads=(bhm, bmbc), writes=(bhs[mj],))
                dma("g", rdb[mj][:, :],
                    w2[ex, 2048 * half + 128 * mj:2048 * half + 128 * (mj + 1), :],
                    writes=(brdb[mj],))
            for tt in range(16):
                PP = PX if tt % 2 == 0 else PZ
                PB_ = bpx if tt % 2 == 0 else bpz
                parts = []
                for mj in range(16):
                    for s2 in range(2):
                        last = (mj == 15) and not (ex == 0 and half == 0)
                        parts.append((PP[s2][:, :], hsb[mj][:, 128 * tt:128 * (tt + 1)],
                                      rdb[mj][:, 512 * s2:512 * (s2 + 1)], mj == 0, last))
                if ex == 0 and half == 0:
                    for s2 in range(2):
                        parts.append((PP[s2][:, :], masksT[:, 128 * tt:128 * (tt + 1)],
                                      b2_sb[:, 512 * s2:512 * (s2 + 1)], False, True))
                mm_raw(parts, reads=[vb, bmT] + bhs + brdb, writes=(PB_[0], PB_[1]))
                sst, bsst = stg[tt % 2], bstg[tt % 2]
                vop(lambda v, sst=sst, PP=PP: v.tensor_copy(sst[:, 0:512], PP[0][:, :]),
                    reads=(PB_[0],), writes=(bsst,))
                vop(lambda v, sst=sst, PP=PP: v.tensor_copy(sst[:, 512:1024], PP[1][:, :]),
                    reads=(PB_[1],), writes=(bsst,))
                dma("g", x1_scr[128 * tt:128 * (tt + 1), :], sst[:, :], reads=(bsst,),
                    writes=(bx1_s[tt],), accum_op=ALU.add)

    P.barrier()

    build_bc(gbc_g, gbc_b, l2g, l2b)
    for c in range(NCH if KPH >= 6 else 0):
        dma("y", stg[0][:, :], x1_scr[128 * c:128 * (c + 1), :], reads=(bx1_s[c],),
            writes=(bstg[0],))
        dma("y", stg[1][:, :], part_scr[128 * c:128 * (c + 1), :], reads=(bpart_s[c],),
            writes=(bstg[1],))
        ln_stats(stg[0][:, :], bstg[0])
        vop(lambda v: v.tensor_scalar(stg[2][:, :], stg[0][:, :], stat[:, 0:1], stat[:, 1:2],
                                      ALU.subtract, ALU.mult),
            reads=(bstg[0], bstat), writes=(bstg[2],))
        vop(lambda v: v.tensor_mul(stg[2][:, :], stg[2][:, :], gbc_g[:, :]),
            reads=(bstg[2], bgbc), writes=(bstg[2],))
        vop(lambda v: v.tensor_add(stg[2][:, :], stg[2][:, :], gbc_b[:, :]),
            reads=(bstg[2], bgbc), writes=(bstg[2],))
        vop(lambda v: v.tensor_add(stg[2][:, :], stg[2][:, :], stg[1][:, :]),
            reads=(bstg[2], bstg[1]), writes=(bstg[2],))
        dma("y", out_hbm[128 * c:128 * (c + 1), :], stg[2][:, :], reads=(bstg[2],))

    for q in DMAQ:
        for i in range(NLANE):
            lane = f"{q}{i}"
            P._wait(q, lane, P.cnt[lane])

    global _LASTP
    _LASTP = P
    with nc.Block() as block:
        block.tensor(lambda eng: _replay(P, "t", eng, sems))
        block.vector(lambda eng: _replay(P, "v", eng, sems))
        block.scalar(lambda eng: _replay(P, "s", eng, sems))
        block.gpsimd(lambda eng: _replay(P, "g", eng, sems))
        block.sync(lambda eng: _replay(P, "y", eng, sems))

    ctx.close()
    return nc


def kernel(**inputs):
    nc = build_program()
    consts = host_consts()
    x = np.ascontiguousarray(np.asarray(inputs["L0_attn_input_tensor"], dtype=np.float32))
    names = {
        "w_in": "L0_attn_W_in", "b_in": "L0_attn_b_in", "a_log": "L0_attn_A_log",
        "d_skip": "L0_attn_D_skip", "dt_bias": "L0_attn_dt_bias", "w_out": "L0_attn_W_out",
        "b_out": "L0_attn_b_out", "ln_g": "L0_attn_ln_gamma", "ln_b": "L0_attn_ln_beta",
        "wq": "L0_ffn_Wq", "bq": "L0_ffn_bq", "wk": "L0_ffn_Wk", "bk": "L0_ffn_bk",
        "wv": "L0_ffn_Wv", "bv": "L0_ffn_bv", "wo": "L0_ffn_Wo", "bo": "L0_ffn_bo",
        "rw": "L0_ffn_router_w", "rb": "L0_ffn_router_b", "w1": "L0_ffn_W1", "b1": "L0_ffn_b1",
        "w2": "L0_ffn_W2", "b2": "L0_ffn_b2", "l1g": "L0_ffn_ln1_g", "l1b": "L0_ffn_ln1_b",
        "l2g": "L0_ffn_ln2_g", "l2b": "L0_ffn_ln2_b",
    }
    base = {k: np.ascontiguousarray(np.asarray(inputs[v], np.float32)) for k, v in names.items()}
    base.update(consts)
    in_maps = []
    for c in range(8):
        m = dict(base)
        m["x_in"] = np.ascontiguousarray(x[c])
        in_maps.append(m)
    res = run_bass_kernel_spmd(nc, in_maps, core_ids=list(range(8)))
    return np.stack([res.results[c]["out"] for c in range(8)], axis=0)


if __name__ == "__main__":
    print("building program...")
    nc = build_program()
    print("built ok")


# version 17
# speedup vs baseline: 1.0482x; 1.0482x over previous
import os
import numpy as np
import ml_dtypes
import concourse.bass as bass
import concourse.mybir as mybir
from concourse.bass_utils import run_bass_kernel_spmd
from contextlib import ExitStack

F32 = mybir.dt.float32
BF16 = mybir.dt.bfloat16
AF = mybir.ActivationFunctionType
ALU = mybir.AluOpType
AX = mybir.AxisListType

T = 2048
D = 1024
NCH = 16
SCLEN = 512
NSC = 4
CPS = 4
INNER = 2048
NHM = 8
PD = 256
STATE = 64
PROJ = 4232
NHA = 16
NPAIR = 8
DFF = 4096
NEXP = 4
EPS = 1e-5
NEG = -1e30

ENGS = ("t", "v", "s", "g", "y")
DMAQ = ("g", "y")
NLANE = 8
SEMS = list(ENGS) + [f"{q}{i}" for q in DMAQ for i in range(NLANE)] + [f"{e}B" for e in ENGS]


class Buf:
    __slots__ = ("w", "r")

    def __init__(self):
        self.w = None
        self.r = {}


class Prog:
    def __init__(self):
        self.cnt = dict.fromkeys(SEMS, 0)
        self.wm = {c: dict.fromkeys(SEMS, 0) for c in ENGS}
        self.items = {e: [] for e in ENGS}
        self.lane_rr = dict.fromkeys(DMAQ, 0)

    def _wait(self, e, pe, v):
        if v <= 0 or v <= self.wm[e][pe]:
            return
        self.wm[e][pe] = v
        self.items[e].append(("w", pe, v))

    def wait_tok(self, e, tok):
        if tok is not None:
            self._wait(e, tok[0], tok[1])

    def do(self, e, fns, reads=(), writes=(), inc=None):
        if callable(fns):
            fns = (fns,)
        for b in reads:
            self.wait_tok(e, b.w)
        for b in writes:
            self.wait_tok(e, b.w)
            for pe, v in b.r.items():
                self._wait(e, pe, v)
        if e in DMAQ:
            lane = f"{e}{self.lane_rr[e]}"
            self.lane_rr[e] = (self.lane_rr[e] + 1) % NLANE
            self._wait(e, lane, self.cnt[lane])
            sem_key = lane
            inc_amt = 16
        else:
            sem_key = e
            inc_amt = 1 if inc is None else inc
        for fn in fns:
            self.items[e].append(("o", fn))
        self.cnt[sem_key] += inc_amt
        self.items[e].append(("i", sem_key, inc_amt))
        tok = (sem_key, self.cnt[sem_key])
        for b in reads:
            b.r[sem_key] = max(b.r.get(sem_key, 0), tok[1])
        for b in writes:
            b.w = tok
            b.r = {}
        return tok

    def barrier(self):
        toks = {}
        for e in ENGS:
            if e in DMAQ:
                for i in range(NLANE):
                    lane = f"{e}{i}"
                    self._wait(e, lane, self.cnt[lane])
            else:
                self._wait(e, e, self.cnt[e])
            b = f"{e}B"
            self.cnt[b] += 1
            self.items[e].append(("b", b, 1))
            toks[b] = self.cnt[b]
        for e in ENGS:
            for pe in ENGS:
                self._wait(e, f"{pe}B", toks[f"{pe}B"])


def _replay(P, e, eng, sems):
    last = None
    for it in P.items[e]:
        k = it[0]
        if k == "o":
            last = it[1](eng)
        elif k == "w":
            eng.wait_ge(sems[it[1]], it[2])
        elif k == "i":
            last.then_inc(sems[it[1]], it[2])
        elif k == "b":
            eng.nop().then_inc(sems[it[1]], it[2])


def host_consts():
    c = {}
    c["c_identf"] = np.eye(128, dtype=np.float32)
    c["c_identb"] = np.eye(128, dtype=np.float32).astype(ml_dtypes.bfloat16)
    c["c_ones1"] = np.ones((1, 128), dtype=np.float32)
    c["c_ones8"] = np.ones((8, 128), dtype=np.float32)
    hs = np.zeros((128, 2), dtype=np.float32)
    hs[0:64, 0] = 1.0
    hs[64:128, 1] = 1.0
    c["c_hsel"] = hs
    tri = np.where(np.arange(128)[:, None] <= np.arange(128)[None, :], 0.0, NEG).astype(np.float32)
    c["c_maskb8"] = np.tile(tri, (1, 8))
    mb = np.zeros((8, 1024), dtype=np.float32)
    for h in range(8):
        mb[h, 128 * h:128 * (h + 1)] = 1.0
    c["c_mask8"] = mb
    c["c_cvec4"] = np.arange(4, dtype=np.float32).reshape(4, 1)
    c["c_ones65"] = np.ones((65, 128), dtype=np.float32)
    c["c_imm"] = np.array([[EPS], [1.0]], dtype=np.float32) * np.ones((1, 1), np.float32)
    c["c_imm"] = np.concatenate([np.full((128, 1), EPS, np.float32),
                                 np.full((128, 1), 1.0, np.float32)], axis=1)
    return c


KPH = int(os.environ.get("KPH", "99"))
_LASTP = None


def build_program():
    nc = bass.Bass("TRN2", target_bir_lowering=False)

    def dram(name, shape, dtype=F32, kind="ExternalInput"):
        return nc.dram_tensor(name, list(shape), dtype, kind=kind).ap()

    x_in = dram("x_in", (T, D))
    w_in = dram("w_in", (D, PROJ))
    b_in = dram("b_in", (PROJ,))
    a_log = dram("a_log", (NHM,))
    d_skip = dram("d_skip", (NHM,))
    dt_bias = dram("dt_bias", (NHM,))
    w_out_d = dram("w_out", (INNER, D))
    b_out = dram("b_out", (D,))
    ln_g = dram("ln_g", (D,))
    ln_b = dram("ln_b", (D,))
    wq = dram("wq", (D, D))
    bq = dram("bq", (D,))
    wk = dram("wk", (D, D))
    bk = dram("bk", (D,))
    wv = dram("wv", (D, D))
    bv = dram("bv", (D,))
    wo = dram("wo", (D, D))
    bo = dram("bo", (D,))
    rw = dram("rw", (D, 1))
    rb = dram("rb", (1,))
    w1 = dram("w1", (NEXP, D, DFF))
    b1 = dram("b1", (NEXP, DFF))
    w2 = dram("w2", (NEXP, DFF, D))
    b2 = dram("b2", (NEXP, D))
    l1g = dram("l1g", (D,))
    l1b = dram("l1b", (D,))
    l2g = dram("l2g", (D,))
    l2b = dram("l2b", (D,))
    c_identf = dram("c_identf", (128, 128))
    c_identb = dram("c_identb", (128, 128), BF16)
    c_ones1 = dram("c_ones1", (1, 128))
    c_ones8 = dram("c_ones8", (8, 128))
    c_hsel = dram("c_hsel", (128, 2))
    c_maskb8 = dram("c_maskb8", (128, 1024))
    c_mask8 = dram("c_mask8", (8, 1024))
    c_cvec4 = dram("c_cvec4", (4, 1))
    c_ones65 = dram("c_ones65", (65, 128))
    c_imm = dram("c_imm", (128, 2))
    xn_scr = dram("xn_scr", (T, D), kind="Internal")
    part_scr = dram("part_scr", (T, D), kind="Internal")
    x1_scr = dram("x1_scr", (T, D), kind="Internal")
    out_hbm = dram("out", (T, D), kind="ExternalOutput")

    ARENA = 196 * 1024
    arena = nc.alloc_sbuf_tensor("arena", [128, ARENA], mybir.dt.uint8)
    arena_addr = nc.lookup_mloc(arena).addr

    def sub(name, shape, dtype, off):
        per_part = int(np.prod(shape[1:])) * mybir.dt.size(dtype)
        assert off + per_part <= ARENA, f"arena overflow: {name}"
        hnd = nc.virtual_sbuf_tensor(name, list(shape), dtype)
        nc._tensor(hnd.name, hnd.shape, hnd.dtype, type="SB")
        mloc = nc.lookup_mloc(hnd)
        mloc.allocated = True
        mloc.addr = arena_addr + off
        return hnd

    K = 1024
    CUR = [0]
    def nxt(name, shape, dtype):
        per = int(np.prod(shape[1:])) * mybir.dt.size(dtype)
        off = (CUR[0] + 31) & ~31
        CUR[0] = off + per
        return sub(name, shape, dtype, off)
    identf = nxt("identf", [128, 128], F32)
    identb = nxt("identb", [128, 128], BF16)
    ones1 = nxt("ones1", [1, 128], F32)
    ones8 = nxt("ones8", [8, 128], F32)
    hsel = nxt("hsel", [128, 2], F32)
    hselb = nxt("hselb", [128, 2], BF16)
    identD = nxt("identD", [128, 8 * 128], BF16)
    cvec4 = nxt("cvec4", [4, 1], F32)
    immt = nxt("immt", [128, 2], F32)
    ones65 = nxt("ones65", [65, 128], F32)
    maskb8 = nxt("maskb8", [128, 1024], F32)
    mask8 = nxt("mask8", [8, 1024], F32)
    b_in_col = nxt("b_in_col", [128, 33], F32)
    comb8 = nxt("comb8", [8, 1], F32)
    acol = nxt("acol", [8, 1], F32)
    tmp8a = nxt("tmp8a", [8, 1], F32)
    drow = nxt("drow", [1, 8], F32)
    dbc = nxt("dbc", [128, 8], F32)
    bqk_col = nxt("bqk_col", [128, 16], F32)
    rw_col = nxt("rw_col", [128, 8], BF16)
    rb_t = nxt("rb_t", [1, 1], F32)
    b1_col = nxt("b1_col", [128, 128], F32)
    lprevc = nxt("lprevc", [8, 1], F32)
    lprevbc = nxt("lprevbc", [128, 8], F32)
    decbc = nxt("decbc", [128, 8], F32)
    wtm = nxt("wtm", [128, 8], F32)
    lendrow = nxt("lendrow", [1, 8], F32)
    lcarry = nxt("lcarry", [8, 1], F32)
    stat = nxt("stat", [128, 16], F32)
    negm = nxt("negm", [128, 2], F32)
    nrow = nxt("nrow", [1, 2], F32)
    ltm = nxt("ltm", [128, 16], F32)
    nltm = nxt("nltm", [128, 8], F32)
    btm = nxt("btm", [128, 64], BF16)
    b2_sb = nxt("b2_sb", [4, D], BF16)
    assert CUR[0] <= 16 * K, CUR[0]
    RI = 16 * K
    rowstage = sub("rowstage", [1, 2048], F32, RI)
    gbc_g = sub("gbc_g", [128, 1024], F32, RI + 8 * K)
    gbc_b = sub("gbc_b", [128, 1024], F32, RI + 12 * K)
    gbc = sub("gbc", [128, 2048], F32, RI + 8 * K)
    wvc = [sub(f"wvc{i}", [128, 1024], BF16, RI + i * 2048) for i in range(8)]
    masksT = sub("masksT", [4, 2048], BF16, RI)
    maskbc = sub("maskbc", [128, 2048], BF16, RI + 4 * K)
    hmsk = sub("hmsk", [128, 2048], BF16, RI + 8 * K)
    RH = 32 * K
    stg = [sub(f"stg{i}", [128, 1024], F32, RH + i * 4096) for i in range(4)]
    nbf = [sub(f"nbf{i}", [128, 1024], BF16, RH + i * 4096) for i in range(2)]
    wstp6 = [sub(f"wstp6{i}", [128, 1024], BF16, RH + 8 * K + i * 2048) for i in range(4)]
    R64 = 48 * K
    szall = sub("szall", [128, 8192], BF16, R64)
    o3 = R64 + 16 * K
    def nx(name, shape, dtype):
        nonlocal o3
        per = int(np.prod(shape[1:])) * mybir.dt.size(dtype)
        off = (o3 + 31) & ~31
        o3 = off + per
        return sub(name, shape, dtype, off)
    bT_b = nx("bT_b", [64, 512], BF16)
    cT_b = nx("cT_b", [64, 512], BF16)
    dts = nx("dts", [8, 512], F32)
    lts = nx("lts", [8, 512], F32)
    ldec = nx("ldec", [8, 512], F32)
    mpre = nx("mpre", [128, 1024], F32)
    mexp = nx("mexp", [128, 1024], BF16)
    gst = nx("gst", [128, 1024], BF16)
    rmsk = nx("rmsk", [8, 1024], F32)
    cpb = nx("cpb", [64, 1024], BF16)
    bph = nx("bph", [128, 8 * 64], BF16)
    hst = nx("hst", [64, 8 * 256], F32)
    hstb = nx("hstb", [64, 8 * 256], BF16)
    xs_bf = nx("xs_bf", [128, 2048], BF16)
    dtx = nx("dtx", [128, 2048], BF16)
    yt = nx("yt", [128, 2048], BF16)
    rmat = nx("rmat", [8, 128], F32)
    btbc_sb = nx("btbc_sb", [128, 128], F32)
    assert o3 <= R64 + 65 * K, o3 - R64
    vtm = [sub(f"vtm{i}", [128, 1040], BF16, R64 + i * 2080) for i in range(16)]
    qpair = [sub(f"qpair{i}", [128, 2048], BF16, R64 + 34 * K + i * 4096) for i in range(2)]
    kpair = [sub(f"kpair{i}", [128, 2048], BF16, R64 + 42 * K + i * 4096) for i in range(2)]
    woc = [sub(f"woc{i}", [128, 1024], BF16, R64 + 50 * K + i * 2048) for i in range(7)] \
        + [sub("woc7", [128, 1024], BF16, R64 + 34 * K)]
    hsb = [sub(f"hsb{m}", [128, 2048], BF16, R64 + m * 4096) for m in range(16)]
    R32A = 113 * K
    fm = [sub(f"fm{i}", [128, 2048], BF16, R32A + i * 4096) for i in range(8)]
    R32B = 145 * K
    rdb = [sub(f"rdb{i}", [128, 1024], BF16, R32B + i * 2048) for i in range(16)]
    attnT = [sub(f"attnT{i}", [128, 2048], BF16, R32B + i * 4096) for i in range(8)]
    R16 = 177 * K
    xsfm = [sub(f"xsfm{i}", [128, 512], BF16, R16 + i * 1024) for i in range(16)]
    wqk = [sub(f"wqk{i}", [128, 1024], BF16, R16 + i * 2048) for i in range(2)]
    ptt = [sub(f"ptt{i}", [128, 1024], BF16, R16 + 4 * K + i * 2048) for i in range(2)]
    ptt2 = [sub(f"ptt2_{i}", [128, 1024], BF16, R16 + 8 * K + i * 2048) for i in range(2)]
    biasrow = sub("biasrow", [1, 1024], F32, R16 + 8 * K)
    rc65 = sub("rc65", [65, 1024], F32, R16 + 12 * K)
    q2s8 = sub("q2s", [128, 4096], BF16, RI + 8 * K)
    keyrow = sub("keyrow", [1, 2048], F32, R16)
    t512a = sub("t512a", [1, 512], F32, R16 + 8 * K)
    t512b = sub("t512b", [1, 512], F32, R16 + 10 * K)
    t512i = sub("t512i", [1, 512], mybir.dt.int32, R16 + 12 * K)
    t512c = sub("t512c", [1, 512], F32, R16 + 14 * K)

    ctx = ExitStack()
    e = ctx.enter_context
    PZ = [e(nc.psum_tensor(f"PZ{i}", [128, 512], F32)) for i in range(2)]
    PX = [e(nc.psum_tensor(f"PX{i}", [128, 512], F32)) for i in range(2)]
    PBIG = e(nc.psum_tensor("PBIG", [128, 1024], F32))
    PS6 = e(nc.psum_tensor("PS6", [128, 512], F32))
    PY = e(nc.psum_tensor("PY", [128, 512], F32))

    def sub_ps(name, shape, dtype, base_handle):
        hnd = nc.virtual_psum_tensor(name, list(shape), dtype)
        nc._tensor(hnd.name, hnd.shape, hnd.dtype, type="PSUM")
        mloc = nc.lookup_mloc(hnd)
        bm = nc.lookup_mloc(base_handle)
        mloc.allocated = True
        mloc.addr = bm.addr
        mloc.bank = bm.bank
        mloc.base = bm.base
        return hnd


    sems = {k: e(nc.semaphore(f"sem_{k}")) for k in SEMS}
    P = Prog()
    _B = {}

    def buf(name):
        if name not in _B:
            _B[name] = Buf()
        return _B[name]

    bpz = [buf("pz0"), buf("pz1")]
    bpx = [buf("px0"), buf("px1")]
    bpb = buf("pbig")
    bp6 = buf("ps6")
    bpy = buf("py")

    bxn_s = [buf(f"xns{c}") for c in range(NCH)]
    bpart_s = [buf(f"parts{c}") for c in range(NCH)]
    bx1_s = [buf(f"x1s{c}") for c in range(NCH)]

    def dma(qe, out_ap, in_ap, reads=(), writes=(), **kw):
        return P.do(qe, lambda g, o=out_ap, i=in_ap, kw=dict(kw): g.dma_start(out=o, in_=i, **kw),
                    reads=reads, writes=writes)

    def mm(out_ap, lhsT_ap, rhs_ap, start, stop, reads=(), writes=()):
        return P.do("t", lambda t, o=out_ap, l=lhsT_ap, r=rhs_ap, a=start, b=stop:
                    t.matmul(o, l, r, start=a, stop=b),
                    reads=reads, writes=writes)

    def mm_raw(parts, reads=(), writes=()):
        fns = [(lambda t, o=o, l=l, r=r, s=s, st=st: t.matmul(o, l, r, start=s, stop=st))
               for (o, l, r, s, st) in parts]
        return P.do("t", fns, reads=reads, writes=writes)

    def act(out_ap, in_ap, func, bias=0.0, scale=1.0, accum=None, reads=(), writes=()):
        return P.do("s", lambda s, o=out_ap, i=in_ap, f=func, b=bias, sc=scale, a=accum:
                    s.activation(o, i, f, bias=b, scale=sc, accum_out=a),
                    reads=reads, writes=writes)

    def vop(fn, reads=(), writes=()):
        return P.do("v", fn, reads=reads, writes=writes)

    def trans(out_ap, in_ap, ident_ap, reads=(), writes=()):
        return P.do("t", lambda t, o=out_ap, i=in_ap, d=ident_ap:
                    t.matmul(o, i, d, start=True, stop=True),
                    reads=reads, writes=writes)

    cb = buf("consts")
    vb = buf("vecs")
    bstg = [buf(f"stg{i}") for i in range(4)]
    bfm = [buf(f"fm{i}") for i in range(8)]
    bgbc = buf("gbc")
    brow = buf("rowstage")
    bstat = buf("stat")

    dma("y", identf[:, :], c_identf, writes=(cb,))
    dma("y", identb[:, :], c_identb, writes=(cb,))
    dma("y", ones1[:, :], c_ones1, writes=(cb,))
    dma("y", ones8[:, :], c_ones8, writes=(cb,))
    dma("y", hsel[:, :], c_hsel, writes=(cb,))
    dma("g", hselb[:, :], c_hsel, writes=(cb,))
    dma("y", maskb8[:, :], c_maskb8, writes=(cb,))
    dma("y", mask8[:, :], c_mask8, writes=(cb,))
    dma("y", cvec4[:, :], c_cvec4, writes=(cb,))
    dma("y", ones65[:, :], c_ones65, writes=(cb,))
    dma("y", immt[:, :], c_imm, writes=(cb,))
    nc.const_aps.register(F32, EPS, immt[:, 0:1])
    nc.const_aps.register(F32, 1.0, immt[:, 1:2])
    dma("y", b_in_col[:, :], b_in[0:4224].rearrange("(m p) -> p m", p=128), writes=(vb,),
        allow_slow_non_contiguous=True)
    dma("y", tmp8a[:, :], b_in[4224:4232].rearrange("(a b) -> a b", b=1), writes=(vb,),
        allow_slow_non_contiguous=True)
    dma("y", comb8[:, :], dt_bias.rearrange("(a b) -> a b", b=1), writes=(vb,),
        allow_slow_non_contiguous=True)
    dma("y", acol[:, :], a_log.rearrange("(a b) -> a b", b=1), writes=(vb,),
        allow_slow_non_contiguous=True)
    dma("y", drow[:, :], d_skip.rearrange("(b a) -> b a", b=1), writes=(vb,))
    dma("y", bqk_col[:, 0:8], bq.rearrange("(m p) -> p m", p=128), writes=(vb,),
        allow_slow_non_contiguous=True)
    dma("y", bqk_col[:, 8:16], bk.rearrange("(m p) -> p m", p=128), writes=(vb,),
        allow_slow_non_contiguous=True)
    dma("y", rb_t[:, :], rb.rearrange("(a b) -> a b", b=1), writes=(vb,))
    dma("y", b1_col[:, :], b1.rearrange("e (m p) -> p (e m)", p=128), writes=(vb,),
        allow_slow_non_contiguous=True)
    dma("g", b2_sb[:, :], b2[:, :], writes=(vb,))
    vop(lambda v: v.tensor_add(comb8[:, :], comb8[:, :], tmp8a[:, :]), reads=(vb,), writes=(vb,))
    act(acol[:, :], acol[:, :], AF.Exp, reads=(vb,), writes=(vb,))
    vop(lambda v: v.tensor_scalar(acol[:, :], acol[:, :], -1.0, None, ALU.mult),
        reads=(vb,), writes=(vb,))
    mm(PS6[:, 0:8], ones1[0:1, :], drow[0:1, :], True, True, reads=(cb, vb), writes=(bp6,))
    vop(lambda v: v.tensor_copy(dbc[:, :], PS6[:, 0:8]), reads=(bp6,), writes=(vb,))
    for h in range(NHM):
        vop(lambda v, h=h: v.tensor_scalar(identD[:, 128 * h:128 * (h + 1)], identb[:, :],
                                           dbc[:, h:h + 1], None, ALU.mult),
            reads=(cb, vb), writes=(cb,))

    def build_bc(dst_g, dst_b, src_g, src_b_):
        for dst, src in ((dst_g, src_g), (dst_b, src_b_)):
            dma("y", rowstage[0:1, 0:D], src.rearrange("(b a) -> b a", b=1), writes=(brow,))
            for j in range(0, D, 512):
                mm(PBIG[:, j:j + 512], ones1[0:1, :], rowstage[0:1, j:j + 512], True, True,
                   reads=(cb, brow), writes=(bpb,))
            vop(lambda v, dst=dst: v.tensor_copy(dst[:, :], PBIG[:, :]),
                reads=(bpb,), writes=(bgbc,))

    sq_scr = stg[3]
    bsq = bstg[3]

    def ln_stats(src_ap, src_buf, so=0, neg_mu=False):
        act(sq_scr[:, :], src_ap, AF.Square, accum=stat[:, so + 2:so + 3], reads=(src_buf,),
            writes=(bsq, bstat))
        vop(lambda v, so=so: v.reduce_sum(stat[:, so + 3:so + 4], src_ap, AX.X),
            reads=(src_buf,), writes=(bstat,))
        vop(lambda v, so=so, sc=(-1.0 / D if neg_mu else 1.0 / D): v.tensor_scalar(
                stat[:, so:so + 1], stat[:, so + 3:so + 4], sc, None, ALU.mult),
            reads=(bstat,), writes=(bstat,))
        vop(lambda v, so=so: v.tensor_scalar(stat[:, so + 2:so + 3], stat[:, so + 2:so + 3],
                                             1.0 / D, None, ALU.mult),
            reads=(bstat,), writes=(bstat,))
        vop(lambda v, so=so: v.tensor_mul(stat[:, so + 4:so + 5], stat[:, so:so + 1],
                                          stat[:, so:so + 1]),
            reads=(bstat,), writes=(bstat,))
        vop(lambda v, so=so: v.tensor_sub(stat[:, so + 4:so + 5], stat[:, so + 2:so + 3],
                                          stat[:, so + 4:so + 5]),
            reads=(bstat,), writes=(bstat,))
        act(stat[:, so + 4:so + 5], stat[:, so + 4:so + 5], AF.Sqrt, bias=EPS,
            reads=(bstat,), writes=(bstat,))
        vop(lambda v, so=so: v.reciprocal(stat[:, so + 1:so + 2], stat[:, so + 4:so + 5]),
            reads=(bstat,), writes=(bstat,))

    def fm_transposes(src_bf_ap, src_buf, c):
        for j in range(8):
            pt = PS6[:, 0:128] if j % 2 == 0 else PY[:, 0:128]
            tb = bp6 if j % 2 == 0 else bpy
            trans(pt, bass.AP(src_bf_ap, 128 * j, [[1024, 128], [1, 128]]),
                  identb[:, :], reads=(src_buf, cb), writes=(tb,))
            if j % 2 == 0:
                vop(lambda v, pt=pt, j=j, c=c: v.tensor_copy(
                        fm[j][:, 128 * c:128 * (c + 1)], pt),
                    reads=(tb,), writes=(bfm[j],))
            else:
                P.do("s", lambda s, pt=pt, j=j, c=c: s.copy(
                        fm[j][:, 128 * c:128 * (c + 1)], pt),
                     reads=(tb,), writes=(bfm[j],))

    build_bc(gbc_g, gbc_b, ln_g, ln_b)
    sn2 = sub("sn2", [128, 1024], F32, R64 + 20 * K)
    bsn2 = buf("sn2")
    for c in range(NCH if KPH >= 1 else 0):
        so = 8 * (c % 2)
        sx, bx = stg[c % 2], bstg[c % 2]
        sn, bn = (stg[2], bstg[2]) if c % 2 == 0 else (sn2, bsn2)
        dma("y", sx[:, :], x_in[128 * c:128 * (c + 1), :], writes=(bx,))
        ln_stats(sx[:, :], bx, so=so, neg_mu=True)
        vop(lambda v, so=so: v.tensor_mul(stat[:, so + 5:so + 6], stat[:, so:so + 1],
                                          stat[:, so + 1:so + 2]),
            reads=(bstat,), writes=(bstat,))
        act(sn[:, :], sx[:, :], AF.Identity, scale=stat[:, so + 1:so + 2],
            bias=stat[:, so + 5:so + 6], reads=(bx, bstat), writes=(bn,))
        dma("y", xn_scr[128 * c:128 * (c + 1), :], sn[:, :], reads=(bn,), writes=(bxn_s[c],))
        ln_stats(sn[:, :], bn, so=so, neg_mu=True)
        vop(lambda v, so=so: v.tensor_mul(stat[:, so + 5:so + 6], stat[:, so:so + 1],
                                          stat[:, so + 1:so + 2]),
            reads=(bstat,), writes=(bstat,))
        act(sn[:, :], sn[:, :], AF.Identity, scale=stat[:, so + 1:so + 2],
            bias=stat[:, so + 5:so + 6], reads=(bn, bstat), writes=(bn,))
        vop(lambda v, sn=sn: v.tensor_mul(sn[:, :], sn[:, :], gbc_g[:, :]),
            reads=(bn, bgbc), writes=(bn,))
        nb, bnb = nbf[c % 2], bstg[c % 2]
        vop(lambda v, nb=nb, sn=sn: v.tensor_add(nb[:, :], sn[:, :], gbc_b[:, :]),
            reads=(bn, bgbc), writes=(bnb,))
        fm_transposes(nb, bnb, c)

    P.barrier()

    dma("y", rowstage[0:1, 0:1024], b_out.rearrange("(b a) -> b a", b=1), writes=(brow,))

    bhst = buf("hst")
    bhstb = [buf(f"hstb{h}") for h in range(8)]
    blpc = buf("lpc")
    blc = buf("lcarry")
    vop(lambda v: v.memset(hst[:, :], 0.0), writes=(bhst,))
    vop(lambda v: v.memset(hstb[:, :], 0.0), writes=tuple(bhstb))
    vop(lambda v: v.memset(lprevc[:, :], 0.0), writes=(blpc,))
    vop(lambda v: v.memset(lprevbc[:, :], 0.0), writes=(blpc,))
    vop(lambda v: v.memset(lcarry[:, :], 0.0), writes=(blc,))

    MODE_B = [33, 32] + list(range(16, 32)) + list(range(16))

    brdb = [buf(f"rdb{i}") for i in range(16)]
    bxsf = [buf(f"xsfm{i}") for i in range(16)]
    bsz = [buf(f"siluz{m}") for m in range(16)]
    bbct = buf("bct")
    bdtl = buf("dtl")
    bldec = buf("ldec")
    bltm = buf("ltm")
    blend = buf("lend")
    bmpre = buf("mpre")
    bnltm = buf("nltm")
    bbtbc = buf("btbc")
    bmexp = buf("mexp")
    bgst = buf("gst")
    brm = buf("rmat")
    brmsk = buf("rmsk")
    bcpb = buf("cpb")
    bbtm = buf("btm")
    bbph = buf("bph")
    bxsb = buf("xs_bf")
    bdtx = buf("dtx")
    byt = [buf(f"yt{j}") for j in range(16)]
    bwm = buf("wdec")

    for j in range(12 if KPH >= 2 else 0):
        dma("g", rdb[j][:, :], w_out_d[128 * j:128 * (j + 1), :], writes=(brdb[j],))

    def emit_wout(c):
        for j in range(16):
            mm_raw([(PY[:, :], yt[:, 128 * j:128 * (j + 1)],
                     rdb[j][:, 0:512], j == 0, False),
                    (PX[0][:, :], yt[:, 128 * j:128 * (j + 1)],
                     rdb[j][:, 512:1024], j == 0, False)],
                   reads=(byt[j], brdb[j]), writes=(bpy, bpx[0]))
        mm_raw([(PY[:, :], ones1[0:1, :], rowstage[0:1, 0:512], False, True),
                (PX[0][:, :], ones1[0:1, :], rowstage[0:1, 512:1024], False, True)],
               reads=(brow, cb), writes=(bpy, bpx[0]))
        dma("y", stg[0][:, :], xn_scr[128 * c:128 * (c + 1), :], reads=(bxn_s[c],),
            writes=(bstg[0],))
        dma("y", stg[1][:, :], x_in[128 * c:128 * (c + 1), :], writes=(bstg[1],))
        vop(lambda v: v.scalar_tensor_tensor(stg[2][:, 0:512], PY[:, :], 1.0,
                                             stg[0][:, 0:512], ALU.mult, ALU.add),
            reads=(bpy, bstg[0]), writes=(bstg[2],))
        vop(lambda v: v.scalar_tensor_tensor(stg[2][:, 512:1024], PX[0][:, :], 1.0,
                                             stg[0][:, 512:1024], ALU.mult, ALU.add),
            reads=(bpx[0], bstg[0]), writes=(bstg[2],))
        vop(lambda v: v.tensor_add(stg[2][:, :], stg[2][:, :], stg[1][:, :]),
            reads=(bstg[2], bstg[1]), writes=(bstg[2],))
        dma("y", part_scr[128 * c:128 * (c + 1), :], stg[2][:, :], reads=(bstg[2],),
            writes=(bpart_s[c],))

    def emit_chain(c, mid=None):
        ct0 = (c % CPS) * 128
        for g in range(4):
            pz, pzb = PZ[g % 2], bpz[g % 2]
            for k in range(4):
                j = 4 * g + k
                trans(pz[:, 128 * k:128 * (k + 1)], xsfm[j][:, ct0:ct0 + 128],
                      identb[:, :], reads=(bxsf[j], cb), writes=(pzb,))
            if g % 2 == 0:
                vop(lambda v, pz=pz, g=g: v.tensor_copy(
                        xs_bf[:, 512 * g:512 * (g + 1)], pz[:, 0:512]),
                    reads=(pzb,), writes=(bxsb,))
            else:
                P.do("s", lambda s, pz=pz, g=g: s.copy(
                        xs_bf[:, 512 * g:512 * (g + 1)], pz[:, 0:512]),
                     reads=(pzb,), writes=(bxsb,))
        mm(PZ[0][:, 160:288], bT_b[:, ct0:ct0 + 128], cT_b[:, ct0:ct0 + 128], True, True,
           reads=(bbct,), writes=(bpz[0],))
        trans(PZ[0][:, 288:352], bT_b[:, ct0:ct0 + 128], identb[0:64, 0:64],
              reads=(bbct, cb), writes=(bpz[0],))
        vop(lambda v: v.tensor_copy(btbc_sb[:, :], PZ[0][:, 160:288]),
            reads=(bpz[0],), writes=(bbtbc,))
        vop(lambda v: v.tensor_copy(btm[:, :], PZ[0][:, 288:352]),
            reads=(bpz[0],), writes=(bbtm,))
        trans(PZ[0][:, 128:136], dts[:, ct0:ct0 + 128], identf[0:8, 0:8],
              reads=(bdtl, cb), writes=(bpz[0],))
        trans(PZ[0][:, 136:144], lts[:, ct0:ct0 + 128], identf[0:8, 0:8],
              reads=(bdtl, cb), writes=(bpz[0],))
        vop(lambda v: v.tensor_copy(ltm[:, :], PZ[0][:, 128:144]),
            reads=(bpz[0],), writes=(bltm,))
        trans(PZ[0][0:1, 144:152], lts[:, ct0 + 127:ct0 + 128], identf[0:8, 0:8],
              reads=(bdtl, cb), writes=(bpz[0],))
        vop(lambda v: v.tensor_copy(lendrow[:, :], PZ[0][0:1, 144:152]),
            reads=(bpz[0],), writes=(blend,))
        mm(PZ[0][:, 152:160], ones1[0:1, :], lendrow[0:1, :], True, True,
           reads=(cb, blend), writes=(bpz[0],))
        vop(lambda v: v.tensor_sub(wtm[:, :], PZ[0][:, 152:160], ltm[:, 8:16]),
            reads=(bpz[0], bltm), writes=(bwm,))
        act(wtm[:, :], wtm[:, :], AF.Exp, reads=(bwm,), writes=(bwm,))
        vop(lambda v: v.tensor_sub(decbc[:, :], PZ[0][:, 152:160], lprevbc[:, :]),
            reads=(bpz[0], blpc), writes=(bwm,))
        act(decbc[:, :], decbc[:, :], AF.Exp, reads=(bwm,), writes=(bwm,))
        vop(lambda v, ct0=ct0: v.tensor_scalar(rmat[:, :], lts[:, ct0:ct0 + 128],
                                               lprevc[:, :], None, ALU.subtract),
            reads=(bdtl, blpc), writes=(brm,))
        act(rmat[:, :], rmat[:, :], AF.Exp, reads=(brm,), writes=(brm,))
        vop(lambda v, ct0=ct0: v.tensor_copy(lprevc[:, :], lts[:, ct0 + 127:ct0 + 128]),
            reads=(bdtl,), writes=(blpc,))
        vop(lambda v: v.tensor_copy(lprevbc[:, :], PZ[0][:, 152:160]),
            reads=(bpz[0],), writes=(blpc,))
        vop(lambda v, ct0=ct0: v.tensor_mul(
                rmsk[:, :], bass.AP(lts, ct0, [[512, 8], [0, 8], [1, 128]]),
                mask8[:, :]),
            reads=(bdtl, cb), writes=(brmsk,))
        act(nltm[:, :], ltm[:, 8:16], AF.Identity, scale=-1.0,
            reads=(bltm,), writes=(bnltm,))
        for half in range(2):
            mm(PZ[1][:, 0:512], ones8[:, :], rmsk[:, 512 * half:512 * (half + 1)],
               True, True, reads=(cb, brmsk), writes=(bpz[1],))
            for hh in range(4):
                h = 4 * half + hh
                act(mpre[:, 128 * h:128 * (h + 1)], PZ[1][:, 128 * hh:128 * (hh + 1)],
                    AF.Identity, bias=nltm[:, h:h + 1],
                    reads=(bpz[1], bnltm), writes=(bmpre,))
        vop(lambda v: v.tensor_add(mpre[:, :], mpre[:, :], maskb8[:, :]),
            reads=(bmpre, cb), writes=(bmpre,))
        act(mexp[:, :], mpre[:, :], AF.Exp, reads=(bmpre,), writes=(bmexp,))
        vop(lambda v: v.tensor_mul(
                bass.AP(gst, 0, [[1024, 128], [128, 8], [1, 128]]),
                bass.AP(mexp, 0, [[1024, 128], [128, 8], [1, 128]]),
                bass.AP(btbc_sb, 0, [[128, 128], [0, 8], [1, 128]])),
            reads=(bmexp, bbtbc), writes=(bgst,))
        for h in range(NHM):
            act(bph[:, 64 * h:64 * (h + 1)], btm[:, :], AF.Identity,
                scale=wtm[:, h:h + 1], reads=(bbtm, bwm), writes=(bbph,))
        vop(lambda v: v.tensor_mul(rmsk[:, :], bass.AP(rmat, 0, [[128, 8], [0, 8], [1, 128]]),
                                   mask8[:, :]),
            reads=(brm, cb), writes=(brmsk,))
        if mid is not None:
            mid()
        for half in range(2):
            mm(PZ[1][0:64, 0:512], ones8[:, 0:64], rmsk[:, 512 * half:512 * (half + 1)],
               True, True, reads=(cb, brmsk), writes=(bpz[1],))
            vop(lambda v, half=half, ct0=ct0: v.tensor_mul(
                    bass.AP(cpb, 512 * half, [[1024, 64], [128, 4], [1, 128]]),
                    bass.AP(cT_b, ct0, [[512, 64], [0, 4], [1, 128]]),
                    bass.AP(PZ[1], 0, [[512, 64], [128, 4], [1, 128]])),
                reads=(bbct, bpz[1]), writes=(bcpb,))
        for h in range(NHM):
            act(dtx[:, 256 * h:256 * (h + 1)], xs_bf[:, 256 * h:256 * (h + 1)],
                AF.Identity, scale=ltm[:, h:h + 1],
                reads=(bxsb, bltm), writes=(bdtx,))

    YSLOT = [(PBIG, 0, bpb), (PBIG, 256, bpb), (PBIG, 512, bpb), (PBIG, 768, bpb),
             (PY, 0, bpy), (PY, 256, bpy), (PX[0], 0, bpx[0]), (PX[0], 256, bpx[0])]

    def emit_y(c):
        ct0 = (c % CPS) * 128
        for h in range(NHM):
            ypp, yc0, ypb = YSLOT[h]
            parts = []
            for q in range(2):
                o = ypp[:, yc0 + 128 * q:yc0 + 128 * (q + 1)]
                parts.append((o, dtx[:, 256 * h + 128 * q:256 * h + 128 * (q + 1)],
                              gst[:, 128 * h:128 * (h + 1)], q == 0, False))
            for q in range(2):
                o = ypp[:, yc0 + 128 * q:yc0 + 128 * (q + 1)]
                parts.append((o, xs_bf[:, 256 * h + 128 * q:256 * h + 128 * (q + 1)],
                              identD[:, 128 * h:128 * (h + 1)], False, False))
            for q in range(2):
                o = ypp[:, yc0 + 128 * q:yc0 + 128 * (q + 1)]
                parts.append((o, bass.AP(hstb, 256 * h + 128 * q, [[2048, 64], [1, 128]]),
                              cpb[:, 128 * h:128 * (h + 1)], False, q == 1))
            mm_raw(parts, reads=[bdtx, bgst, bxsb, cb, bhstb[h], bcpb], writes=(ypb,))
            hps = PS6[0:64, 0:256] if h % 2 == 0 else PX[1][0:64, 0:256]
            hpb = bp6 if h % 2 == 0 else bpx[1]
            mm(hps, bph[:, 64 * h:64 * (h + 1)], dtx[:, 256 * h:256 * (h + 1)],
               True, True, reads=(bbph, bdtx), writes=(hpb,))
            vop(lambda v, h=h, hps=hps: v.scalar_tensor_tensor(
                    bass.AP(hst, 256 * h, [[2048, 64], [1, 256]]),
                    bass.AP(hst, 256 * h, [[2048, 64], [1, 256]]),
                    decbc[0:64, h:h + 1], hps, ALU.mult, ALU.add),
                reads=(hpb, bwm, bhst), writes=(bhst,))
            vop(lambda v, h=h: v.tensor_copy(
                    bass.AP(hstb, 256 * h, [[2048, 64], [1, 256]]),
                    bass.AP(hst, 256 * h, [[2048, 64], [1, 256]])),
                reads=(bhst,), writes=(bhstb[h],))
        for (ypp, ypb, m0, nt) in ((PBIG, bpb, 0, 8), (PY, bpy, 8, 4), (PX[0], bpx[0], 12, 4)):
            vop(lambda v, ypp=ypp, m0=m0, nt=nt, ct0=ct0: v.tensor_mul(
                    bass.AP(yt, 128 * m0, [[2048, 128], [128, nt], [1, 128]]),
                    bass.AP(ypp, 0, [[128 * nt, 128], [128, nt], [1, 128]]),
                    bass.AP(szall, 512 * m0 + ct0, [[8192, 128], [512, nt], [1, 128]])),
                reads=tuple([ypb] + [bsz[m0 + i] for i in range(nt)]),
                writes=tuple(byt[m0 + i] for i in range(nt)))

    for sc in range(NSC if KPH >= 2 else 0):
        t0 = sc * SCLEN
        for mi, m in enumerate(MODE_B):
            wr, wbr = rdb[12 + mi % 4], brdb[12 + mi % 4]
            cw = 128 if m < 33 else 8
            dma("g", bass.AP(wr, 0, [[1024, 128], [128, 8], [1, cw]]),
                w_in[:, 128 * m:128 * m + cw].rearrange("(c p) n -> p c n", p=128),
                writes=(wbr,))
            pz, pzb = PZ[mi % 2], bpz[mi % 2]
            parts = []
            for fi in range(8):
                parts.append((pz[0:cw, 0:512],
                              bass.AP(wr, 128 * fi, [[1024, 128], [1, cw]]),
                              fm[fi][:, t0:t0 + 512], fi == 0, fi == 7))
            mm_raw(parts, reads=[wbr] + bfm, writes=(pzb,))
            if m < 16:
                act(bass.AP(szall, 512 * m, [[8192, 128], [1, 512]]), pz[:, :],
                    AF.Silu, bias=b_in_col[:, m:m + 1],
                    reads=(pzb, vb), writes=(bsz[m],))
            elif m < 32:
                act(xsfm[m - 16][:, :], pz[:, :], AF.Identity, bias=b_in_col[:, m:m + 1],
                    reads=(pzb, vb), writes=(bxsf[m - 16],))
            elif m == 32:
                act(bT_b[:, :], pz[0:64, :], AF.Identity, bias=b_in_col[0:64, 32:33],
                    reads=(pzb, vb), writes=(bbct,))
                act(cT_b[:, :], pz[64:128, :], AF.Identity, bias=b_in_col[64:128, 32:33],
                    reads=(pzb, vb), writes=(bbct,))
            else:
                act(dts[:, :], pz[0:8, :], AF.Identity, bias=comb8[:, :],
                    reads=(pzb, vb), writes=(bdtl,))
            if mi == 1:
                act(ldec[:, :], dts[:, :], AF.Abs, reads=(bdtl,), writes=(bldec,))
                act(ldec[:, :], ldec[:, :], AF.Exp, scale=-1.0, reads=(bldec,), writes=(bldec,))
                act(ldec[:, :], ldec[:, :], AF.Ln, bias=1.0, reads=(bldec,), writes=(bldec,))
                vop(lambda v: v.tensor_scalar(rmsk[0:8, 0:512], dts[:, :], 0.0, None, ALU.max),
                    reads=(bdtl,), writes=(brmsk,))
                vop(lambda v: v.tensor_add(dts[:, :], ldec[:, :], rmsk[0:8, 0:512]),
                    reads=(bldec, brmsk), writes=(bdtl,))
                vop(lambda v: v.tensor_scalar(ldec[:, :], dts[:, :], acol[:, :], None, ALU.mult),
                    reads=(bdtl, vb), writes=(bldec,))
                vop(lambda v: v.memset(rmsk[0:8, 0:512], 0.0), writes=(brmsk,))
                vop(lambda v: v.tensor_tensor_scan(lts[:, :], ldec[:, :], rmsk[0:8, 0:512],
                                                   lcarry[:, :], ALU.add, ALU.add),
                    reads=(bldec, brmsk, blc), writes=(bdtl,))
                vop(lambda v: v.tensor_copy(lcarry[:, :], lts[:, 511:512]),
                    reads=(bdtl,), writes=(blc,))
        for j in (12, 13, 14, 15):
            dma("g", rdb[j][:, :], w_out_d[128 * j:128 * (j + 1), :], writes=(brdb[j],))
        emit_chain(sc * CPS)

        for tc in range(CPS):
            c = sc * CPS + tc
            emit_y(c)
            if tc < CPS - 1:
                emit_chain(c + 1, mid=lambda c=c: emit_wout(c))
            else:
                emit_wout(c)

    P.barrier()

    for c in range(NCH if KPH >= 3 else 0):
        nb, bnb = nbf[c % 2], bstg[c % 2]
        dma("g", nb[:, :], xn_scr[128 * c:128 * (c + 1), :], reads=(bxn_s[c],), writes=(bnb,))
        fm_transposes(nb, bnb, c)

    bwvc = [buf(f"wvc{i}") for i in range(8)]
    bvtm = [buf(f"vtm{i}") for i in range(16)]
    bbias = buf("biasrow")
    for j in range(8 if KPH >= 3 else 0):
        dma("g", wvc[j][:, :], wv[128 * j:128 * (j + 1), :], writes=(bwvc[j],))
    dma("y", biasrow[0:1, :], bv.rearrange("(b a) -> b a", b=1), writes=(bbias,))
    for tt in range(16 if KPH >= 3 else 0):
        vop(lambda v, tt=tt: v.memset(bass.AP(vtm[tt], 64, [[1040, 128], [65, 16]]), 1.0),
            writes=(bvtm[tt],))
        parts = []
        for fi in range(8):
            for nh in range(2):
                parts.append((PBIG[:, 512 * nh:512 * (nh + 1)],
                              fm[fi][:, 128 * tt:128 * (tt + 1)],
                              wvc[fi][:, 512 * nh:512 * (nh + 1)], fi == 0, False))
        for nh in range(2):
            parts.append((PBIG[:, 512 * nh:512 * (nh + 1)], ones1[0:1, :],
                          biasrow[0:1, 512 * nh:512 * (nh + 1)], False, True))
        mm_raw(parts, reads=[cb, bbias] + bfm + bwvc, writes=(bpb,))
        vop(lambda v, tt=tt: v.tensor_copy(
                bass.AP(vtm[tt], 0, [[1040, 128], [65, 16], [1, 64]]), PBIG[:, :]),
            reads=(bpb,), writes=(bvtm[tt],))

    _bq2 = buf("q2s")
    for j in range(4, 8):
        wb_ = buf(f"wvc{j}")
        if wb_.w is not None:
            _bq2.r[wb_.w[0]] = max(_bq2.r.get(wb_.w[0], 0), wb_.w[1])
        for pe_, v_ in wb_.r.items():
            _bq2.r[pe_] = max(_bq2.r.get(pe_, 0), v_)

    bqp = [buf("qp0"), buf("qp1")]
    bkp = [buf("kp0"), buf("kp1")]
    bwqk = [buf("bwqk0"), buf("bwqk1")]
    batnT = [buf(f"atnT{i}") for i in range(8)]
    bq2 = buf("q2s")
    bnegm = buf("negm")
    bnrow = buf("nrow")
    bptt = [buf("ptt0"), buf("ptt1")]
    bptt2 = [buf("ptt2_0"), buf("ptt2_1")]
    for _b2 in bptt2:
        wb_ = buf("biasrow")
        if wb_.w is not None:
            _b2.r[wb_.w[0]] = max(_b2.r.get(wb_.w[0], 0), wb_.w[1])
        for pe_, v_ in wb_.r.items():
            _b2.r[pe_] = max(_b2.r.get(pe_, 0), v_)
    brc = buf("rc65")

    pend_norm = [None]

    def flush_norm():
        if pend_norm[0] is not None:
            pend_norm[0]()
            pend_norm[0] = None

    for p in range(NPAIR if KPH >= 3 else 0):
        qp, kp = qpair[p % 2], kpair[p % 2]
        bq_, bk_ = bqp[p % 2], bkp[p % 2]
        for (wdram, stg_t, stg_b) in ((wq, wqk[0], bwqk[0]), (wk, wqk[1], bwqk[1])):
            dma("g", bass.AP(stg_t, 0, [[1024, 128], [128, 8], [1, 128]]),
                wdram[:, 128 * p:128 * (p + 1)].rearrange("(c p) n -> p c n", p=128),
                writes=(stg_b,))
        for (wstg, bwstg, dst, bdst, bcol) in ((wqk[0], bwqk[0], qp, bq_, 0),
                                               (wqk[1], bwqk[1], kp, bk_, 8)):
            for ns in range(4):
                pz = PBIG[:, 512 * (ns % 2):512 * (ns % 2 + 1)]
                pzb = bpb
                parts = []
                for fi in range(8):
                    parts.append((pz, bass.AP(wstg, 128 * fi, [[1024, 128], [1, 128]]),
                                  fm[fi][:, 512 * ns:512 * (ns + 1)], fi == 0, fi == 7))
                mm_raw(parts, reads=[bwstg] + bfm, writes=(pzb,))
                act(dst[:, 512 * ns:512 * (ns + 1)], pz, AF.Identity,
                    bias=bqk_col[:, bcol + p:bcol + p + 1], reads=(pzb, vb), writes=(bdst,))
                if ns == 0 and wstg is wqk[0]:
                    flush_norm()
        for which, (src, bsrc) in enumerate(((qp, bq_), (kp, bk_))):
            for ns in range(4):
                s = 4 * which + ns
                act(q2s8[:, 512 * s:512 * (s + 1)], src[:, 512 * ns:512 * (ns + 1)],
                    AF.Square, reads=(bsrc,), writes=(bq2,))
        for which in range(2):
            for ns in range(4):
                s = 4 * which + ns
                mbank, mb = (PS6, bp6) if s % 2 == 0 else (PY, bpy)
                mm(mbank[0:2, 0:512], hselb[:, :], q2s8[:, 512 * s:512 * (s + 1)],
                   True, True, reads=(cb, bq2), writes=(mb,))
                if ns == 0:
                    vop(lambda v, which=which, mbank=mbank: v.reduce_max(
                            negm[0:2, which:which + 1], mbank[0:2, 0:512], AX.X),
                        reads=(mb,), writes=(bnegm,))
                else:
                    vop(lambda v, which=which, mbank=mbank: v.reduce_max(
                            stat[0:2, 6 + which:7 + which], mbank[0:2, 0:512], AX.X),
                        reads=(mb,), writes=(bstat,))
                    vop(lambda v, which=which: v.tensor_max(negm[0:2, which:which + 1],
                                                            negm[0:2, which:which + 1],
                                                            stat[0:2, 6 + which:7 + which]),
                        reads=(bnegm, bstat), writes=(bnegm,))
        vop(lambda v: v.tensor_mul(negm[0:2, 0:1], negm[0:2, 0:1], negm[0:2, 1:2]),
            reads=(bnegm,), writes=(bnegm,))
        act(negm[0:2, 0:1], negm[0:2, 0:1], AF.Sqrt, scale=1.0 / 64.0,
            reads=(bnegm,), writes=(bnegm,))
        vop(lambda v: v.tensor_scalar(negm[0:2, 0:1], negm[0:2, 0:1], -1.0, None, ALU.mult),
            reads=(bnegm,), writes=(bnegm,))
        trans(PS6[0:1, 24:26], negm[0:2, 0:1], identf[0:2, 0:2], reads=(bnegm, cb), writes=(bp6,))
        vop(lambda v: v.tensor_copy(nrow[:, :], PS6[0:1, 24:26]), reads=(bp6,), writes=(bnrow,))
        mm(PS6[:, 26:28], ones1[0:1, :], nrow[0:1, :], True, True, reads=(cb, bnrow), writes=(bp6,))
        vop(lambda v: v.tensor_copy(negm[:, :], PS6[:, 26:28]), reads=(bp6,), writes=(bnegm,))

        for qq in range(4):
            q0 = 512 * qq

            def pt_slot(kt, head):
                base = ptt if kt % 4 < 2 else ptt2
                sl = 512 * (kt % 2)
                return base[head][:, sl:sl + 512], (bptt if kt % 4 < 2 else bptt2)[head]

            def st_emit(kt):
                par = kt % 2
                pa, ba = pt_slot(kt, 0)
                pb, bb = pt_slot(kt, 1)
                mm_raw([(PZ[par][:, :], kp[0:64, 128 * kt:128 * (kt + 1)],
                         qp[0:64, q0:q0 + 512], True, True),
                        (PX[par][:, :], kp[64:128, 128 * kt:128 * (kt + 1)],
                         qp[64:128, q0:q0 + 512], True, True)],
                       reads=(bq_, bk_), writes=(bpz[par], bpx[par]))
                act(pa, PZ[par][:, :], AF.Exp, bias=negm[:, 0:1], scale=0.125,
                    reads=(bpz[par], bnegm), writes=(ba,))
                act(pb, PX[par][:, :], AF.Exp, bias=negm[:, 1:2], scale=0.125,
                    reads=(bpx[par], bnegm), writes=(bb,))

            if qq % 2 == 1:
                AVB = [(PS6[0:65, :], PS6[64:65, :], PS6[64:128, :], PS6[0:64, :], bp6),
                       (PY[0:65, :], PY[64:65, :], PY[64:128, :], PY[0:64, :], bpy)]
            else:
                AVB = [(PBIG[0:65, 0:512], PBIG[64:65, 0:512], PBIG[64:128, 0:512],
                        PBIG[0:64, 0:512], bpb),
                       (PBIG[0:65, 512:1024], PBIG[64:65, 512:1024],
                        PBIG[64:128, 512:1024], PBIG[0:64, 512:1024], bpb)]

            def av_emit(kt, AVB=AVB):
                pa, ba = pt_slot(kt, 0)
                pb, bb = pt_slot(kt, 1)
                mm_raw([(AVB[0][0], vtm[kt][:, 65 * (2 * p):65 * (2 * p) + 65],
                         pa, kt == 0, kt == 15)],
                       reads=(bvtm[kt], ba), writes=(AVB[0][4],))
                mm_raw([(AVB[1][0], vtm[kt][:, 65 * (2 * p + 1):65 * (2 * p + 1) + 65],
                         pb, kt == 0, kt == 15)],
                       reads=(bvtm[kt], bb), writes=(AVB[1][4],))

            st_emit(0)
            st_emit(1)
            for kt in range(2, 16):
                st_emit(kt)
                av_emit(kt - 2)
                if kt == 7:
                    flush_norm()
            av_emit(14)
            av_emit(15)

            rcsl2 = [rc65[64:65, 512 * ((qq + hh) % 2):512 * ((qq + hh) % 2) + 512]
                     for hh in range(2)]
            for hh, (avfull, avsum, avbc, avdat, avb) in enumerate(AVB):
                vop(lambda v, avsum=avsum, rcsl=rcsl2[hh]: v.reciprocal(rcsl, avsum),
                    reads=(avb,), writes=(brc,))

            def norm_emit(AVB=AVB, qq=qq, p=p, q0=q0, rcsl2=rcsl2):
                rcbc = rc65[0:64, 512 * (qq % 2):512 * (qq % 2) + 512]
                for hh, (avfull, avsum, avbc, avdat, avb) in enumerate(AVB):
                    mm_raw([(avbc, ones65[64:65, 0:64], rcsl2[hh], True, True)],
                           reads=(brc, cb), writes=(avb,))
                    vop(lambda v, avbc=avbc, rcbc=rcbc: v.tensor_copy(rcbc, avbc),
                        reads=(avb,), writes=(brc,))
                    vop(lambda v, hh=hh, p=p, q0=q0, avdat=avdat, rcbc=rcbc: v.tensor_mul(
                            attnT[p][64 * hh:64 * (hh + 1), q0:q0 + 512],
                            avdat, rcbc),
                        reads=(avb, brc), writes=(batnT[p],))

            pend_norm[0] = norm_emit

    flush_norm()
    P.barrier()

    rwbc = sub("rwbc", [128, 1024], F32, R16 + 12 * K)
    brwbc = buf("rwbc")
    for j in range(8 if KPH >= 4 else 0):
        dma("g", woc[j][:, :], wo[128 * j:128 * (j + 1), :], writes=(buf(f"woc{j}"),))
    bwoc = [buf(f"woc{j}") for j in range(8)]
    dma("y", biasrow[0:1, :], bo.rearrange("(b a) -> b a", b=1), writes=(bbias,))
    build_bc(gbc_g, gbc_b, l1g, l1b)
    if KPH >= 4:
        dma("y", rowstage[0:1, 0:1024], rw.rearrange("(b a) 1 -> b a", b=1), writes=(brow,))
        for j in range(0, D, 512):
            mm(PBIG[:, j:j + 512], ones1[0:1, :], rowstage[0:1, j:j + 512], True, True,
               reads=(cb, brow), writes=(bpb,))
        vop(lambda v: v.tensor_copy(rwbc[:, :], PBIG[:, :]), reads=(bpb,), writes=(brwbc,))
    bkey = buf("keyrow")
    pend = []
    for c in range(NCH if KPH >= 4 else 0):
        WOP = [PBIG[:, 0:512], PBIG[:, 512:1024]] if c % 2 == 0 else [PZ[0][:, :], PZ[1][:, :]]
        WOB = (bpb,) if c % 2 == 0 else (bpz[0], bpz[1])
        parts = []
        for j in range(8):
            for nh in range(2):
                parts.append((WOP[nh], attnT[j][:, 128 * c:128 * (c + 1)],
                              woc[j][:, 512 * nh:512 * (nh + 1)], j == 0, False))
        for nh in range(2):
            parts.append((WOP[nh], ones1[0:1, :],
                          biasrow[0:1, 512 * nh:512 * (nh + 1)], False, True))
        mm_raw(parts, reads=[cb, bbias] + batnT + bwoc, writes=WOB)
        for fn in pend:
            fn()
        pend = []
        dma("y", stg[1][:, :], xn_scr[128 * c:128 * (c + 1), :], reads=(bxn_s[c],),
            writes=(bstg[1],))
        if c % 2 == 0:
            vop(lambda v: v.scalar_tensor_tensor(stg[2][:, :], PBIG[:, :], 1.0, stg[1][:, :],
                                                 ALU.mult, ALU.add),
                reads=(bpb, bstg[1]), writes=(bstg[2],))
        else:
            vop(lambda v: v.scalar_tensor_tensor(stg[2][:, 0:512], PZ[0][:, :], 1.0,
                                                 stg[1][:, 0:512], ALU.mult, ALU.add),
                reads=(bpz[0], bstg[1]), writes=(bstg[2],))
            vop(lambda v: v.scalar_tensor_tensor(stg[2][:, 512:1024], PZ[1][:, :], 1.0,
                                                 stg[1][:, 512:1024], ALU.mult, ALU.add),
                reads=(bpz[1], bstg[1]), writes=(bstg[2],))
        ln_stats(stg[2][:, :], bstg[2])
        vop(lambda v: v.tensor_scalar(stg[2][:, :], stg[2][:, :], stat[:, 0:1], stat[:, 1:2],
                                      ALU.subtract, ALU.mult),
            reads=(bstg[2], bstat), writes=(bstg[2],))
        vop(lambda v: v.tensor_mul(stg[2][:, :], stg[2][:, :], gbc_g[:, :]),
            reads=(bstg[2], bgbc), writes=(bstg[2],))
        vop(lambda v: v.tensor_add(stg[2][:, :], stg[2][:, :], gbc_b[:, :]),
            reads=(bstg[2], bgbc), writes=(bstg[2],))
        dma("y", x1_scr[128 * c:128 * (c + 1), :], stg[2][:, :], reads=(bstg[2],),
            writes=(bx1_s[c],))
        vop(lambda v: v.tensor_mul(sq_scr[:, :], stg[2][:, :], rwbc[:, :]),
            reads=(bstg[2], brwbc), writes=(bsq,))
        vop(lambda v: v.reduce_sum(stat[:, 5:6], sq_scr[:, :], AX.X),
            reads=(bsq,), writes=(bstat,))
        act(nbf[0][:, :], stg[2][:, :], AF.Copy, reads=(bstg[2],), writes=(bstg[0],))

        def mk_pend(c):
            def fl():
                trans(PS6[0:1, 32:160], stat[:, 5:6], identf[:, :], reads=(bstat, cb),
                      writes=(bp6,))
                vop(lambda v, c=c: v.tensor_copy(keyrow[0:1, 128 * c:128 * (c + 1)],
                                                 PS6[0:1, 32:160]),
                    reads=(bp6,), writes=(bkey,))
                fm_transposes(nbf[0], bstg[0], c)
            return [fl]

        pend = mk_pend(c)
    for fn in pend:
        fn()

    P.barrier()

    bt5a = buf("t512a")
    bt5b = buf("t512b")
    bmT = buf("masksT")
    bmbc = buf("maskbc")
    bhm = buf("hmsk")
    bhs = [buf(f"hsb{m}") for m in range(16)]
    bw6 = [buf(f"wstp6{i}") for i in range(4)]
    for ns in range(4 if KPH >= 5 else 0):
        bt5c = buf("t512c")
        vop(lambda v, ns=ns: v.tensor_scalar(t512a[0:1, :], keyrow[0:1, 512 * ns:512 * (ns + 1)],
                                             rb_t[0:1, 0:1], 64.0, ALU.add, ALU.add),
            reads=(bkey, vb), writes=(bt5a,))
        for dstep in (64.0, 32.0, 16.0, 8.0, 4.0):
            vop(lambda v, d=dstep: v.tensor_scalar(t512b[0:1, :], t512a[0:1, :], d, -d,
                                                   ALU.is_ge, ALU.mult),
                reads=(bt5a,), writes=(bt5b,))
            vop(lambda v: v.tensor_add(t512a[0:1, :], t512a[0:1, :], t512b[0:1, :]),
                reads=(bt5a, bt5b), writes=(bt5a,))
        vop(lambda v: v.tensor_scalar(t512b[0:1, :], t512a[0:1, :], 1.0, None, ALU.is_ge),
            reads=(bt5a,), writes=(bt5b,))
        vop(lambda v: v.tensor_scalar(t512c[0:1, :], t512a[0:1, :], 2.0, None, ALU.is_ge),
            reads=(bt5a,), writes=(bt5c,))
        vop(lambda v: v.tensor_add(t512b[0:1, :], t512b[0:1, :], t512c[0:1, :]),
            reads=(bt5b, bt5c), writes=(bt5b,))
        vop(lambda v: v.tensor_scalar(t512c[0:1, :], t512a[0:1, :], 3.0, None, ALU.is_ge),
            reads=(bt5a,), writes=(bt5c,))
        vop(lambda v, ns=ns: v.tensor_add(keyrow[0:1, 512 * ns:512 * (ns + 1)], t512b[0:1, :],
                                          t512c[0:1, :]),
            reads=(bt5b, bt5c), writes=(bkey,))
    for ns in range(4 if KPH >= 5 else 0):
        mm(PZ[1][0:4, :], ones1[0:1, 0:4], keyrow[0:1, 512 * ns:512 * (ns + 1)], True, True,
           reads=(cb, bkey), writes=(bpz[1],))
        vop(lambda v, ns=ns: v.tensor_scalar(masksT[:, 512 * ns:512 * (ns + 1)], PZ[1][0:4, :],
                                             cvec4[:, :], None, ALU.is_equal),
            reads=(bpz[1], cb), writes=(bmT,))

    for ex in range(NEXP if KPH >= 5 else 0):
        for ns in range(4):
            vop(lambda v, ns=ns, ex=ex: v.tensor_scalar(
                    t512a[0:1, :], keyrow[0:1, 512 * ns:512 * (ns + 1)], float(ex), None,
                    ALU.is_equal),
                reads=(bkey,), writes=(bt5a,))
            mm(PZ[1][:, :], ones1[0:1, :], t512a[0:1, :], True, True,
               reads=(cb, bt5a), writes=(bpz[1],))
            vop(lambda v, ns=ns: v.tensor_copy(maskbc[:, 512 * ns:512 * (ns + 1)], PZ[1][:, :]),
                reads=(bpz[1],), writes=(bmbc,))
        for half in range(2):
            for mj in range(16):
                wr, wbr = wstp6[mj % 4], bw6[mj % 4]
                ffc = 2048 * half + 128 * mj
                dma("g", bass.AP(wr, 0, [[1024, 128], [128, 8], [1, 128]]),
                    w1[ex, :, ffc:ffc + 128].rearrange("(c p) n -> p c n", p=128), writes=(wbr,))
                if mj % 2 == 0:
                    pset = [PBIG[:, 0:512], PBIG[:, 512:1024], PZ[0][:, :], PZ[1][:, :]]
                    pbufs = [bpb, bpb, bpz[0], bpz[1]]
                else:
                    pset = [PX[0][:, :], PX[1][:, :], PS6[:, :], PY[:, :]]
                    pbufs = [bpx[0], bpx[1], bp6, bpy]
                parts = []
                for fi in range(8):
                    for ns in range(4):
                        parts.append((pset[ns], bass.AP(wr, 128 * fi, [[1024, 128], [1, 128]]),
                                      fm[fi][:, 512 * ns:512 * (ns + 1)], fi == 0, fi == 7))
                mm_raw(parts, reads=[wbr] + bfm, writes=tuple(set(pbufs)))
                bcol = b1_col[:, 32 * ex + 16 * half + mj:32 * ex + 16 * half + mj + 1]
                for ns in range(4):
                    act(hmsk[:, 512 * ns:512 * (ns + 1)], pset[ns], AF.Relu, bias=bcol,
                        reads=(pbufs[ns], vb), writes=(bhm,))
                vop(lambda v, mj=mj: v.tensor_mul(hsb[mj][:, :], hmsk[:, :], maskbc[:, :]),
                    reads=(bhm, bmbc), writes=(bhs[mj],))
                dma("g", rdb[mj][:, :],
                    w2[ex, 2048 * half + 128 * mj:2048 * half + 128 * (mj + 1), :],
                    writes=(brdb[mj],))
            for tt in range(16):
                PP = PX if tt % 2 == 0 else PZ
                PB_ = bpx if tt % 2 == 0 else bpz
                parts = []
                for mj in range(16):
                    for s2 in range(2):
                        last = (mj == 15) and not (ex == 0 and half == 0)
                        parts.append((PP[s2][:, :], hsb[mj][:, 128 * tt:128 * (tt + 1)],
                                      rdb[mj][:, 512 * s2:512 * (s2 + 1)], mj == 0, last))
                if ex == 0 and half == 0:
                    for s2 in range(2):
                        parts.append((PP[s2][:, :], masksT[:, 128 * tt:128 * (tt + 1)],
                                      b2_sb[:, 512 * s2:512 * (s2 + 1)], False, True))
                mm_raw(parts, reads=[vb, bmT] + bhs + brdb, writes=(PB_[0], PB_[1]))
                sst, bsst = stg[tt % 2], bstg[tt % 2]
                vop(lambda v, sst=sst, PP=PP: v.tensor_copy(sst[:, 0:512], PP[0][:, :]),
                    reads=(PB_[0],), writes=(bsst,))
                vop(lambda v, sst=sst, PP=PP: v.tensor_copy(sst[:, 512:1024], PP[1][:, :]),
                    reads=(PB_[1],), writes=(bsst,))
                dma("g", x1_scr[128 * tt:128 * (tt + 1), :], sst[:, :], reads=(bsst,),
                    writes=(bx1_s[tt],), accum_op=ALU.add)

    P.barrier()

    build_bc(gbc_g, gbc_b, l2g, l2b)
    for c in range(NCH if KPH >= 6 else 0):
        dma("y", stg[0][:, :], x1_scr[128 * c:128 * (c + 1), :], reads=(bx1_s[c],),
            writes=(bstg[0],))
        dma("y", stg[1][:, :], part_scr[128 * c:128 * (c + 1), :], reads=(bpart_s[c],),
            writes=(bstg[1],))
        ln_stats(stg[0][:, :], bstg[0])
        vop(lambda v: v.tensor_scalar(stg[2][:, :], stg[0][:, :], stat[:, 0:1], stat[:, 1:2],
                                      ALU.subtract, ALU.mult),
            reads=(bstg[0], bstat), writes=(bstg[2],))
        vop(lambda v: v.tensor_mul(stg[2][:, :], stg[2][:, :], gbc_g[:, :]),
            reads=(bstg[2], bgbc), writes=(bstg[2],))
        vop(lambda v: v.tensor_add(stg[2][:, :], stg[2][:, :], gbc_b[:, :]),
            reads=(bstg[2], bgbc), writes=(bstg[2],))
        vop(lambda v: v.tensor_add(stg[2][:, :], stg[2][:, :], stg[1][:, :]),
            reads=(bstg[2], bstg[1]), writes=(bstg[2],))
        dma("y", out_hbm[128 * c:128 * (c + 1), :], stg[2][:, :], reads=(bstg[2],))

    for q in DMAQ:
        for i in range(NLANE):
            lane = f"{q}{i}"
            P._wait(q, lane, P.cnt[lane])

    global _LASTP
    _LASTP = P
    with nc.Block() as block:
        block.tensor(lambda eng: _replay(P, "t", eng, sems))
        block.vector(lambda eng: _replay(P, "v", eng, sems))
        block.scalar(lambda eng: _replay(P, "s", eng, sems))
        block.gpsimd(lambda eng: _replay(P, "g", eng, sems))
        block.sync(lambda eng: _replay(P, "y", eng, sems))

    ctx.close()
    return nc


def kernel(**inputs):
    nc = build_program()
    consts = host_consts()
    x = np.ascontiguousarray(np.asarray(inputs["L0_attn_input_tensor"], dtype=np.float32))
    names = {
        "w_in": "L0_attn_W_in", "b_in": "L0_attn_b_in", "a_log": "L0_attn_A_log",
        "d_skip": "L0_attn_D_skip", "dt_bias": "L0_attn_dt_bias", "w_out": "L0_attn_W_out",
        "b_out": "L0_attn_b_out", "ln_g": "L0_attn_ln_gamma", "ln_b": "L0_attn_ln_beta",
        "wq": "L0_ffn_Wq", "bq": "L0_ffn_bq", "wk": "L0_ffn_Wk", "bk": "L0_ffn_bk",
        "wv": "L0_ffn_Wv", "bv": "L0_ffn_bv", "wo": "L0_ffn_Wo", "bo": "L0_ffn_bo",
        "rw": "L0_ffn_router_w", "rb": "L0_ffn_router_b", "w1": "L0_ffn_W1", "b1": "L0_ffn_b1",
        "w2": "L0_ffn_W2", "b2": "L0_ffn_b2", "l1g": "L0_ffn_ln1_g", "l1b": "L0_ffn_ln1_b",
        "l2g": "L0_ffn_ln2_g", "l2b": "L0_ffn_ln2_b",
    }
    base = {k: np.ascontiguousarray(np.asarray(inputs[v], np.float32)) for k, v in names.items()}
    base.update(consts)
    in_maps = []
    for c in range(8):
        m = dict(base)
        m["x_in"] = np.ascontiguousarray(x[c])
        in_maps.append(m)
    res = run_bass_kernel_spmd(nc, in_maps, core_ids=list(range(8)))
    return np.stack([res.results[c]["out"] for c in range(8)], axis=0)


if __name__ == "__main__":
    print("building program...")
    nc = build_program()
    print("built ok")


# version 18 (final)
# speedup vs baseline: 1.0548x; 1.0063x over previous
import os
import numpy as np
import ml_dtypes
import concourse.bass as bass
import concourse.mybir as mybir
from concourse.bass_utils import run_bass_kernel_spmd
from contextlib import ExitStack

F32 = mybir.dt.float32
BF16 = mybir.dt.bfloat16
AF = mybir.ActivationFunctionType
ALU = mybir.AluOpType
AX = mybir.AxisListType

T = 2048
D = 1024
NCH = 16
SCLEN = 512
NSC = 4
CPS = 4
INNER = 2048
NHM = 8
PD = 256
STATE = 64
PROJ = 4232
NHA = 16
NPAIR = 8
DFF = 4096
NEXP = 4
EPS = 1e-5
NEG = -1e30

ENGS = ("t", "v", "s", "g", "y")
DMAQ = ("g", "y")
NLANE = 8
SEMS = list(ENGS) + [f"{q}{i}" for q in DMAQ for i in range(NLANE)] + [f"{e}B" for e in ENGS]


class Buf:
    __slots__ = ("w", "r")

    def __init__(self):
        self.w = None
        self.r = {}


class Prog:
    def __init__(self):
        self.cnt = dict.fromkeys(SEMS, 0)
        self.wm = {c: dict.fromkeys(SEMS, 0) for c in ENGS}
        self.items = {e: [] for e in ENGS}
        self.lane_rr = dict.fromkeys(DMAQ, 0)

    def _wait(self, e, pe, v):
        if v <= 0 or v <= self.wm[e][pe]:
            return
        self.wm[e][pe] = v
        self.items[e].append(("w", pe, v))

    def wait_tok(self, e, tok):
        if tok is not None:
            self._wait(e, tok[0], tok[1])

    def do(self, e, fns, reads=(), writes=(), inc=None):
        if callable(fns):
            fns = (fns,)
        for b in reads:
            self.wait_tok(e, b.w)
        for b in writes:
            self.wait_tok(e, b.w)
            for pe, v in b.r.items():
                self._wait(e, pe, v)
        if e in DMAQ:
            lane = f"{e}{self.lane_rr[e]}"
            self.lane_rr[e] = (self.lane_rr[e] + 1) % NLANE
            self._wait(e, lane, self.cnt[lane])
            sem_key = lane
            inc_amt = 16
        else:
            sem_key = e
            inc_amt = 1 if inc is None else inc
        for fn in fns:
            self.items[e].append(("o", fn))
        self.cnt[sem_key] += inc_amt
        self.items[e].append(("i", sem_key, inc_amt))
        tok = (sem_key, self.cnt[sem_key])
        for b in reads:
            b.r[sem_key] = max(b.r.get(sem_key, 0), tok[1])
        for b in writes:
            b.w = tok
            b.r = {}
        return tok

    def barrier(self):
        toks = {}
        for e in ENGS:
            if e in DMAQ:
                for i in range(NLANE):
                    lane = f"{e}{i}"
                    self._wait(e, lane, self.cnt[lane])
            else:
                self._wait(e, e, self.cnt[e])
            b = f"{e}B"
            self.cnt[b] += 1
            self.items[e].append(("b", b, 1))
            toks[b] = self.cnt[b]
        for e in ENGS:
            for pe in ENGS:
                self._wait(e, f"{pe}B", toks[f"{pe}B"])


def _replay(P, e, eng, sems):
    last = None
    for it in P.items[e]:
        k = it[0]
        if k == "o":
            last = it[1](eng)
        elif k == "w":
            eng.wait_ge(sems[it[1]], it[2])
        elif k == "i":
            last.then_inc(sems[it[1]], it[2])
        elif k == "b":
            eng.nop().then_inc(sems[it[1]], it[2])


def host_consts():
    c = {}
    c["c_identf"] = np.eye(128, dtype=np.float32)
    c["c_identb"] = np.eye(128, dtype=np.float32).astype(ml_dtypes.bfloat16)
    c["c_ones1"] = np.ones((1, 128), dtype=np.float32)
    c["c_ones8"] = np.ones((8, 128), dtype=np.float32)
    hs = np.zeros((128, 2), dtype=np.float32)
    hs[0:64, 0] = 1.0
    hs[64:128, 1] = 1.0
    c["c_hsel"] = hs
    tri = np.where(np.arange(128)[:, None] <= np.arange(128)[None, :], 0.0, NEG).astype(np.float32)
    c["c_maskb8"] = np.tile(tri, (1, 8))
    mb = np.zeros((8, 1024), dtype=np.float32)
    for h in range(8):
        mb[h, 128 * h:128 * (h + 1)] = 1.0
    c["c_mask8"] = mb
    c["c_cvec4"] = np.arange(4, dtype=np.float32).reshape(4, 1)
    c["c_ones65"] = np.ones((65, 128), dtype=np.float32)
    c["c_imm"] = np.array([[EPS], [1.0]], dtype=np.float32) * np.ones((1, 1), np.float32)
    c["c_imm"] = np.concatenate([np.full((128, 1), EPS, np.float32),
                                 np.full((128, 1), 1.0, np.float32)], axis=1)
    return c


KPH = int(os.environ.get("KPH", "99"))
_LASTP = None


def build_program():
    nc = bass.Bass("TRN2", target_bir_lowering=False)

    def dram(name, shape, dtype=F32, kind="ExternalInput"):
        return nc.dram_tensor(name, list(shape), dtype, kind=kind).ap()

    x_in = dram("x_in", (T, D))
    w_in = dram("w_in", (D, PROJ))
    b_in = dram("b_in", (PROJ,))
    a_log = dram("a_log", (NHM,))
    d_skip = dram("d_skip", (NHM,))
    dt_bias = dram("dt_bias", (NHM,))
    w_out_d = dram("w_out", (INNER, D))
    b_out = dram("b_out", (D,))
    ln_g = dram("ln_g", (D,))
    ln_b = dram("ln_b", (D,))
    wq = dram("wq", (D, D))
    bq = dram("bq", (D,))
    wk = dram("wk", (D, D))
    bk = dram("bk", (D,))
    wv = dram("wv", (D, D))
    bv = dram("bv", (D,))
    wo = dram("wo", (D, D))
    bo = dram("bo", (D,))
    rw = dram("rw", (D, 1))
    rb = dram("rb", (1,))
    w1 = dram("w1", (NEXP, D, DFF))
    b1 = dram("b1", (NEXP, DFF))
    w2 = dram("w2", (NEXP, DFF, D))
    b2 = dram("b2", (NEXP, D))
    l1g = dram("l1g", (D,))
    l1b = dram("l1b", (D,))
    l2g = dram("l2g", (D,))
    l2b = dram("l2b", (D,))
    c_identf = dram("c_identf", (128, 128))
    c_identb = dram("c_identb", (128, 128), BF16)
    c_ones1 = dram("c_ones1", (1, 128))
    c_ones8 = dram("c_ones8", (8, 128))
    c_hsel = dram("c_hsel", (128, 2))
    c_maskb8 = dram("c_maskb8", (128, 1024))
    c_mask8 = dram("c_mask8", (8, 1024))
    c_cvec4 = dram("c_cvec4", (4, 1))
    c_ones65 = dram("c_ones65", (65, 128))
    c_imm = dram("c_imm", (128, 2))
    xn_scr = dram("xn_scr", (T, D), kind="Internal")
    part_scr = dram("part_scr", (T, D), kind="Internal")
    x1_scr = dram("x1_scr", (T, D), kind="Internal")
    out_hbm = dram("out", (T, D), kind="ExternalOutput")

    ARENA = 196 * 1024
    arena = nc.alloc_sbuf_tensor("arena", [128, ARENA], mybir.dt.uint8)
    arena_addr = nc.lookup_mloc(arena).addr

    def sub(name, shape, dtype, off):
        per_part = int(np.prod(shape[1:])) * mybir.dt.size(dtype)
        assert off + per_part <= ARENA, f"arena overflow: {name}"
        hnd = nc.virtual_sbuf_tensor(name, list(shape), dtype)
        nc._tensor(hnd.name, hnd.shape, hnd.dtype, type="SB")
        mloc = nc.lookup_mloc(hnd)
        mloc.allocated = True
        mloc.addr = arena_addr + off
        return hnd

    K = 1024
    CUR = [0]
    def nxt(name, shape, dtype):
        per = int(np.prod(shape[1:])) * mybir.dt.size(dtype)
        off = (CUR[0] + 31) & ~31
        CUR[0] = off + per
        return sub(name, shape, dtype, off)
    identf = nxt("identf", [128, 128], F32)
    identb = nxt("identb", [128, 128], BF16)
    ones1 = nxt("ones1", [1, 128], F32)
    ones8 = nxt("ones8", [8, 128], F32)
    hsel = nxt("hsel", [128, 2], F32)
    hselb = nxt("hselb", [128, 2], BF16)
    identD = nxt("identD", [128, 8 * 128], BF16)
    cvec4 = nxt("cvec4", [4, 1], F32)
    immt = nxt("immt", [128, 2], F32)
    ones65 = nxt("ones65", [65, 128], F32)
    maskb8 = nxt("maskb8", [128, 1024], F32)
    mask8 = nxt("mask8", [8, 1024], F32)
    b_in_col = nxt("b_in_col", [128, 33], F32)
    comb8 = nxt("comb8", [8, 1], F32)
    acol = nxt("acol", [8, 1], F32)
    tmp8a = nxt("tmp8a", [8, 1], F32)
    drow = nxt("drow", [1, 8], F32)
    dbc = nxt("dbc", [128, 8], F32)
    bqk_col = nxt("bqk_col", [128, 16], F32)
    rw_col = nxt("rw_col", [128, 8], BF16)
    rb_t = nxt("rb_t", [1, 1], F32)
    b1_col = nxt("b1_col", [128, 128], F32)
    lprevc = nxt("lprevc", [8, 1], F32)
    lprevbc = nxt("lprevbc", [128, 8], F32)
    decbc = nxt("decbc", [128, 8], F32)
    wtm = nxt("wtm", [128, 8], F32)
    lendrow = nxt("lendrow", [1, 8], F32)
    lcarry = nxt("lcarry", [8, 1], F32)
    stat = nxt("stat", [128, 16], F32)
    negm = nxt("negm", [128, 2], F32)
    nrow = nxt("nrow", [1, 2], F32)
    ltm = nxt("ltm", [128, 16], F32)
    nltm = nxt("nltm", [128, 8], F32)
    btm = nxt("btm", [128, 64], BF16)
    b2_sb = nxt("b2_sb", [4, D], BF16)
    assert CUR[0] <= 16 * K, CUR[0]
    RI = 16 * K
    rowstage = sub("rowstage", [1, 2048], F32, RI)
    gbc_g = sub("gbc_g", [128, 1024], F32, RI + 8 * K)
    gbc_b = sub("gbc_b", [128, 1024], F32, RI + 12 * K)
    gbc = sub("gbc", [128, 2048], F32, RI + 8 * K)
    wvc = [sub(f"wvc{i}", [128, 1024], BF16, RI + i * 2048) for i in range(8)]
    masksT = sub("masksT", [4, 2048], BF16, RI)
    maskbc = sub("maskbc", [128, 2048], BF16, RI + 4 * K)
    hmsk = sub("hmsk", [128, 2048], BF16, RI + 8 * K)
    RH = 32 * K
    stg = [sub(f"stg{i}", [128, 1024], F32, RH + i * 4096) for i in range(4)]
    nbf = [sub(f"nbf{i}", [128, 1024], BF16, RH + i * 4096) for i in range(2)]
    wstp6 = [sub(f"wstp6{i}", [128, 1024], BF16, RH + 8 * K + i * 2048) for i in range(4)]
    R64 = 48 * K
    szall = sub("szall", [128, 8192], BF16, R64)
    o3 = R64 + 16 * K
    def nx(name, shape, dtype):
        nonlocal o3
        per = int(np.prod(shape[1:])) * mybir.dt.size(dtype)
        off = (o3 + 31) & ~31
        o3 = off + per
        return sub(name, shape, dtype, off)
    bT_b = nx("bT_b", [64, 512], BF16)
    cT_b = nx("cT_b", [64, 512], BF16)
    dts = nx("dts", [8, 512], F32)
    lts = nx("lts", [8, 512], F32)
    ldec = nx("ldec", [8, 512], F32)
    mpre = nx("mpre", [128, 1024], F32)
    mexp = nx("mexp", [128, 1024], BF16)
    gst = nx("gst", [128, 1024], BF16)
    rmsk = nx("rmsk", [8, 1024], F32)
    cpb = nx("cpb", [64, 1024], BF16)
    bph = nx("bph", [128, 8 * 64], BF16)
    hst = nx("hst", [64, 8 * 256], F32)
    hstb = nx("hstb", [64, 8 * 256], BF16)
    xs_bf = nx("xs_bf", [128, 2048], BF16)
    dtx = nx("dtx", [128, 2048], BF16)
    yt = nx("yt", [128, 2048], BF16)
    rmat = nx("rmat", [8, 128], F32)
    btbc_sb = nx("btbc_sb", [128, 128], F32)
    assert o3 <= R64 + 65 * K, o3 - R64
    vtm = [sub(f"vtm{i}", [128, 1040], BF16, R64 + i * 2080) for i in range(16)]
    qpair = [sub(f"qpair{i}", [128, 2048], BF16, R64 + 34 * K + i * 4096) for i in range(2)]
    kpair = [sub(f"kpair{i}", [128, 2048], BF16, R64 + 42 * K + i * 4096) for i in range(2)]
    woc = [sub(f"woc{i}", [128, 1024], BF16, R64 + 50 * K + i * 2048) for i in range(7)] \
        + [sub("woc7", [128, 1024], BF16, R64 + 34 * K)]
    hsb = [sub(f"hsb{m}", [128, 2048], BF16, R64 + m * 4096) for m in range(16)]
    R32A = 113 * K
    fm = [sub(f"fm{i}", [128, 2048], BF16, R32A + i * 4096) for i in range(8)]
    R32B = 145 * K
    rdb = [sub(f"rdb{i}", [128, 1024], BF16, R32B + i * 2048) for i in range(16)]
    attnT = [sub(f"attnT{i}", [128, 2048], BF16, R32B + i * 4096) for i in range(8)]
    R16 = 177 * K
    xsfm = [sub(f"xsfm{i}", [128, 512], BF16, R16 + i * 1024) for i in range(16)]
    wqk = [sub(f"wqk{i}", [128, 1024], BF16, R16 + i * 2048) for i in range(2)]
    ptt = [sub(f"ptt{i}", [128, 1024], BF16, R16 + 4 * K + i * 2048) for i in range(2)]
    ptt2 = [sub(f"ptt2_{i}", [128, 1024], BF16, R16 + 8 * K + i * 2048) for i in range(2)]
    biasrow = sub("biasrow", [1, 1024], F32, R16 + 8 * K)
    rc65 = sub("rc65", [65, 1024], F32, R16 + 12 * K)
    q2s8 = sub("q2s", [128, 4096], BF16, RI + 8 * K)
    keyrow = sub("keyrow", [1, 2048], F32, R16)
    t512a = sub("t512a", [1, 512], F32, R16 + 8 * K)
    t512b = sub("t512b", [1, 512], F32, R16 + 10 * K)
    t512i = sub("t512i", [1, 512], mybir.dt.int32, R16 + 12 * K)
    t512c = sub("t512c", [1, 512], F32, R16 + 14 * K)

    ctx = ExitStack()
    e = ctx.enter_context
    PZ = [e(nc.psum_tensor(f"PZ{i}", [128, 512], F32)) for i in range(2)]
    PX = [e(nc.psum_tensor(f"PX{i}", [128, 512], F32)) for i in range(2)]
    PBIG = e(nc.psum_tensor("PBIG", [128, 1024], F32))
    PS6 = e(nc.psum_tensor("PS6", [128, 512], F32))
    PY = e(nc.psum_tensor("PY", [128, 512], F32))

    def sub_ps(name, shape, dtype, base_handle):
        hnd = nc.virtual_psum_tensor(name, list(shape), dtype)
        nc._tensor(hnd.name, hnd.shape, hnd.dtype, type="PSUM")
        mloc = nc.lookup_mloc(hnd)
        bm = nc.lookup_mloc(base_handle)
        mloc.allocated = True
        mloc.addr = bm.addr
        mloc.bank = bm.bank
        mloc.base = bm.base
        return hnd


    sems = {k: e(nc.semaphore(f"sem_{k}")) for k in SEMS}
    P = Prog()
    _B = {}

    def buf(name):
        if name not in _B:
            _B[name] = Buf()
        return _B[name]

    bpz = [buf("pz0"), buf("pz1")]
    bpx = [buf("px0"), buf("px1")]
    bpb = buf("pbig")
    bp6 = buf("ps6")
    bpy = buf("py")

    bxn_s = [buf(f"xns{c}") for c in range(NCH)]
    bpart_s = [buf(f"parts{c}") for c in range(NCH)]
    bx1_s = [buf(f"x1s{c}") for c in range(NCH)]

    def dma(qe, out_ap, in_ap, reads=(), writes=(), **kw):
        return P.do(qe, lambda g, o=out_ap, i=in_ap, kw=dict(kw): g.dma_start(out=o, in_=i, **kw),
                    reads=reads, writes=writes)

    def mm(out_ap, lhsT_ap, rhs_ap, start, stop, reads=(), writes=()):
        return P.do("t", lambda t, o=out_ap, l=lhsT_ap, r=rhs_ap, a=start, b=stop:
                    t.matmul(o, l, r, start=a, stop=b),
                    reads=reads, writes=writes)

    def mm_raw(parts, reads=(), writes=()):
        fns = [(lambda t, o=o, l=l, r=r, s=s, st=st: t.matmul(o, l, r, start=s, stop=st))
               for (o, l, r, s, st) in parts]
        return P.do("t", fns, reads=reads, writes=writes)

    def act(out_ap, in_ap, func, bias=0.0, scale=1.0, accum=None, reads=(), writes=()):
        return P.do("s", lambda s, o=out_ap, i=in_ap, f=func, b=bias, sc=scale, a=accum:
                    s.activation(o, i, f, bias=b, scale=sc, accum_out=a),
                    reads=reads, writes=writes)

    def vop(fn, reads=(), writes=()):
        return P.do("v", fn, reads=reads, writes=writes)

    def trans(out_ap, in_ap, ident_ap, reads=(), writes=()):
        return P.do("t", lambda t, o=out_ap, i=in_ap, d=ident_ap:
                    t.matmul(o, i, d, start=True, stop=True),
                    reads=reads, writes=writes)

    cb = buf("consts")
    vb = buf("vecs")
    bstg = [buf(f"stg{i}") for i in range(4)]
    bfm = [buf(f"fm{i}") for i in range(8)]
    bgbc = buf("gbc")
    brow = buf("rowstage")
    bstat = buf("stat")

    dma("y", identf[:, :], c_identf, writes=(cb,))
    dma("y", identb[:, :], c_identb, writes=(cb,))
    dma("y", ones1[:, :], c_ones1, writes=(cb,))
    dma("y", ones8[:, :], c_ones8, writes=(cb,))
    dma("y", hsel[:, :], c_hsel, writes=(cb,))
    dma("g", hselb[:, :], c_hsel, writes=(cb,))
    dma("y", maskb8[:, :], c_maskb8, writes=(cb,))
    dma("y", mask8[:, :], c_mask8, writes=(cb,))
    dma("y", cvec4[:, :], c_cvec4, writes=(cb,))
    dma("y", ones65[:, :], c_ones65, writes=(cb,))
    dma("y", immt[:, :], c_imm, writes=(cb,))
    nc.const_aps.register(F32, EPS, immt[:, 0:1])
    nc.const_aps.register(F32, 1.0, immt[:, 1:2])
    dma("y", b_in_col[:, :], b_in[0:4224].rearrange("(m p) -> p m", p=128), writes=(vb,),
        allow_slow_non_contiguous=True)
    dma("y", tmp8a[:, :], b_in[4224:4232].rearrange("(a b) -> a b", b=1), writes=(vb,),
        allow_slow_non_contiguous=True)
    dma("y", comb8[:, :], dt_bias.rearrange("(a b) -> a b", b=1), writes=(vb,),
        allow_slow_non_contiguous=True)
    dma("y", acol[:, :], a_log.rearrange("(a b) -> a b", b=1), writes=(vb,),
        allow_slow_non_contiguous=True)
    dma("y", drow[:, :], d_skip.rearrange("(b a) -> b a", b=1), writes=(vb,))
    dma("y", bqk_col[:, 0:8], bq.rearrange("(m p) -> p m", p=128), writes=(vb,),
        allow_slow_non_contiguous=True)
    dma("y", bqk_col[:, 8:16], bk.rearrange("(m p) -> p m", p=128), writes=(vb,),
        allow_slow_non_contiguous=True)
    dma("y", rb_t[:, :], rb.rearrange("(a b) -> a b", b=1), writes=(vb,))
    dma("y", b1_col[:, :], b1.rearrange("e (m p) -> p (e m)", p=128), writes=(vb,),
        allow_slow_non_contiguous=True)
    dma("g", b2_sb[:, :], b2[:, :], writes=(vb,))
    vop(lambda v: v.tensor_add(comb8[:, :], comb8[:, :], tmp8a[:, :]), reads=(vb,), writes=(vb,))
    act(acol[:, :], acol[:, :], AF.Exp, reads=(vb,), writes=(vb,))
    vop(lambda v: v.tensor_scalar(acol[:, :], acol[:, :], -1.0, None, ALU.mult),
        reads=(vb,), writes=(vb,))
    mm(PS6[:, 0:8], ones1[0:1, :], drow[0:1, :], True, True, reads=(cb, vb), writes=(bp6,))
    vop(lambda v: v.tensor_copy(dbc[:, :], PS6[:, 0:8]), reads=(bp6,), writes=(vb,))
    for h in range(NHM):
        vop(lambda v, h=h: v.tensor_scalar(identD[:, 128 * h:128 * (h + 1)], identb[:, :],
                                           dbc[:, h:h + 1], None, ALU.mult),
            reads=(cb, vb), writes=(cb,))

    def build_bc(dst_g, dst_b, src_g, src_b_):
        for dst, src in ((dst_g, src_g), (dst_b, src_b_)):
            dma("y", rowstage[0:1, 0:D], src.rearrange("(b a) -> b a", b=1), writes=(brow,))
            for j in range(0, D, 512):
                mm(PBIG[:, j:j + 512], ones1[0:1, :], rowstage[0:1, j:j + 512], True, True,
                   reads=(cb, brow), writes=(bpb,))
            vop(lambda v, dst=dst: v.tensor_copy(dst[:, :], PBIG[:, :]),
                reads=(bpb,), writes=(bgbc,))

    sq_scr = stg[3]
    bsq = bstg[3]

    def ln_stats(src_ap, src_buf, so=0, neg_mu=False):
        act(sq_scr[:, :], src_ap, AF.Square, accum=stat[:, so + 2:so + 3], reads=(src_buf,),
            writes=(bsq, bstat))
        vop(lambda v, so=so: v.reduce_sum(stat[:, so + 3:so + 4], src_ap, AX.X),
            reads=(src_buf,), writes=(bstat,))
        vop(lambda v, so=so, sc=(-1.0 / D if neg_mu else 1.0 / D): v.tensor_scalar(
                stat[:, so:so + 1], stat[:, so + 3:so + 4], sc, None, ALU.mult),
            reads=(bstat,), writes=(bstat,))
        vop(lambda v, so=so: v.tensor_scalar(stat[:, so + 2:so + 3], stat[:, so + 2:so + 3],
                                             1.0 / D, None, ALU.mult),
            reads=(bstat,), writes=(bstat,))
        vop(lambda v, so=so: v.tensor_mul(stat[:, so + 4:so + 5], stat[:, so:so + 1],
                                          stat[:, so:so + 1]),
            reads=(bstat,), writes=(bstat,))
        vop(lambda v, so=so: v.tensor_sub(stat[:, so + 4:so + 5], stat[:, so + 2:so + 3],
                                          stat[:, so + 4:so + 5]),
            reads=(bstat,), writes=(bstat,))
        act(stat[:, so + 4:so + 5], stat[:, so + 4:so + 5], AF.Sqrt, bias=EPS,
            reads=(bstat,), writes=(bstat,))
        vop(lambda v, so=so: v.reciprocal(stat[:, so + 1:so + 2], stat[:, so + 4:so + 5]),
            reads=(bstat,), writes=(bstat,))

    def fm_transposes(src_bf_ap, src_buf, c):
        for j in range(8):
            pt = PS6[:, 0:128] if j % 2 == 0 else PY[:, 0:128]
            tb = bp6 if j % 2 == 0 else bpy
            trans(pt, bass.AP(src_bf_ap, 128 * j, [[1024, 128], [1, 128]]),
                  identb[:, :], reads=(src_buf, cb), writes=(tb,))
            if j % 2 == 0:
                vop(lambda v, pt=pt, j=j, c=c: v.tensor_copy(
                        fm[j][:, 128 * c:128 * (c + 1)], pt),
                    reads=(tb,), writes=(bfm[j],))
            else:
                P.do("s", lambda s, pt=pt, j=j, c=c: s.copy(
                        fm[j][:, 128 * c:128 * (c + 1)], pt),
                     reads=(tb,), writes=(bfm[j],))

    build_bc(gbc_g, gbc_b, ln_g, ln_b)
    sn2 = sub("sn2", [128, 1024], F32, R64 + 20 * K)
    bsn2 = buf("sn2")

    def ln_stats_pair(AB):
        for fn in (lambda src, sb, so: act(sq_scr[:, :], src, AF.Square,
                                           accum=stat[:, so + 2:so + 3],
                                           reads=(sb,), writes=(bsq, bstat)),
                   lambda src, sb, so: vop(lambda v, src=src, so=so: v.reduce_sum(
                                               stat[:, so + 3:so + 4], src, AX.X),
                                           reads=(sb,), writes=(bstat,)),
                   lambda src, sb, so: vop(lambda v, so=so: v.tensor_scalar(
                                               stat[:, so:so + 1], stat[:, so + 3:so + 4],
                                               -1.0 / D, None, ALU.mult),
                                           reads=(bstat,), writes=(bstat,)),
                   lambda src, sb, so: vop(lambda v, so=so: v.tensor_scalar(
                                               stat[:, so + 2:so + 3], stat[:, so + 2:so + 3],
                                               1.0 / D, None, ALU.mult),
                                           reads=(bstat,), writes=(bstat,)),
                   lambda src, sb, so: vop(lambda v, so=so: v.tensor_mul(
                                               stat[:, so + 4:so + 5], stat[:, so:so + 1],
                                               stat[:, so:so + 1]),
                                           reads=(bstat,), writes=(bstat,)),
                   lambda src, sb, so: vop(lambda v, so=so: v.tensor_sub(
                                               stat[:, so + 4:so + 5], stat[:, so + 2:so + 3],
                                               stat[:, so + 4:so + 5]),
                                           reads=(bstat,), writes=(bstat,)),
                   lambda src, sb, so: act(stat[:, so + 4:so + 5], stat[:, so + 4:so + 5],
                                           AF.Sqrt, bias=EPS, reads=(bstat,), writes=(bstat,)),
                   lambda src, sb, so: vop(lambda v, so=so: v.reciprocal(
                                               stat[:, so + 1:so + 2], stat[:, so + 4:so + 5]),
                                           reads=(bstat,), writes=(bstat,)),
                   lambda src, sb, so: vop(lambda v, so=so: v.tensor_mul(
                                               stat[:, so + 5:so + 6], stat[:, so:so + 1],
                                               stat[:, so + 1:so + 2]),
                                           reads=(bstat,), writes=(bstat,))):
            for (src, sb, so) in AB:
                fn(src, sb, so)

    for cp in range(0, NCH if KPH >= 1 else 0, 2):
        cA, cB = cp, cp + 1
        sxA, bxA, sxB, bxB = stg[0], bstg[0], stg[1], bstg[1]
        snA, bnA, snB, bnB = stg[2], bstg[2], sn2, bsn2
        dma("y", sxA[:, :], x_in[128 * cA:128 * (cA + 1), :], writes=(bxA,))
        dma("y", sxB[:, :], x_in[128 * cB:128 * (cB + 1), :], writes=(bxB,))
        ln_stats_pair([(sxA[:, :], bxA, 0), (sxB[:, :], bxB, 8)])
        act(snA[:, :], sxA[:, :], AF.Identity, scale=stat[:, 1:2],
            bias=stat[:, 5:6], reads=(bxA, bstat), writes=(bnA,))
        act(snB[:, :], sxB[:, :], AF.Identity, scale=stat[:, 9:10],
            bias=stat[:, 13:14], reads=(bxB, bstat), writes=(bnB,))
        dma("y", xn_scr[128 * cA:128 * (cA + 1), :], snA[:, :], reads=(bnA,),
            writes=(bxn_s[cA],))
        dma("y", xn_scr[128 * cB:128 * (cB + 1), :], snB[:, :], reads=(bnB,),
            writes=(bxn_s[cB],))
        ln_stats_pair([(snA[:, :], bnA, 0), (snB[:, :], bnB, 8)])
        act(snA[:, :], snA[:, :], AF.Identity, scale=stat[:, 1:2],
            bias=stat[:, 5:6], reads=(bnA, bstat), writes=(bnA,))
        act(snB[:, :], snB[:, :], AF.Identity, scale=stat[:, 9:10],
            bias=stat[:, 13:14], reads=(bnB, bstat), writes=(bnB,))
        vop(lambda v: v.tensor_mul(snA[:, :], snA[:, :], gbc_g[:, :]),
            reads=(bnA, bgbc), writes=(bnA,))
        vop(lambda v: v.tensor_mul(snB[:, :], snB[:, :], gbc_g[:, :]),
            reads=(bnB, bgbc), writes=(bnB,))
        vop(lambda v: v.tensor_add(nbf[0][:, :], snA[:, :], gbc_b[:, :]),
            reads=(bnA, bgbc), writes=(bstg[0],))
        vop(lambda v: v.tensor_add(nbf[1][:, :], snB[:, :], gbc_b[:, :]),
            reads=(bnB, bgbc), writes=(bstg[1],))
        fm_transposes(nbf[0], bstg[0], cA)
        fm_transposes(nbf[1], bstg[1], cB)

    P.barrier()

    dma("y", rowstage[0:1, 0:1024], b_out.rearrange("(b a) -> b a", b=1), writes=(brow,))

    bhst = buf("hst")
    bhstb = [buf(f"hstb{h}") for h in range(8)]
    blpc = buf("lpc")
    blc = buf("lcarry")
    vop(lambda v: v.memset(hst[:, :], 0.0), writes=(bhst,))
    vop(lambda v: v.memset(hstb[:, :], 0.0), writes=tuple(bhstb))
    vop(lambda v: v.memset(lprevc[:, :], 0.0), writes=(blpc,))
    vop(lambda v: v.memset(lprevbc[:, :], 0.0), writes=(blpc,))
    vop(lambda v: v.memset(lcarry[:, :], 0.0), writes=(blc,))

    MODE_B = [33, 32] + list(range(16, 32)) + list(range(16))

    brdb = [buf(f"rdb{i}") for i in range(16)]
    bxsf = [buf(f"xsfm{i}") for i in range(16)]
    bsz = [buf(f"siluz{m}") for m in range(16)]
    bbct = buf("bct")
    bdtl = buf("dtl")
    bldec = buf("ldec")
    bltm = buf("ltm")
    blend = buf("lend")
    bmpre = buf("mpre")
    bnltm = buf("nltm")
    bbtbc = buf("btbc")
    bmexp = buf("mexp")
    bgst = buf("gst")
    brm = buf("rmat")
    brmsk = buf("rmsk")
    bcpb = buf("cpb")
    bbtm = buf("btm")
    bbph = buf("bph")
    bxsb = buf("xs_bf")
    bdtx = buf("dtx")
    byt = [buf(f"yt{j}") for j in range(16)]
    bwm = buf("wdec")

    for j in range(12 if KPH >= 2 else 0):
        dma("g", rdb[j][:, :], w_out_d[128 * j:128 * (j + 1), :], writes=(brdb[j],))

    def emit_wout(c):
        for j in range(16):
            mm_raw([(PY[:, :], yt[:, 128 * j:128 * (j + 1)],
                     rdb[j][:, 0:512], j == 0, False),
                    (PX[0][:, :], yt[:, 128 * j:128 * (j + 1)],
                     rdb[j][:, 512:1024], j == 0, False)],
                   reads=(byt[j], brdb[j]), writes=(bpy, bpx[0]))
        mm_raw([(PY[:, :], ones1[0:1, :], rowstage[0:1, 0:512], False, True),
                (PX[0][:, :], ones1[0:1, :], rowstage[0:1, 512:1024], False, True)],
               reads=(brow, cb), writes=(bpy, bpx[0]))
        dma("y", stg[0][:, :], xn_scr[128 * c:128 * (c + 1), :], reads=(bxn_s[c],),
            writes=(bstg[0],))
        dma("y", stg[1][:, :], x_in[128 * c:128 * (c + 1), :], writes=(bstg[1],))
        vop(lambda v: v.scalar_tensor_tensor(stg[2][:, 0:512], PY[:, :], 1.0,
                                             stg[0][:, 0:512], ALU.mult, ALU.add),
            reads=(bpy, bstg[0]), writes=(bstg[2],))
        vop(lambda v: v.scalar_tensor_tensor(stg[2][:, 512:1024], PX[0][:, :], 1.0,
                                             stg[0][:, 512:1024], ALU.mult, ALU.add),
            reads=(bpx[0], bstg[0]), writes=(bstg[2],))
        vop(lambda v: v.tensor_add(stg[2][:, :], stg[2][:, :], stg[1][:, :]),
            reads=(bstg[2], bstg[1]), writes=(bstg[2],))
        dma("y", part_scr[128 * c:128 * (c + 1), :], stg[2][:, :], reads=(bstg[2],),
            writes=(bpart_s[c],))

    def emit_chain(c, mid=None):
        ct0 = (c % CPS) * 128
        for g in range(4):
            pz, pzb = PZ[g % 2], bpz[g % 2]
            for k in range(4):
                j = 4 * g + k
                trans(pz[:, 128 * k:128 * (k + 1)], xsfm[j][:, ct0:ct0 + 128],
                      identb[:, :], reads=(bxsf[j], cb), writes=(pzb,))
            if g % 2 == 0:
                vop(lambda v, pz=pz, g=g: v.tensor_copy(
                        xs_bf[:, 512 * g:512 * (g + 1)], pz[:, 0:512]),
                    reads=(pzb,), writes=(bxsb,))
            else:
                P.do("s", lambda s, pz=pz, g=g: s.copy(
                        xs_bf[:, 512 * g:512 * (g + 1)], pz[:, 0:512]),
                     reads=(pzb,), writes=(bxsb,))
        mm(PZ[0][:, 160:288], bT_b[:, ct0:ct0 + 128], cT_b[:, ct0:ct0 + 128], True, True,
           reads=(bbct,), writes=(bpz[0],))
        trans(PZ[0][:, 288:352], bT_b[:, ct0:ct0 + 128], identb[0:64, 0:64],
              reads=(bbct, cb), writes=(bpz[0],))
        vop(lambda v: v.tensor_copy(btbc_sb[:, :], PZ[0][:, 160:288]),
            reads=(bpz[0],), writes=(bbtbc,))
        vop(lambda v: v.tensor_copy(btm[:, :], PZ[0][:, 288:352]),
            reads=(bpz[0],), writes=(bbtm,))
        trans(PZ[0][:, 128:136], dts[:, ct0:ct0 + 128], identf[0:8, 0:8],
              reads=(bdtl, cb), writes=(bpz[0],))
        trans(PZ[0][:, 136:144], lts[:, ct0:ct0 + 128], identf[0:8, 0:8],
              reads=(bdtl, cb), writes=(bpz[0],))
        vop(lambda v: v.tensor_copy(ltm[:, :], PZ[0][:, 128:144]),
            reads=(bpz[0],), writes=(bltm,))
        trans(PZ[0][0:1, 144:152], lts[:, ct0 + 127:ct0 + 128], identf[0:8, 0:8],
              reads=(bdtl, cb), writes=(bpz[0],))
        vop(lambda v: v.tensor_copy(lendrow[:, :], PZ[0][0:1, 144:152]),
            reads=(bpz[0],), writes=(blend,))
        mm(PZ[0][:, 152:160], ones1[0:1, :], lendrow[0:1, :], True, True,
           reads=(cb, blend), writes=(bpz[0],))
        vop(lambda v: v.tensor_sub(wtm[:, :], PZ[0][:, 152:160], ltm[:, 8:16]),
            reads=(bpz[0], bltm), writes=(bwm,))
        act(wtm[:, :], wtm[:, :], AF.Exp, reads=(bwm,), writes=(bwm,))
        vop(lambda v: v.tensor_sub(decbc[:, :], PZ[0][:, 152:160], lprevbc[:, :]),
            reads=(bpz[0], blpc), writes=(bwm,))
        act(decbc[:, :], decbc[:, :], AF.Exp, reads=(bwm,), writes=(bwm,))
        vop(lambda v, ct0=ct0: v.tensor_scalar(rmat[:, :], lts[:, ct0:ct0 + 128],
                                               lprevc[:, :], None, ALU.subtract),
            reads=(bdtl, blpc), writes=(brm,))
        act(rmat[:, :], rmat[:, :], AF.Exp, reads=(brm,), writes=(brm,))
        vop(lambda v, ct0=ct0: v.tensor_copy(lprevc[:, :], lts[:, ct0 + 127:ct0 + 128]),
            reads=(bdtl,), writes=(blpc,))
        vop(lambda v: v.tensor_copy(lprevbc[:, :], PZ[0][:, 152:160]),
            reads=(bpz[0],), writes=(blpc,))
        vop(lambda v, ct0=ct0: v.tensor_mul(
                rmsk[:, :], bass.AP(lts, ct0, [[512, 8], [0, 8], [1, 128]]),
                mask8[:, :]),
            reads=(bdtl, cb), writes=(brmsk,))
        act(nltm[:, :], ltm[:, 8:16], AF.Identity, scale=-1.0,
            reads=(bltm,), writes=(bnltm,))
        for half in range(2):
            mm(PZ[1][:, 0:512], ones8[:, :], rmsk[:, 512 * half:512 * (half + 1)],
               True, True, reads=(cb, brmsk), writes=(bpz[1],))
            for hh in range(4):
                h = 4 * half + hh
                act(mpre[:, 128 * h:128 * (h + 1)], PZ[1][:, 128 * hh:128 * (hh + 1)],
                    AF.Identity, bias=nltm[:, h:h + 1],
                    reads=(bpz[1], bnltm), writes=(bmpre,))
        vop(lambda v: v.tensor_add(mpre[:, :], mpre[:, :], maskb8[:, :]),
            reads=(bmpre, cb), writes=(bmpre,))
        act(mexp[:, :], mpre[:, :], AF.Exp, reads=(bmpre,), writes=(bmexp,))
        vop(lambda v: v.tensor_mul(
                bass.AP(gst, 0, [[1024, 128], [128, 8], [1, 128]]),
                bass.AP(mexp, 0, [[1024, 128], [128, 8], [1, 128]]),
                bass.AP(btbc_sb, 0, [[128, 128], [0, 8], [1, 128]])),
            reads=(bmexp, bbtbc), writes=(bgst,))
        for h in range(NHM):
            act(bph[:, 64 * h:64 * (h + 1)], btm[:, :], AF.Identity,
                scale=wtm[:, h:h + 1], reads=(bbtm, bwm), writes=(bbph,))
        vop(lambda v: v.tensor_mul(rmsk[:, :], bass.AP(rmat, 0, [[128, 8], [0, 8], [1, 128]]),
                                   mask8[:, :]),
            reads=(brm, cb), writes=(brmsk,))
        if mid is not None:
            mid()
        for half in range(2):
            mm(PZ[1][0:64, 0:512], ones8[:, 0:64], rmsk[:, 512 * half:512 * (half + 1)],
               True, True, reads=(cb, brmsk), writes=(bpz[1],))
            vop(lambda v, half=half, ct0=ct0: v.tensor_mul(
                    bass.AP(cpb, 512 * half, [[1024, 64], [128, 4], [1, 128]]),
                    bass.AP(cT_b, ct0, [[512, 64], [0, 4], [1, 128]]),
                    bass.AP(PZ[1], 0, [[512, 64], [128, 4], [1, 128]])),
                reads=(bbct, bpz[1]), writes=(bcpb,))
        for h in range(NHM):
            act(dtx[:, 256 * h:256 * (h + 1)], xs_bf[:, 256 * h:256 * (h + 1)],
                AF.Identity, scale=ltm[:, h:h + 1],
                reads=(bxsb, bltm), writes=(bdtx,))

    YSLOT = [(PBIG, 0, bpb), (PBIG, 256, bpb), (PBIG, 512, bpb), (PBIG, 768, bpb),
             (PY, 0, bpy), (PY, 256, bpy), (PX[0], 0, bpx[0]), (PX[0], 256, bpx[0])]

    def emit_y(c):
        ct0 = (c % CPS) * 128
        for h in range(NHM):
            ypp, yc0, ypb = YSLOT[h]
            parts = []
            for q in range(2):
                o = ypp[:, yc0 + 128 * q:yc0 + 128 * (q + 1)]
                parts.append((o, dtx[:, 256 * h + 128 * q:256 * h + 128 * (q + 1)],
                              gst[:, 128 * h:128 * (h + 1)], q == 0, False))
            for q in range(2):
                o = ypp[:, yc0 + 128 * q:yc0 + 128 * (q + 1)]
                parts.append((o, xs_bf[:, 256 * h + 128 * q:256 * h + 128 * (q + 1)],
                              identD[:, 128 * h:128 * (h + 1)], False, False))
            for q in range(2):
                o = ypp[:, yc0 + 128 * q:yc0 + 128 * (q + 1)]
                parts.append((o, bass.AP(hstb, 256 * h + 128 * q, [[2048, 64], [1, 128]]),
                              cpb[:, 128 * h:128 * (h + 1)], False, q == 1))
            mm_raw(parts, reads=[bdtx, bgst, bxsb, cb, bhstb[h], bcpb], writes=(ypb,))
            hps = PS6[0:64, 0:256] if h % 2 == 0 else PX[1][0:64, 0:256]
            hpb = bp6 if h % 2 == 0 else bpx[1]
            mm(hps, bph[:, 64 * h:64 * (h + 1)], dtx[:, 256 * h:256 * (h + 1)],
               True, True, reads=(bbph, bdtx), writes=(hpb,))
            vop(lambda v, h=h, hps=hps: v.scalar_tensor_tensor(
                    bass.AP(hst, 256 * h, [[2048, 64], [1, 256]]),
                    bass.AP(hst, 256 * h, [[2048, 64], [1, 256]]),
                    decbc[0:64, h:h + 1], hps, ALU.mult, ALU.add),
                reads=(hpb, bwm, bhst), writes=(bhst,))
            vop(lambda v, h=h: v.tensor_copy(
                    bass.AP(hstb, 256 * h, [[2048, 64], [1, 256]]),
                    bass.AP(hst, 256 * h, [[2048, 64], [1, 256]])),
                reads=(bhst,), writes=(bhstb[h],))
        for (ypp, ypb, m0, nt) in ((PBIG, bpb, 0, 8), (PY, bpy, 8, 4), (PX[0], bpx[0], 12, 4)):
            vop(lambda v, ypp=ypp, m0=m0, nt=nt, ct0=ct0: v.tensor_mul(
                    bass.AP(yt, 128 * m0, [[2048, 128], [128, nt], [1, 128]]),
                    bass.AP(ypp, 0, [[128 * nt, 128], [128, nt], [1, 128]]),
                    bass.AP(szall, 512 * m0 + ct0, [[8192, 128], [512, nt], [1, 128]])),
                reads=tuple([ypb] + [bsz[m0 + i] for i in range(nt)]),
                writes=tuple(byt[m0 + i] for i in range(nt)))

    for sc in range(NSC if KPH >= 2 else 0):
        t0 = sc * SCLEN
        for mi, m in enumerate(MODE_B):
            wr, wbr = rdb[12 + mi % 4], brdb[12 + mi % 4]
            cw = 128 if m < 33 else 8
            dma("g", bass.AP(wr, 0, [[1024, 128], [128, 8], [1, cw]]),
                w_in[:, 128 * m:128 * m + cw].rearrange("(c p) n -> p c n", p=128),
                writes=(wbr,))
            pz, pzb = PZ[mi % 2], bpz[mi % 2]
            parts = []
            for fi in range(8):
                parts.append((pz[0:cw, 0:512],
                              bass.AP(wr, 128 * fi, [[1024, 128], [1, cw]]),
                              fm[fi][:, t0:t0 + 512], fi == 0, fi == 7))
            mm_raw(parts, reads=[wbr] + bfm, writes=(pzb,))
            if m < 16:
                act(bass.AP(szall, 512 * m, [[8192, 128], [1, 512]]), pz[:, :],
                    AF.Silu, bias=b_in_col[:, m:m + 1],
                    reads=(pzb, vb), writes=(bsz[m],))
            elif m < 32:
                act(xsfm[m - 16][:, :], pz[:, :], AF.Identity, bias=b_in_col[:, m:m + 1],
                    reads=(pzb, vb), writes=(bxsf[m - 16],))
            elif m == 32:
                act(bT_b[:, :], pz[0:64, :], AF.Identity, bias=b_in_col[0:64, 32:33],
                    reads=(pzb, vb), writes=(bbct,))
                act(cT_b[:, :], pz[64:128, :], AF.Identity, bias=b_in_col[64:128, 32:33],
                    reads=(pzb, vb), writes=(bbct,))
            else:
                act(dts[:, :], pz[0:8, :], AF.Identity, bias=comb8[:, :],
                    reads=(pzb, vb), writes=(bdtl,))
            if mi == 1:
                act(ldec[:, :], dts[:, :], AF.Abs, reads=(bdtl,), writes=(bldec,))
                act(ldec[:, :], ldec[:, :], AF.Exp, scale=-1.0, reads=(bldec,), writes=(bldec,))
                act(ldec[:, :], ldec[:, :], AF.Ln, bias=1.0, reads=(bldec,), writes=(bldec,))
                vop(lambda v: v.tensor_scalar(rmsk[0:8, 0:512], dts[:, :], 0.0, None, ALU.max),
                    reads=(bdtl,), writes=(brmsk,))
                vop(lambda v: v.tensor_add(dts[:, :], ldec[:, :], rmsk[0:8, 0:512]),
                    reads=(bldec, brmsk), writes=(bdtl,))
                vop(lambda v: v.tensor_scalar(ldec[:, :], dts[:, :], acol[:, :], None, ALU.mult),
                    reads=(bdtl, vb), writes=(bldec,))
                vop(lambda v: v.memset(rmsk[0:8, 0:512], 0.0), writes=(brmsk,))
                vop(lambda v: v.tensor_tensor_scan(lts[:, :], ldec[:, :], rmsk[0:8, 0:512],
                                                   lcarry[:, :], ALU.add, ALU.add),
                    reads=(bldec, brmsk, blc), writes=(bdtl,))
                vop(lambda v: v.tensor_copy(lcarry[:, :], lts[:, 511:512]),
                    reads=(bdtl,), writes=(blc,))
        for j in (12, 13, 14, 15):
            dma("g", rdb[j][:, :], w_out_d[128 * j:128 * (j + 1), :], writes=(brdb[j],))
        emit_chain(sc * CPS)

        for tc in range(CPS):
            c = sc * CPS + tc
            emit_y(c)
            if tc < CPS - 1:
                emit_chain(c + 1, mid=lambda c=c: emit_wout(c))
            else:
                emit_wout(c)

    P.barrier()

    for c in range(NCH if KPH >= 3 else 0):
        nb, bnb = nbf[c % 2], bstg[c % 2]
        dma("g", nb[:, :], xn_scr[128 * c:128 * (c + 1), :], reads=(bxn_s[c],), writes=(bnb,))
        fm_transposes(nb, bnb, c)

    bwvc = [buf(f"wvc{i}") for i in range(8)]
    bvtm = [buf(f"vtm{i}") for i in range(16)]
    bbias = buf("biasrow")
    for j in range(8 if KPH >= 3 else 0):
        dma("g", wvc[j][:, :], wv[128 * j:128 * (j + 1), :], writes=(bwvc[j],))
    dma("y", biasrow[0:1, :], bv.rearrange("(b a) -> b a", b=1), writes=(bbias,))
    for tt in range(16 if KPH >= 3 else 0):
        vop(lambda v, tt=tt: v.memset(bass.AP(vtm[tt], 64, [[1040, 128], [65, 16]]), 1.0),
            writes=(bvtm[tt],))
        parts = []
        for fi in range(8):
            for nh in range(2):
                parts.append((PBIG[:, 512 * nh:512 * (nh + 1)],
                              fm[fi][:, 128 * tt:128 * (tt + 1)],
                              wvc[fi][:, 512 * nh:512 * (nh + 1)], fi == 0, False))
        for nh in range(2):
            parts.append((PBIG[:, 512 * nh:512 * (nh + 1)], ones1[0:1, :],
                          biasrow[0:1, 512 * nh:512 * (nh + 1)], False, True))
        mm_raw(parts, reads=[cb, bbias] + bfm + bwvc, writes=(bpb,))
        vop(lambda v, tt=tt: v.tensor_copy(
                bass.AP(vtm[tt], 0, [[1040, 128], [65, 16], [1, 64]]), PBIG[:, :]),
            reads=(bpb,), writes=(bvtm[tt],))

    _bq2 = buf("q2s")
    for j in range(4, 8):
        wb_ = buf(f"wvc{j}")
        if wb_.w is not None:
            _bq2.r[wb_.w[0]] = max(_bq2.r.get(wb_.w[0], 0), wb_.w[1])
        for pe_, v_ in wb_.r.items():
            _bq2.r[pe_] = max(_bq2.r.get(pe_, 0), v_)

    bqp = [buf("qp0"), buf("qp1")]
    bkp = [buf("kp0"), buf("kp1")]
    bwqk = [buf("bwqk0"), buf("bwqk1")]
    batnT = [buf(f"atnT{i}") for i in range(8)]
    bq2 = buf("q2s")
    bnegm = buf("negm")
    bnrow = buf("nrow")
    bptt = [buf("ptt0"), buf("ptt1")]
    bptt2 = [buf("ptt2_0"), buf("ptt2_1")]
    for _b2 in bptt2:
        wb_ = buf("biasrow")
        if wb_.w is not None:
            _b2.r[wb_.w[0]] = max(_b2.r.get(wb_.w[0], 0), wb_.w[1])
        for pe_, v_ in wb_.r.items():
            _b2.r[pe_] = max(_b2.r.get(pe_, 0), v_)
    brc = buf("rc65")

    pend_norm = [None]

    def flush_norm():
        if pend_norm[0] is not None:
            pend_norm[0]()
            pend_norm[0] = None

    for p in range(NPAIR if KPH >= 3 else 0):
        qp, kp = qpair[p % 2], kpair[p % 2]
        bq_, bk_ = bqp[p % 2], bkp[p % 2]
        for (wdram, stg_t, stg_b) in ((wq, wqk[0], bwqk[0]), (wk, wqk[1], bwqk[1])):
            dma("g", bass.AP(stg_t, 0, [[1024, 128], [128, 8], [1, 128]]),
                wdram[:, 128 * p:128 * (p + 1)].rearrange("(c p) n -> p c n", p=128),
                writes=(stg_b,))
        for (wstg, bwstg, dst, bdst, bcol) in ((wqk[0], bwqk[0], qp, bq_, 0),
                                               (wqk[1], bwqk[1], kp, bk_, 8)):
            for ns in range(4):
                pz = PBIG[:, 512 * (ns % 2):512 * (ns % 2 + 1)]
                pzb = bpb
                parts = []
                for fi in range(8):
                    parts.append((pz, bass.AP(wstg, 128 * fi, [[1024, 128], [1, 128]]),
                                  fm[fi][:, 512 * ns:512 * (ns + 1)], fi == 0, fi == 7))
                mm_raw(parts, reads=[bwstg] + bfm, writes=(pzb,))
                act(dst[:, 512 * ns:512 * (ns + 1)], pz, AF.Identity,
                    bias=bqk_col[:, bcol + p:bcol + p + 1], reads=(pzb, vb), writes=(bdst,))
                if ns == 0 and wstg is wqk[0]:
                    flush_norm()
        for which, (src, bsrc) in enumerate(((qp, bq_), (kp, bk_))):
            for ns in range(4):
                s = 4 * which + ns
                act(q2s8[:, 512 * s:512 * (s + 1)], src[:, 512 * ns:512 * (ns + 1)],
                    AF.Square, reads=(bsrc,), writes=(bq2,))
        for which in range(2):
            for ns in range(4):
                s = 4 * which + ns
                mbank, mb = (PS6, bp6) if s % 2 == 0 else (PY, bpy)
                mm(mbank[0:2, 0:512], hselb[:, :], q2s8[:, 512 * s:512 * (s + 1)],
                   True, True, reads=(cb, bq2), writes=(mb,))
                if ns == 0:
                    vop(lambda v, which=which, mbank=mbank: v.reduce_max(
                            negm[0:2, which:which + 1], mbank[0:2, 0:512], AX.X),
                        reads=(mb,), writes=(bnegm,))
                else:
                    vop(lambda v, which=which, mbank=mbank: v.reduce_max(
                            stat[0:2, 6 + which:7 + which], mbank[0:2, 0:512], AX.X),
                        reads=(mb,), writes=(bstat,))
                    vop(lambda v, which=which: v.tensor_max(negm[0:2, which:which + 1],
                                                            negm[0:2, which:which + 1],
                                                            stat[0:2, 6 + which:7 + which]),
                        reads=(bnegm, bstat), writes=(bnegm,))
        vop(lambda v: v.tensor_mul(negm[0:2, 0:1], negm[0:2, 0:1], negm[0:2, 1:2]),
            reads=(bnegm,), writes=(bnegm,))
        act(negm[0:2, 0:1], negm[0:2, 0:1], AF.Sqrt, scale=1.0 / 64.0,
            reads=(bnegm,), writes=(bnegm,))
        vop(lambda v: v.tensor_scalar(negm[0:2, 0:1], negm[0:2, 0:1], -1.0, None, ALU.mult),
            reads=(bnegm,), writes=(bnegm,))
        trans(PS6[0:1, 24:26], negm[0:2, 0:1], identf[0:2, 0:2], reads=(bnegm, cb), writes=(bp6,))
        vop(lambda v: v.tensor_copy(nrow[:, :], PS6[0:1, 24:26]), reads=(bp6,), writes=(bnrow,))
        mm(PS6[:, 26:28], ones1[0:1, :], nrow[0:1, :], True, True, reads=(cb, bnrow), writes=(bp6,))
        vop(lambda v: v.tensor_copy(negm[:, :], PS6[:, 26:28]), reads=(bp6,), writes=(bnegm,))

        for qq in range(4):
            q0 = 512 * qq

            def pt_slot(kt, head):
                base = ptt if kt % 4 < 2 else ptt2
                sl = 512 * (kt % 2)
                return base[head][:, sl:sl + 512], (bptt if kt % 4 < 2 else bptt2)[head]

            def st_emit(kt):
                par = kt % 2
                pa, ba = pt_slot(kt, 0)
                pb, bb = pt_slot(kt, 1)
                mm_raw([(PZ[par][:, :], kp[0:64, 128 * kt:128 * (kt + 1)],
                         qp[0:64, q0:q0 + 512], True, True),
                        (PX[par][:, :], kp[64:128, 128 * kt:128 * (kt + 1)],
                         qp[64:128, q0:q0 + 512], True, True)],
                       reads=(bq_, bk_), writes=(bpz[par], bpx[par]))
                act(pa, PZ[par][:, :], AF.Exp, bias=negm[:, 0:1], scale=0.125,
                    reads=(bpz[par], bnegm), writes=(ba,))
                act(pb, PX[par][:, :], AF.Exp, bias=negm[:, 1:2], scale=0.125,
                    reads=(bpx[par], bnegm), writes=(bb,))

            if qq % 2 == 1:
                AVB = [(PS6[0:65, :], PS6[64:65, :], PS6[64:128, :], PS6[0:64, :], bp6),
                       (PY[0:65, :], PY[64:65, :], PY[64:128, :], PY[0:64, :], bpy)]
            else:
                AVB = [(PBIG[0:65, 0:512], PBIG[64:65, 0:512], PBIG[64:128, 0:512],
                        PBIG[0:64, 0:512], bpb),
                       (PBIG[0:65, 512:1024], PBIG[64:65, 512:1024],
                        PBIG[64:128, 512:1024], PBIG[0:64, 512:1024], bpb)]

            def av_emit(kt, AVB=AVB):
                pa, ba = pt_slot(kt, 0)
                pb, bb = pt_slot(kt, 1)
                mm_raw([(AVB[0][0], vtm[kt][:, 65 * (2 * p):65 * (2 * p) + 65],
                         pa, kt == 0, kt == 15)],
                       reads=(bvtm[kt], ba), writes=(AVB[0][4],))
                mm_raw([(AVB[1][0], vtm[kt][:, 65 * (2 * p + 1):65 * (2 * p + 1) + 65],
                         pb, kt == 0, kt == 15)],
                       reads=(bvtm[kt], bb), writes=(AVB[1][4],))

            st_emit(0)
            st_emit(1)
            for kt in range(2, 16):
                st_emit(kt)
                av_emit(kt - 2)
                if kt == 7:
                    flush_norm()
            av_emit(14)
            av_emit(15)

            rcsl2 = [rc65[64:65, 512 * ((qq + hh) % 2):512 * ((qq + hh) % 2) + 512]
                     for hh in range(2)]
            for hh, (avfull, avsum, avbc, avdat, avb) in enumerate(AVB):
                vop(lambda v, avsum=avsum, rcsl=rcsl2[hh]: v.reciprocal(rcsl, avsum),
                    reads=(avb,), writes=(brc,))

            def norm_emit(AVB=AVB, qq=qq, p=p, q0=q0, rcsl2=rcsl2):
                rcbc = rc65[0:64, 512 * (qq % 2):512 * (qq % 2) + 512]
                for hh, (avfull, avsum, avbc, avdat, avb) in enumerate(AVB):
                    mm_raw([(avbc, ones65[64:65, 0:64], rcsl2[hh], True, True)],
                           reads=(brc, cb), writes=(avb,))
                    vop(lambda v, avbc=avbc, rcbc=rcbc: v.tensor_copy(rcbc, avbc),
                        reads=(avb,), writes=(brc,))
                    vop(lambda v, hh=hh, p=p, q0=q0, avdat=avdat, rcbc=rcbc: v.tensor_mul(
                            attnT[p][64 * hh:64 * (hh + 1), q0:q0 + 512],
                            avdat, rcbc),
                        reads=(avb, brc), writes=(batnT[p],))

            pend_norm[0] = norm_emit

    flush_norm()
    P.barrier()

    rwbc = sub("rwbc", [128, 1024], F32, R16 + 12 * K)
    brwbc = buf("rwbc")
    for j in range(8 if KPH >= 4 else 0):
        dma("g", woc[j][:, :], wo[128 * j:128 * (j + 1), :], writes=(buf(f"woc{j}"),))
    bwoc = [buf(f"woc{j}") for j in range(8)]
    dma("y", biasrow[0:1, :], bo.rearrange("(b a) -> b a", b=1), writes=(bbias,))
    build_bc(gbc_g, gbc_b, l1g, l1b)
    if KPH >= 4:
        dma("y", rowstage[0:1, 0:1024], rw.rearrange("(b a) 1 -> b a", b=1), writes=(brow,))
        for j in range(0, D, 512):
            mm(PBIG[:, j:j + 512], ones1[0:1, :], rowstage[0:1, j:j + 512], True, True,
               reads=(cb, brow), writes=(bpb,))
        vop(lambda v: v.tensor_copy(rwbc[:, :], PBIG[:, :]), reads=(bpb,), writes=(brwbc,))
    bkey = buf("keyrow")
    pend = []
    for c in range(NCH if KPH >= 4 else 0):
        WOP = [PBIG[:, 0:512], PBIG[:, 512:1024]] if c % 2 == 0 else [PZ[0][:, :], PZ[1][:, :]]
        WOB = (bpb,) if c % 2 == 0 else (bpz[0], bpz[1])
        parts = []
        for j in range(8):
            for nh in range(2):
                parts.append((WOP[nh], attnT[j][:, 128 * c:128 * (c + 1)],
                              woc[j][:, 512 * nh:512 * (nh + 1)], j == 0, False))
        for nh in range(2):
            parts.append((WOP[nh], ones1[0:1, :],
                          biasrow[0:1, 512 * nh:512 * (nh + 1)], False, True))
        mm_raw(parts, reads=[cb, bbias] + batnT + bwoc, writes=WOB)
        for fn in pend:
            fn()
        pend = []
        dma("y", stg[1][:, :], xn_scr[128 * c:128 * (c + 1), :], reads=(bxn_s[c],),
            writes=(bstg[1],))
        if c % 2 == 0:
            vop(lambda v: v.scalar_tensor_tensor(stg[2][:, :], PBIG[:, :], 1.0, stg[1][:, :],
                                                 ALU.mult, ALU.add),
                reads=(bpb, bstg[1]), writes=(bstg[2],))
        else:
            vop(lambda v: v.scalar_tensor_tensor(stg[2][:, 0:512], PZ[0][:, :], 1.0,
                                                 stg[1][:, 0:512], ALU.mult, ALU.add),
                reads=(bpz[0], bstg[1]), writes=(bstg[2],))
            vop(lambda v: v.scalar_tensor_tensor(stg[2][:, 512:1024], PZ[1][:, :], 1.0,
                                                 stg[1][:, 512:1024], ALU.mult, ALU.add),
                reads=(bpz[1], bstg[1]), writes=(bstg[2],))
        ln_stats(stg[2][:, :], bstg[2])
        vop(lambda v: v.tensor_scalar(stg[2][:, :], stg[2][:, :], stat[:, 0:1], stat[:, 1:2],
                                      ALU.subtract, ALU.mult),
            reads=(bstg[2], bstat), writes=(bstg[2],))
        vop(lambda v: v.tensor_mul(stg[2][:, :], stg[2][:, :], gbc_g[:, :]),
            reads=(bstg[2], bgbc), writes=(bstg[2],))
        vop(lambda v: v.tensor_add(stg[2][:, :], stg[2][:, :], gbc_b[:, :]),
            reads=(bstg[2], bgbc), writes=(bstg[2],))
        dma("y", x1_scr[128 * c:128 * (c + 1), :], stg[2][:, :], reads=(bstg[2],),
            writes=(bx1_s[c],))
        vop(lambda v: v.tensor_mul(sq_scr[:, :], stg[2][:, :], rwbc[:, :]),
            reads=(bstg[2], brwbc), writes=(bsq,))
        vop(lambda v: v.reduce_sum(stat[:, 5:6], sq_scr[:, :], AX.X),
            reads=(bsq,), writes=(bstat,))
        act(nbf[0][:, :], stg[2][:, :], AF.Copy, reads=(bstg[2],), writes=(bstg[0],))

        def mk_pend(c):
            def fl():
                trans(PS6[0:1, 32:160], stat[:, 5:6], identf[:, :], reads=(bstat, cb),
                      writes=(bp6,))
                vop(lambda v, c=c: v.tensor_copy(keyrow[0:1, 128 * c:128 * (c + 1)],
                                                 PS6[0:1, 32:160]),
                    reads=(bp6,), writes=(bkey,))
                fm_transposes(nbf[0], bstg[0], c)
            return [fl]

        pend = mk_pend(c)
    for fn in pend:
        fn()

    P.barrier()

    bt5a = buf("t512a")
    bt5b = buf("t512b")
    bmT = buf("masksT")
    bmbc = buf("maskbc")
    bhm = buf("hmsk")
    bhs = [buf(f"hsb{m}") for m in range(16)]
    bw6 = [buf(f"wstp6{i}") for i in range(4)]
    for ns in range(4 if KPH >= 5 else 0):
        bt5c = buf("t512c")
        vop(lambda v, ns=ns: v.tensor_scalar(t512a[0:1, :], keyrow[0:1, 512 * ns:512 * (ns + 1)],
                                             rb_t[0:1, 0:1], 64.0, ALU.add, ALU.add),
            reads=(bkey, vb), writes=(bt5a,))
        for dstep in (64.0, 32.0, 16.0, 8.0, 4.0):
            vop(lambda v, d=dstep: v.tensor_scalar(t512b[0:1, :], t512a[0:1, :], d, -d,
                                                   ALU.is_ge, ALU.mult),
                reads=(bt5a,), writes=(bt5b,))
            vop(lambda v: v.tensor_add(t512a[0:1, :], t512a[0:1, :], t512b[0:1, :]),
                reads=(bt5a, bt5b), writes=(bt5a,))
        vop(lambda v: v.tensor_scalar(t512b[0:1, :], t512a[0:1, :], 1.0, None, ALU.is_ge),
            reads=(bt5a,), writes=(bt5b,))
        vop(lambda v: v.tensor_scalar(t512c[0:1, :], t512a[0:1, :], 2.0, None, ALU.is_ge),
            reads=(bt5a,), writes=(bt5c,))
        vop(lambda v: v.tensor_add(t512b[0:1, :], t512b[0:1, :], t512c[0:1, :]),
            reads=(bt5b, bt5c), writes=(bt5b,))
        vop(lambda v: v.tensor_scalar(t512c[0:1, :], t512a[0:1, :], 3.0, None, ALU.is_ge),
            reads=(bt5a,), writes=(bt5c,))
        vop(lambda v, ns=ns: v.tensor_add(keyrow[0:1, 512 * ns:512 * (ns + 1)], t512b[0:1, :],
                                          t512c[0:1, :]),
            reads=(bt5b, bt5c), writes=(bkey,))
    for ns in range(4 if KPH >= 5 else 0):
        mm(PZ[1][0:4, :], ones1[0:1, 0:4], keyrow[0:1, 512 * ns:512 * (ns + 1)], True, True,
           reads=(cb, bkey), writes=(bpz[1],))
        vop(lambda v, ns=ns: v.tensor_scalar(masksT[:, 512 * ns:512 * (ns + 1)], PZ[1][0:4, :],
                                             cvec4[:, :], None, ALU.is_equal),
            reads=(bpz[1], cb), writes=(bmT,))

    for ex in range(NEXP if KPH >= 5 else 0):
        for ns in range(4):
            vop(lambda v, ns=ns, ex=ex: v.tensor_scalar(
                    t512a[0:1, :], keyrow[0:1, 512 * ns:512 * (ns + 1)], float(ex), None,
                    ALU.is_equal),
                reads=(bkey,), writes=(bt5a,))
            mm(PZ[1][:, :], ones1[0:1, :], t512a[0:1, :], True, True,
               reads=(cb, bt5a), writes=(bpz[1],))
            vop(lambda v, ns=ns: v.tensor_copy(maskbc[:, 512 * ns:512 * (ns + 1)], PZ[1][:, :]),
                reads=(bpz[1],), writes=(bmbc,))
        for half in range(2):
            for mj in range(16):
                wr, wbr = wstp6[mj % 4], bw6[mj % 4]
                ffc = 2048 * half + 128 * mj
                dma("g", bass.AP(wr, 0, [[1024, 128], [128, 8], [1, 128]]),
                    w1[ex, :, ffc:ffc + 128].rearrange("(c p) n -> p c n", p=128), writes=(wbr,))
                if mj % 2 == 0:
                    pset = [PBIG[:, 0:512], PBIG[:, 512:1024], PZ[0][:, :], PZ[1][:, :]]
                    pbufs = [bpb, bpb, bpz[0], bpz[1]]
                else:
                    pset = [PX[0][:, :], PX[1][:, :], PS6[:, :], PY[:, :]]
                    pbufs = [bpx[0], bpx[1], bp6, bpy]
                parts = []
                for fi in range(8):
                    for ns in range(4):
                        parts.append((pset[ns], bass.AP(wr, 128 * fi, [[1024, 128], [1, 128]]),
                                      fm[fi][:, 512 * ns:512 * (ns + 1)], fi == 0, fi == 7))
                mm_raw(parts, reads=[wbr] + bfm, writes=tuple(set(pbufs)))
                bcol = b1_col[:, 32 * ex + 16 * half + mj:32 * ex + 16 * half + mj + 1]
                for ns in range(4):
                    act(hmsk[:, 512 * ns:512 * (ns + 1)], pset[ns], AF.Relu, bias=bcol,
                        reads=(pbufs[ns], vb), writes=(bhm,))
                vop(lambda v, mj=mj: v.tensor_mul(hsb[mj][:, :], hmsk[:, :], maskbc[:, :]),
                    reads=(bhm, bmbc), writes=(bhs[mj],))
                dma("g", rdb[mj][:, :],
                    w2[ex, 2048 * half + 128 * mj:2048 * half + 128 * (mj + 1), :],
                    writes=(brdb[mj],))
            for tt in range(16):
                PP = PX if tt % 2 == 0 else PZ
                PB_ = bpx if tt % 2 == 0 else bpz
                parts = []
                for mj in range(16):
                    for s2 in range(2):
                        last = (mj == 15) and not (ex == 0 and half == 0)
                        parts.append((PP[s2][:, :], hsb[mj][:, 128 * tt:128 * (tt + 1)],
                                      rdb[mj][:, 512 * s2:512 * (s2 + 1)], mj == 0, last))
                if ex == 0 and half == 0:
                    for s2 in range(2):
                        parts.append((PP[s2][:, :], masksT[:, 128 * tt:128 * (tt + 1)],
                                      b2_sb[:, 512 * s2:512 * (s2 + 1)], False, True))
                mm_raw(parts, reads=[vb, bmT] + bhs + brdb, writes=(PB_[0], PB_[1]))
                sst, bsst = stg[tt % 2], bstg[tt % 2]
                vop(lambda v, sst=sst, PP=PP: v.tensor_copy(sst[:, 0:512], PP[0][:, :]),
                    reads=(PB_[0],), writes=(bsst,))
                vop(lambda v, sst=sst, PP=PP: v.tensor_copy(sst[:, 512:1024], PP[1][:, :]),
                    reads=(PB_[1],), writes=(bsst,))
                dma("g", x1_scr[128 * tt:128 * (tt + 1), :], sst[:, :], reads=(bsst,),
                    writes=(bx1_s[tt],), accum_op=ALU.add)

    P.barrier()

    build_bc(gbc_g, gbc_b, l2g, l2b)
    for c in range(NCH if KPH >= 6 else 0):
        dma("y", stg[0][:, :], x1_scr[128 * c:128 * (c + 1), :], reads=(bx1_s[c],),
            writes=(bstg[0],))
        dma("y", stg[1][:, :], part_scr[128 * c:128 * (c + 1), :], reads=(bpart_s[c],),
            writes=(bstg[1],))
        ln_stats(stg[0][:, :], bstg[0])
        vop(lambda v: v.tensor_scalar(stg[2][:, :], stg[0][:, :], stat[:, 0:1], stat[:, 1:2],
                                      ALU.subtract, ALU.mult),
            reads=(bstg[0], bstat), writes=(bstg[2],))
        vop(lambda v: v.tensor_mul(stg[2][:, :], stg[2][:, :], gbc_g[:, :]),
            reads=(bstg[2], bgbc), writes=(bstg[2],))
        vop(lambda v: v.tensor_add(stg[2][:, :], stg[2][:, :], gbc_b[:, :]),
            reads=(bstg[2], bgbc), writes=(bstg[2],))
        vop(lambda v: v.tensor_add(stg[2][:, :], stg[2][:, :], stg[1][:, :]),
            reads=(bstg[2], bstg[1]), writes=(bstg[2],))
        dma("y", out_hbm[128 * c:128 * (c + 1), :], stg[2][:, :], reads=(bstg[2],))

    for q in DMAQ:
        for i in range(NLANE):
            lane = f"{q}{i}"
            P._wait(q, lane, P.cnt[lane])

    global _LASTP
    _LASTP = P
    with nc.Block() as block:
        block.tensor(lambda eng: _replay(P, "t", eng, sems))
        block.vector(lambda eng: _replay(P, "v", eng, sems))
        block.scalar(lambda eng: _replay(P, "s", eng, sems))
        block.gpsimd(lambda eng: _replay(P, "g", eng, sems))
        block.sync(lambda eng: _replay(P, "y", eng, sems))

    ctx.close()
    return nc


def kernel(**inputs):
    nc = build_program()
    consts = host_consts()
    x = np.ascontiguousarray(np.asarray(inputs["L0_attn_input_tensor"], dtype=np.float32))
    names = {
        "w_in": "L0_attn_W_in", "b_in": "L0_attn_b_in", "a_log": "L0_attn_A_log",
        "d_skip": "L0_attn_D_skip", "dt_bias": "L0_attn_dt_bias", "w_out": "L0_attn_W_out",
        "b_out": "L0_attn_b_out", "ln_g": "L0_attn_ln_gamma", "ln_b": "L0_attn_ln_beta",
        "wq": "L0_ffn_Wq", "bq": "L0_ffn_bq", "wk": "L0_ffn_Wk", "bk": "L0_ffn_bk",
        "wv": "L0_ffn_Wv", "bv": "L0_ffn_bv", "wo": "L0_ffn_Wo", "bo": "L0_ffn_bo",
        "rw": "L0_ffn_router_w", "rb": "L0_ffn_router_b", "w1": "L0_ffn_W1", "b1": "L0_ffn_b1",
        "w2": "L0_ffn_W2", "b2": "L0_ffn_b2", "l1g": "L0_ffn_ln1_g", "l1b": "L0_ffn_ln1_b",
        "l2g": "L0_ffn_ln2_g", "l2b": "L0_ffn_ln2_b",
    }
    base = {k: np.ascontiguousarray(np.asarray(inputs[v], np.float32)) for k, v in names.items()}
    base.update(consts)
    in_maps = []
    for c in range(8):
        m = dict(base)
        m["x_in"] = np.ascontiguousarray(x[c])
        in_maps.append(m)
    res = run_bass_kernel_spmd(nc, in_maps, core_ids=list(range(8)))
    return np.stack([res.results[c]["out"] for c in range(8)], axis=0)


if __name__ == "__main__":
    print("building program...")
    nc = build_program()
    print("built ok")


# version 20
# speedup vs baseline: 1.0606x; 1.0054x over previous
import os
import numpy as np
import ml_dtypes
import concourse.bass as bass
import concourse.mybir as mybir
from concourse.bass_utils import run_bass_kernel_spmd
from contextlib import ExitStack

F32 = mybir.dt.float32
BF16 = mybir.dt.bfloat16
AF = mybir.ActivationFunctionType
ALU = mybir.AluOpType
AX = mybir.AxisListType

T = 2048
D = 1024
NCH = 16
SCLEN = 512
NSC = 4
CPS = 4
INNER = 2048
NHM = 8
PD = 256
STATE = 64
PROJ = 4232
NHA = 16
NPAIR = 8
DFF = 4096
NEXP = 4
EPS = 1e-5
NEG = -1e30

ENGS = ("t", "v", "s", "g", "y")
DMAQ = ("g", "y")
NLANE = 8
SEMS = list(ENGS) + [f"{q}{i}" for q in DMAQ for i in range(NLANE)] + [f"{e}B" for e in ENGS]


class Buf:
    __slots__ = ("w", "r")

    def __init__(self):
        self.w = None
        self.r = {}


class Prog:
    def __init__(self):
        self.cnt = dict.fromkeys(SEMS, 0)
        self.wm = {c: dict.fromkeys(SEMS, 0) for c in ENGS}
        self.items = {e: [] for e in ENGS}
        self.lane_rr = dict.fromkeys(DMAQ, 0)

    def _wait(self, e, pe, v):
        if v <= 0 or v <= self.wm[e][pe]:
            return
        self.wm[e][pe] = v
        self.items[e].append(("w", pe, v))

    def wait_tok(self, e, tok):
        if tok is not None:
            self._wait(e, tok[0], tok[1])

    def do(self, e, fns, reads=(), writes=(), inc=None):
        if callable(fns):
            fns = (fns,)
        for b in reads:
            self.wait_tok(e, b.w)
        for b in writes:
            self.wait_tok(e, b.w)
            for pe, v in b.r.items():
                self._wait(e, pe, v)
        if e in DMAQ:
            lane = f"{e}{self.lane_rr[e]}"
            self.lane_rr[e] = (self.lane_rr[e] + 1) % NLANE
            self._wait(e, lane, self.cnt[lane])
            sem_key = lane
            inc_amt = 16
        else:
            sem_key = e
            inc_amt = 1 if inc is None else inc
        for fn in fns:
            self.items[e].append(("o", fn))
        self.cnt[sem_key] += inc_amt
        self.items[e].append(("i", sem_key, inc_amt))
        tok = (sem_key, self.cnt[sem_key])
        for b in reads:
            b.r[sem_key] = max(b.r.get(sem_key, 0), tok[1])
        for b in writes:
            b.w = tok
            b.r = {}
        return tok

    def barrier(self):
        toks = {}
        for e in ENGS:
            if e in DMAQ:
                for i in range(NLANE):
                    lane = f"{e}{i}"
                    self._wait(e, lane, self.cnt[lane])
            else:
                self._wait(e, e, self.cnt[e])
            b = f"{e}B"
            self.cnt[b] += 1
            self.items[e].append(("b", b, 1))
            toks[b] = self.cnt[b]
        for e in ENGS:
            for pe in ENGS:
                self._wait(e, f"{pe}B", toks[f"{pe}B"])


def _replay(P, e, eng, sems):
    last = None
    for it in P.items[e]:
        k = it[0]
        if k == "o":
            last = it[1](eng)
        elif k == "w":
            eng.wait_ge(sems[it[1]], it[2])
        elif k == "i":
            last.then_inc(sems[it[1]], it[2])
        elif k == "b":
            eng.nop().then_inc(sems[it[1]], it[2])


def host_consts():
    c = {}
    c["c_identf"] = np.eye(128, dtype=np.float32)
    c["c_identb"] = np.eye(128, dtype=np.float32).astype(ml_dtypes.bfloat16)
    c["c_ones1"] = np.ones((1, 128), dtype=np.float32)
    c["c_ones8"] = np.ones((8, 128), dtype=np.float32)
    hs = np.zeros((128, 2), dtype=np.float32)
    hs[0:64, 0] = 1.0
    hs[64:128, 1] = 1.0
    c["c_hsel"] = hs
    tri = np.where(np.arange(128)[:, None] <= np.arange(128)[None, :], 0.0, NEG).astype(np.float32)
    c["c_maskb8"] = np.tile(tri, (1, 8))
    mb = np.zeros((8, 1024), dtype=np.float32)
    for h in range(8):
        mb[h, 128 * h:128 * (h + 1)] = 1.0
    c["c_mask8"] = mb
    c["c_cvec4"] = np.arange(4, dtype=np.float32).reshape(4, 1)
    c["c_ones65"] = np.ones((65, 128), dtype=np.float32)
    c["c_imm"] = np.array([[EPS], [1.0]], dtype=np.float32) * np.ones((1, 1), np.float32)
    c["c_imm"] = np.concatenate([np.full((128, 1), EPS, np.float32),
                                 np.full((128, 1), 1.0, np.float32)], axis=1)
    return c


KPH = int(os.environ.get("KPH", "99"))
_LASTP = None


def build_program():
    nc = bass.Bass("TRN2", target_bir_lowering=False)

    def dram(name, shape, dtype=F32, kind="ExternalInput"):
        return nc.dram_tensor(name, list(shape), dtype, kind=kind).ap()

    x_in = dram("x_in", (T, D))
    w_in = dram("w_in", (D, PROJ))
    b_in = dram("b_in", (PROJ,))
    a_log = dram("a_log", (NHM,))
    d_skip = dram("d_skip", (NHM,))
    dt_bias = dram("dt_bias", (NHM,))
    w_out_d = dram("w_out", (INNER, D))
    b_out = dram("b_out", (D,))
    ln_g = dram("ln_g", (D,))
    ln_b = dram("ln_b", (D,))
    wq = dram("wq", (D, D))
    bq = dram("bq", (D,))
    wk = dram("wk", (D, D))
    bk = dram("bk", (D,))
    wv = dram("wv", (D, D))
    bv = dram("bv", (D,))
    wo = dram("wo", (D, D))
    bo = dram("bo", (D,))
    rw = dram("rw", (D, 1))
    rb = dram("rb", (1,))
    w1 = dram("w1", (NEXP, D, DFF))
    b1 = dram("b1", (NEXP, DFF))
    w2 = dram("w2", (NEXP, DFF, D))
    b2 = dram("b2", (NEXP, D))
    l1g = dram("l1g", (D,))
    l1b = dram("l1b", (D,))
    l2g = dram("l2g", (D,))
    l2b = dram("l2b", (D,))
    c_identf = dram("c_identf", (128, 128))
    c_identb = dram("c_identb", (128, 128), BF16)
    c_ones1 = dram("c_ones1", (1, 128))
    c_ones8 = dram("c_ones8", (8, 128))
    c_hsel = dram("c_hsel", (128, 2))
    c_maskb8 = dram("c_maskb8", (128, 1024))
    c_mask8 = dram("c_mask8", (8, 1024))
    c_cvec4 = dram("c_cvec4", (4, 1))
    c_ones65 = dram("c_ones65", (65, 128))
    c_imm = dram("c_imm", (128, 2))
    xn_scr = dram("xn_scr", (T, D), kind="Internal")
    part_scr = dram("part_scr", (T, D), kind="Internal")
    x1_scr = dram("x1_scr", (T, D), kind="Internal")
    out_hbm = dram("out", (T, D), kind="ExternalOutput")

    ARENA = 196 * 1024
    arena = nc.alloc_sbuf_tensor("arena", [128, ARENA], mybir.dt.uint8)
    arena_addr = nc.lookup_mloc(arena).addr

    def sub(name, shape, dtype, off):
        per_part = int(np.prod(shape[1:])) * mybir.dt.size(dtype)
        assert off + per_part <= ARENA, f"arena overflow: {name}"
        hnd = nc.virtual_sbuf_tensor(name, list(shape), dtype)
        nc._tensor(hnd.name, hnd.shape, hnd.dtype, type="SB")
        mloc = nc.lookup_mloc(hnd)
        mloc.allocated = True
        mloc.addr = arena_addr + off
        return hnd

    K = 1024
    CUR = [0]
    def nxt(name, shape, dtype):
        per = int(np.prod(shape[1:])) * mybir.dt.size(dtype)
        off = (CUR[0] + 31) & ~31
        CUR[0] = off + per
        return sub(name, shape, dtype, off)
    identf = nxt("identf", [128, 128], F32)
    identb = nxt("identb", [128, 128], BF16)
    ones1 = nxt("ones1", [1, 128], F32)
    ones8 = nxt("ones8", [8, 128], F32)
    hsel = nxt("hsel", [128, 2], F32)
    hselb = nxt("hselb", [128, 2], BF16)
    identD = nxt("identD", [128, 8 * 128], BF16)
    cvec4 = nxt("cvec4", [4, 1], F32)
    immt = nxt("immt", [128, 2], F32)
    ones65 = nxt("ones65", [65, 128], F32)
    maskb8 = nxt("maskb8", [128, 1024], F32)
    mask8 = nxt("mask8", [8, 1024], F32)
    b_in_col = nxt("b_in_col", [128, 33], F32)
    comb8 = nxt("comb8", [8, 1], F32)
    acol = nxt("acol", [8, 1], F32)
    tmp8a = nxt("tmp8a", [8, 1], F32)
    drow = nxt("drow", [1, 8], F32)
    dbc = nxt("dbc", [128, 8], F32)
    bqk_col = nxt("bqk_col", [128, 16], F32)
    rw_col = nxt("rw_col", [128, 8], BF16)
    rb_t = nxt("rb_t", [1, 1], F32)
    b1_col = nxt("b1_col", [128, 128], F32)
    lprevc = nxt("lprevc", [8, 1], F32)
    lprevbc = nxt("lprevbc", [128, 8], F32)
    decbc = nxt("decbc", [128, 8], F32)
    wtm = nxt("wtm", [128, 8], F32)
    lendrow = nxt("lendrow", [1, 8], F32)
    lcarry = nxt("lcarry", [8, 1], F32)
    stat = nxt("stat", [128, 16], F32)
    negm = nxt("negm", [128, 2], F32)
    nrow = nxt("nrow", [1, 2], F32)
    ltm = nxt("ltm", [128, 16], F32)
    nltm = nxt("nltm", [128, 8], F32)
    btm = nxt("btm", [128, 64], BF16)
    b2_sb = nxt("b2_sb", [4, D], BF16)
    assert CUR[0] <= 16 * K, CUR[0]
    RI = 16 * K
    rowstage = sub("rowstage", [1, 2048], F32, RI)
    gbc_g = sub("gbc_g", [128, 1024], F32, RI + 8 * K)
    gbc_b = sub("gbc_b", [128, 1024], F32, RI + 12 * K)
    gbc = sub("gbc", [128, 2048], F32, RI + 8 * K)
    wvc = [sub(f"wvc{i}", [128, 1024], BF16, RI + i * 2048) for i in range(8)]
    masksT = sub("masksT", [4, 2048], BF16, RI)
    maskbc = sub("maskbc", [128, 2048], BF16, RI + 4 * K)
    hmsk = sub("hmsk", [128, 2048], BF16, RI + 8 * K)
    RH = 32 * K
    stg = [sub(f"stg{i}", [128, 1024], F32, RH + i * 4096) for i in range(4)]
    nbf = [sub(f"nbf{i}", [128, 1024], BF16, RH + i * 4096) for i in range(2)]
    wstp6 = [sub(f"wstp6{i}", [128, 1024], BF16, RH + 8 * K + i * 2048) for i in range(4)]
    R64 = 48 * K
    szall = sub("szall", [128, 8192], BF16, R64)
    o3 = R64 + 16 * K
    def nx(name, shape, dtype):
        nonlocal o3
        per = int(np.prod(shape[1:])) * mybir.dt.size(dtype)
        off = (o3 + 31) & ~31
        o3 = off + per
        return sub(name, shape, dtype, off)
    bT_b = nx("bT_b", [64, 512], BF16)
    cT_b = nx("cT_b", [64, 512], BF16)
    dts = nx("dts", [8, 512], F32)
    lts = nx("lts", [8, 512], F32)
    ldec = nx("ldec", [8, 512], F32)
    mpre = nx("mpre", [128, 1024], F32)
    mexp = nx("mexp", [128, 1024], BF16)
    gst = nx("gst", [128, 1024], BF16)
    rmsk = nx("rmsk", [8, 1024], F32)
    cpb = nx("cpb", [64, 1024], BF16)
    bph = nx("bph", [128, 8 * 64], BF16)
    hst = nx("hst", [64, 8 * 256], F32)
    hstb = nx("hstb", [64, 8 * 256], BF16)
    xs_bf = nx("xs_bf", [128, 2048], BF16)
    dtx = nx("dtx", [128, 2048], BF16)
    yt = nx("yt", [128, 2048], BF16)
    rmat = nx("rmat", [8, 128], F32)
    btbc_sb = nx("btbc_sb", [128, 128], F32)
    assert o3 <= R64 + 65 * K, o3 - R64
    vtm = [sub(f"vtm{i}", [128, 1040], BF16, R64 + i * 2080) for i in range(16)]
    qpair = [sub(f"qpair{i}", [128, 2048], BF16, R64 + 34 * K + i * 4096) for i in range(2)]
    kpair = [sub(f"kpair{i}", [128, 2048], BF16, R64 + 42 * K + i * 4096) for i in range(2)]
    woc = [sub(f"woc{i}", [128, 1024], BF16, R64 + 50 * K + i * 2048) for i in range(7)] \
        + [sub("woc7", [128, 1024], BF16, R64 + 34 * K)]
    hsb = [sub(f"hsb{m}", [128, 2048], BF16, R64 + m * 4096) for m in range(16)]
    R32A = 113 * K
    fm = [sub(f"fm{i}", [128, 2048], BF16, R32A + i * 4096) for i in range(8)]
    R32B = 145 * K
    rdb = [sub(f"rdb{i}", [128, 1024], BF16, R32B + i * 2048) for i in range(16)]
    attnT = [sub(f"attnT{i}", [128, 2048], BF16, R32B + i * 4096) for i in range(8)]
    R16 = 177 * K
    xsfm = [sub(f"xsfm{i}", [128, 512], BF16, R16 + i * 1024) for i in range(16)]
    wqk = [sub(f"wqk{i}", [128, 1024], BF16, R16 + i * 2048) for i in range(2)]
    ptt = [sub(f"ptt{i}", [128, 1024], BF16, R16 + 4 * K + i * 2048) for i in range(2)]
    ptt2 = [sub(f"ptt2_{i}", [128, 1024], BF16, R16 + 8 * K + i * 2048) for i in range(2)]
    biasrow = sub("biasrow", [1, 1024], F32, R16 + 8 * K)
    rc65 = sub("rc65", [65, 1024], F32, R16 + 12 * K)
    q2s8 = sub("q2s", [128, 4096], BF16, RI + 8 * K)
    keyrow = sub("keyrow", [1, 2048], F32, R16)
    t512a = sub("t512a", [1, 512], F32, R16 + 8 * K)
    t512b = sub("t512b", [1, 512], F32, R16 + 10 * K)
    t512i = sub("t512i", [1, 512], mybir.dt.int32, R16 + 12 * K)
    t512c = sub("t512c", [1, 512], F32, R16 + 14 * K)

    ctx = ExitStack()
    e = ctx.enter_context
    PZ = [e(nc.psum_tensor(f"PZ{i}", [128, 512], F32)) for i in range(2)]
    PX = [e(nc.psum_tensor(f"PX{i}", [128, 512], F32)) for i in range(2)]
    PBIG = e(nc.psum_tensor("PBIG", [128, 1024], F32))
    PS6 = e(nc.psum_tensor("PS6", [128, 512], F32))
    PY = e(nc.psum_tensor("PY", [128, 512], F32))

    def sub_ps(name, shape, dtype, base_handle):
        hnd = nc.virtual_psum_tensor(name, list(shape), dtype)
        nc._tensor(hnd.name, hnd.shape, hnd.dtype, type="PSUM")
        mloc = nc.lookup_mloc(hnd)
        bm = nc.lookup_mloc(base_handle)
        mloc.allocated = True
        mloc.addr = bm.addr
        mloc.bank = bm.bank
        mloc.base = bm.base
        return hnd


    sems = {k: e(nc.semaphore(f"sem_{k}")) for k in SEMS}
    P = Prog()
    _B = {}

    def buf(name):
        if name not in _B:
            _B[name] = Buf()
        return _B[name]

    bpz = [buf("pz0"), buf("pz1")]
    bpx = [buf("px0"), buf("px1")]
    bpb = buf("pbig")
    bp6 = buf("ps6")
    bpy = buf("py")

    bxn_s = [buf(f"xns{c}") for c in range(NCH)]
    bpart_s = [buf(f"parts{c}") for c in range(NCH)]
    bx1_s = [buf(f"x1s{c}") for c in range(NCH)]

    def dma(qe, out_ap, in_ap, reads=(), writes=(), **kw):
        return P.do(qe, lambda g, o=out_ap, i=in_ap, kw=dict(kw): g.dma_start(out=o, in_=i, **kw),
                    reads=reads, writes=writes)

    def mm(out_ap, lhsT_ap, rhs_ap, start, stop, reads=(), writes=()):
        return P.do("t", lambda t, o=out_ap, l=lhsT_ap, r=rhs_ap, a=start, b=stop:
                    t.matmul(o, l, r, start=a, stop=b),
                    reads=reads, writes=writes)

    def mm_raw(parts, reads=(), writes=()):
        fns = [(lambda t, o=o, l=l, r=r, s=s, st=st: t.matmul(o, l, r, start=s, stop=st))
               for (o, l, r, s, st) in parts]
        return P.do("t", fns, reads=reads, writes=writes)

    def act(out_ap, in_ap, func, bias=0.0, scale=1.0, accum=None, reads=(), writes=()):
        return P.do("s", lambda s, o=out_ap, i=in_ap, f=func, b=bias, sc=scale, a=accum:
                    s.activation(o, i, f, bias=b, scale=sc, accum_out=a),
                    reads=reads, writes=writes)

    def vop(fn, reads=(), writes=()):
        return P.do("v", fn, reads=reads, writes=writes)

    def trans(out_ap, in_ap, ident_ap, reads=(), writes=()):
        return P.do("t", lambda t, o=out_ap, i=in_ap, d=ident_ap:
                    t.matmul(o, i, d, start=True, stop=True),
                    reads=reads, writes=writes)

    cb = buf("consts")
    vb = buf("vecs")
    bstg = [buf(f"stg{i}") for i in range(4)]
    bfm = [buf(f"fm{i}") for i in range(8)]
    bgbc = buf("gbc")
    brow = buf("rowstage")
    bstat = buf("stat")

    dma("y", identf[:, :], c_identf, writes=(cb,))
    dma("y", identb[:, :], c_identb, writes=(cb,))
    dma("y", ones1[:, :], c_ones1, writes=(cb,))
    dma("y", ones8[:, :], c_ones8, writes=(cb,))
    dma("y", hsel[:, :], c_hsel, writes=(cb,))
    dma("g", hselb[:, :], c_hsel, writes=(cb,))
    dma("y", maskb8[:, :], c_maskb8, writes=(cb,))
    dma("y", mask8[:, :], c_mask8, writes=(cb,))
    dma("y", cvec4[:, :], c_cvec4, writes=(cb,))
    dma("y", ones65[:, :], c_ones65, writes=(cb,))
    dma("y", immt[:, :], c_imm, writes=(cb,))
    nc.const_aps.register(F32, EPS, immt[:, 0:1])
    nc.const_aps.register(F32, 1.0, immt[:, 1:2])
    dma("y", b_in_col[:, :], b_in[0:4224].rearrange("(m p) -> p m", p=128), writes=(vb,),
        allow_slow_non_contiguous=True)
    dma("y", tmp8a[:, :], b_in[4224:4232].rearrange("(a b) -> a b", b=1), writes=(vb,),
        allow_slow_non_contiguous=True)
    dma("y", comb8[:, :], dt_bias.rearrange("(a b) -> a b", b=1), writes=(vb,),
        allow_slow_non_contiguous=True)
    dma("y", acol[:, :], a_log.rearrange("(a b) -> a b", b=1), writes=(vb,),
        allow_slow_non_contiguous=True)
    dma("y", drow[:, :], d_skip.rearrange("(b a) -> b a", b=1), writes=(vb,))
    dma("y", bqk_col[:, 0:8], bq.rearrange("(m p) -> p m", p=128), writes=(vb,),
        allow_slow_non_contiguous=True)
    dma("y", bqk_col[:, 8:16], bk.rearrange("(m p) -> p m", p=128), writes=(vb,),
        allow_slow_non_contiguous=True)
    dma("y", rb_t[:, :], rb.rearrange("(a b) -> a b", b=1), writes=(vb,))
    dma("y", b1_col[:, :], b1.rearrange("e (m p) -> p (e m)", p=128), writes=(vb,),
        allow_slow_non_contiguous=True)
    dma("g", b2_sb[:, :], b2[:, :], writes=(vb,))
    vop(lambda v: v.tensor_add(comb8[:, :], comb8[:, :], tmp8a[:, :]), reads=(vb,), writes=(vb,))
    act(acol[:, :], acol[:, :], AF.Exp, reads=(vb,), writes=(vb,))
    vop(lambda v: v.tensor_scalar(acol[:, :], acol[:, :], -1.0, None, ALU.mult),
        reads=(vb,), writes=(vb,))
    mm(PS6[:, 0:8], ones1[0:1, :], drow[0:1, :], True, True, reads=(cb, vb), writes=(bp6,))
    vop(lambda v: v.tensor_copy(dbc[:, :], PS6[:, 0:8]), reads=(bp6,), writes=(vb,))
    for h in range(NHM):
        vop(lambda v, h=h: v.tensor_scalar(identD[:, 128 * h:128 * (h + 1)], identb[:, :],
                                           dbc[:, h:h + 1], None, ALU.mult),
            reads=(cb, vb), writes=(cb,))

    def build_bc(dst_g, dst_b, src_g, src_b_):
        for dst, src in ((dst_g, src_g), (dst_b, src_b_)):
            dma("y", rowstage[0:1, 0:D], src.rearrange("(b a) -> b a", b=1), writes=(brow,))
            for j in range(0, D, 512):
                mm(PBIG[:, j:j + 512], ones1[0:1, :], rowstage[0:1, j:j + 512], True, True,
                   reads=(cb, brow), writes=(bpb,))
            vop(lambda v, dst=dst: v.tensor_copy(dst[:, :], PBIG[:, :]),
                reads=(bpb,), writes=(bgbc,))

    sq_scr = stg[3]
    bsq = bstg[3]

    def ln_stats(src_ap, src_buf, so=0, neg_mu=False):
        act(sq_scr[:, :], src_ap, AF.Square, accum=stat[:, so + 2:so + 3], reads=(src_buf,),
            writes=(bsq, bstat))
        vop(lambda v, so=so: v.reduce_sum(stat[:, so + 3:so + 4], src_ap, AX.X),
            reads=(src_buf,), writes=(bstat,))
        vop(lambda v, so=so, sc=(-1.0 / D if neg_mu else 1.0 / D): v.tensor_scalar(
                stat[:, so:so + 1], stat[:, so + 3:so + 4], sc, None, ALU.mult),
            reads=(bstat,), writes=(bstat,))
        vop(lambda v, so=so: v.tensor_scalar(stat[:, so + 2:so + 3], stat[:, so + 2:so + 3],
                                             1.0 / D, None, ALU.mult),
            reads=(bstat,), writes=(bstat,))
        vop(lambda v, so=so: v.tensor_mul(stat[:, so + 4:so + 5], stat[:, so:so + 1],
                                          stat[:, so:so + 1]),
            reads=(bstat,), writes=(bstat,))
        vop(lambda v, so=so: v.tensor_sub(stat[:, so + 4:so + 5], stat[:, so + 2:so + 3],
                                          stat[:, so + 4:so + 5]),
            reads=(bstat,), writes=(bstat,))
        act(stat[:, so + 4:so + 5], stat[:, so + 4:so + 5], AF.Sqrt, bias=EPS,
            reads=(bstat,), writes=(bstat,))
        vop(lambda v, so=so: v.reciprocal(stat[:, so + 1:so + 2], stat[:, so + 4:so + 5]),
            reads=(bstat,), writes=(bstat,))

    def fm_transposes(src_bf_ap, src_buf, c):
        for j in range(8):
            pt = PS6[:, 0:128] if j % 2 == 0 else PY[:, 0:128]
            tb = bp6 if j % 2 == 0 else bpy
            trans(pt, bass.AP(src_bf_ap, 128 * j, [[1024, 128], [1, 128]]),
                  identb[:, :], reads=(src_buf, cb), writes=(tb,))
            if j % 2 == 0:
                vop(lambda v, pt=pt, j=j, c=c: v.tensor_copy(
                        fm[j][:, 128 * c:128 * (c + 1)], pt),
                    reads=(tb,), writes=(bfm[j],))
            else:
                P.do("s", lambda s, pt=pt, j=j, c=c: s.copy(
                        fm[j][:, 128 * c:128 * (c + 1)], pt),
                     reads=(tb,), writes=(bfm[j],))

    build_bc(gbc_g, gbc_b, ln_g, ln_b)
    sn2 = sub("sn2", [128, 1024], F32, R64 + 20 * K)
    bsn2 = buf("sn2")

    def ln_stats_pair(AB):
        for fn in (lambda src, sb, so: act(sq_scr[:, :], src, AF.Square,
                                           accum=stat[:, so + 2:so + 3],
                                           reads=(sb,), writes=(bsq, bstat)),
                   lambda src, sb, so: vop(lambda v, src=src, so=so: v.reduce_sum(
                                               stat[:, so + 3:so + 4], src, AX.X),
                                           reads=(sb,), writes=(bstat,)),
                   lambda src, sb, so: vop(lambda v, so=so: v.tensor_scalar(
                                               stat[:, so:so + 1], stat[:, so + 3:so + 4],
                                               -1.0 / D, None, ALU.mult),
                                           reads=(bstat,), writes=(bstat,)),
                   lambda src, sb, so: vop(lambda v, so=so: v.tensor_scalar(
                                               stat[:, so + 2:so + 3], stat[:, so + 2:so + 3],
                                               1.0 / D, None, ALU.mult),
                                           reads=(bstat,), writes=(bstat,)),
                   lambda src, sb, so: vop(lambda v, so=so: v.tensor_mul(
                                               stat[:, so + 4:so + 5], stat[:, so:so + 1],
                                               stat[:, so:so + 1]),
                                           reads=(bstat,), writes=(bstat,)),
                   lambda src, sb, so: vop(lambda v, so=so: v.tensor_sub(
                                               stat[:, so + 4:so + 5], stat[:, so + 2:so + 3],
                                               stat[:, so + 4:so + 5]),
                                           reads=(bstat,), writes=(bstat,)),
                   lambda src, sb, so: act(stat[:, so + 4:so + 5], stat[:, so + 4:so + 5],
                                           AF.Sqrt, bias=EPS, reads=(bstat,), writes=(bstat,)),
                   lambda src, sb, so: vop(lambda v, so=so: v.reciprocal(
                                               stat[:, so + 1:so + 2], stat[:, so + 4:so + 5]),
                                           reads=(bstat,), writes=(bstat,)),
                   lambda src, sb, so: vop(lambda v, so=so: v.tensor_mul(
                                               stat[:, so + 5:so + 6], stat[:, so:so + 1],
                                               stat[:, so + 1:so + 2]),
                                           reads=(bstat,), writes=(bstat,))):
            for (src, sb, so) in AB:
                fn(src, sb, so)

    for cp in range(0, NCH if KPH >= 1 else 0, 2):
        cA, cB = cp, cp + 1
        sxA, bxA, sxB, bxB = stg[0], bstg[0], stg[1], bstg[1]
        snA, bnA, snB, bnB = stg[2], bstg[2], sn2, bsn2
        dma("y", sxA[:, :], x_in[128 * cA:128 * (cA + 1), :], writes=(bxA,))
        dma("y", sxB[:, :], x_in[128 * cB:128 * (cB + 1), :], writes=(bxB,))
        ln_stats_pair([(sxA[:, :], bxA, 0), (sxB[:, :], bxB, 8)])
        act(snA[:, :], sxA[:, :], AF.Identity, scale=stat[:, 1:2],
            bias=stat[:, 5:6], reads=(bxA, bstat), writes=(bnA,))
        act(snB[:, :], sxB[:, :], AF.Identity, scale=stat[:, 9:10],
            bias=stat[:, 13:14], reads=(bxB, bstat), writes=(bnB,))
        dma("y", xn_scr[128 * cA:128 * (cA + 1), :], snA[:, :], reads=(bnA,),
            writes=(bxn_s[cA],))
        dma("y", xn_scr[128 * cB:128 * (cB + 1), :], snB[:, :], reads=(bnB,),
            writes=(bxn_s[cB],))
        ln_stats_pair([(snA[:, :], bnA, 0), (snB[:, :], bnB, 8)])
        act(snA[:, :], snA[:, :], AF.Identity, scale=stat[:, 1:2],
            bias=stat[:, 5:6], reads=(bnA, bstat), writes=(bnA,))
        act(snB[:, :], snB[:, :], AF.Identity, scale=stat[:, 9:10],
            bias=stat[:, 13:14], reads=(bnB, bstat), writes=(bnB,))
        vop(lambda v: v.tensor_mul(snA[:, :], snA[:, :], gbc_g[:, :]),
            reads=(bnA, bgbc), writes=(bnA,))
        vop(lambda v: v.tensor_mul(snB[:, :], snB[:, :], gbc_g[:, :]),
            reads=(bnB, bgbc), writes=(bnB,))
        vop(lambda v: v.tensor_add(nbf[0][:, :], snA[:, :], gbc_b[:, :]),
            reads=(bnA, bgbc), writes=(bstg[0],))
        vop(lambda v: v.tensor_add(nbf[1][:, :], snB[:, :], gbc_b[:, :]),
            reads=(bnB, bgbc), writes=(bstg[1],))
        fm_transposes(nbf[0], bstg[0], cA)
        fm_transposes(nbf[1], bstg[1], cB)

    P.barrier()

    dma("y", rowstage[0:1, 0:1024], b_out.rearrange("(b a) -> b a", b=1), writes=(brow,))

    bhst = buf("hst")
    bhstb = [buf(f"hstb{h}") for h in range(8)]
    blpc = buf("lpc")
    blc = buf("lcarry")
    vop(lambda v: v.memset(hst[:, :], 0.0), writes=(bhst,))
    vop(lambda v: v.memset(hstb[:, :], 0.0), writes=tuple(bhstb))
    vop(lambda v: v.memset(lprevc[:, :], 0.0), writes=(blpc,))
    vop(lambda v: v.memset(lprevbc[:, :], 0.0), writes=(blpc,))
    vop(lambda v: v.memset(lcarry[:, :], 0.0), writes=(blc,))

    MODE_B = [33, 32] + list(range(16, 32)) + list(range(16))

    brdb = [buf(f"rdb{i}") for i in range(16)]
    bxsf = [buf(f"xsfm{i}") for i in range(16)]
    bsz = [buf(f"siluz{m}") for m in range(16)]
    bbct = buf("bct")
    bdtl = buf("dtl")
    bldec = buf("ldec")
    bltm = buf("ltm")
    blend = buf("lend")
    bmpre = buf("mpre")
    bnltm = buf("nltm")
    bbtbc = buf("btbc")
    bmexp = buf("mexp")
    bgst = buf("gst")
    brm = buf("rmat")
    brmsk = buf("rmsk")
    bcpb = buf("cpb")
    bbtm = buf("btm")
    bbph = buf("bph")
    bxsb = buf("xs_bf")
    bdtx = buf("dtx")
    byt = [buf(f"yt{j}") for j in range(16)]
    bwm = buf("wdec")

    for j in range(12 if KPH >= 2 else 0):
        dma("g", rdb[j][:, :], w_out_d[128 * j:128 * (j + 1), :], writes=(brdb[j],))

    def emit_wout(c):
        for j in range(16):
            mm_raw([(PY[:, :], yt[:, 128 * j:128 * (j + 1)],
                     rdb[j][:, 0:512], j == 0, False),
                    (PX[0][:, :], yt[:, 128 * j:128 * (j + 1)],
                     rdb[j][:, 512:1024], j == 0, False)],
                   reads=(byt[j], brdb[j]), writes=(bpy, bpx[0]))
        mm_raw([(PY[:, :], ones1[0:1, :], rowstage[0:1, 0:512], False, True),
                (PX[0][:, :], ones1[0:1, :], rowstage[0:1, 512:1024], False, True)],
               reads=(brow, cb), writes=(bpy, bpx[0]))
        dma("y", stg[0][:, :], xn_scr[128 * c:128 * (c + 1), :], reads=(bxn_s[c],),
            writes=(bstg[0],))
        dma("y", stg[1][:, :], x_in[128 * c:128 * (c + 1), :], writes=(bstg[1],))
        vop(lambda v: v.scalar_tensor_tensor(stg[2][:, 0:512], PY[:, :], 1.0,
                                             stg[0][:, 0:512], ALU.mult, ALU.add),
            reads=(bpy, bstg[0]), writes=(bstg[2],))
        vop(lambda v: v.scalar_tensor_tensor(stg[2][:, 512:1024], PX[0][:, :], 1.0,
                                             stg[0][:, 512:1024], ALU.mult, ALU.add),
            reads=(bpx[0], bstg[0]), writes=(bstg[2],))
        vop(lambda v: v.tensor_add(stg[2][:, :], stg[2][:, :], stg[1][:, :]),
            reads=(bstg[2], bstg[1]), writes=(bstg[2],))
        dma("y", part_scr[128 * c:128 * (c + 1), :], stg[2][:, :], reads=(bstg[2],),
            writes=(bpart_s[c],))

    def emit_chain(c, mid=None):
        ct0 = (c % CPS) * 128
        for g in range(4):
            pz, pzb = PZ[g % 2], bpz[g % 2]
            for k in range(4):
                j = 4 * g + k
                trans(pz[:, 128 * k:128 * (k + 1)], xsfm[j][:, ct0:ct0 + 128],
                      identb[:, :], reads=(bxsf[j], cb), writes=(pzb,))
            if g % 2 == 0:
                vop(lambda v, pz=pz, g=g: v.tensor_copy(
                        xs_bf[:, 512 * g:512 * (g + 1)], pz[:, 0:512]),
                    reads=(pzb,), writes=(bxsb,))
            else:
                P.do("s", lambda s, pz=pz, g=g: s.copy(
                        xs_bf[:, 512 * g:512 * (g + 1)], pz[:, 0:512]),
                     reads=(pzb,), writes=(bxsb,))
        mm(PZ[0][:, 160:288], bT_b[:, ct0:ct0 + 128], cT_b[:, ct0:ct0 + 128], True, True,
           reads=(bbct,), writes=(bpz[0],))
        trans(PZ[0][:, 288:352], bT_b[:, ct0:ct0 + 128], identb[0:64, 0:64],
              reads=(bbct, cb), writes=(bpz[0],))
        vop(lambda v: v.tensor_copy(btbc_sb[:, :], PZ[0][:, 160:288]),
            reads=(bpz[0],), writes=(bbtbc,))
        vop(lambda v: v.tensor_copy(btm[:, :], PZ[0][:, 288:352]),
            reads=(bpz[0],), writes=(bbtm,))
        trans(PZ[0][:, 128:136], dts[:, ct0:ct0 + 128], identf[0:8, 0:8],
              reads=(bdtl, cb), writes=(bpz[0],))
        trans(PZ[0][:, 136:144], lts[:, ct0:ct0 + 128], identf[0:8, 0:8],
              reads=(bdtl, cb), writes=(bpz[0],))
        vop(lambda v: v.tensor_copy(ltm[:, :], PZ[0][:, 128:144]),
            reads=(bpz[0],), writes=(bltm,))
        trans(PZ[0][0:1, 144:152], lts[:, ct0 + 127:ct0 + 128], identf[0:8, 0:8],
              reads=(bdtl, cb), writes=(bpz[0],))
        vop(lambda v: v.tensor_copy(lendrow[:, :], PZ[0][0:1, 144:152]),
            reads=(bpz[0],), writes=(blend,))
        mm(PZ[0][:, 152:160], ones1[0:1, :], lendrow[0:1, :], True, True,
           reads=(cb, blend), writes=(bpz[0],))
        vop(lambda v: v.tensor_sub(wtm[:, :], PZ[0][:, 152:160], ltm[:, 8:16]),
            reads=(bpz[0], bltm), writes=(bwm,))
        act(wtm[:, :], wtm[:, :], AF.Exp, reads=(bwm,), writes=(bwm,))
        vop(lambda v: v.tensor_sub(decbc[:, :], PZ[0][:, 152:160], lprevbc[:, :]),
            reads=(bpz[0], blpc), writes=(bwm,))
        act(decbc[:, :], decbc[:, :], AF.Exp, reads=(bwm,), writes=(bwm,))
        vop(lambda v, ct0=ct0: v.tensor_scalar(rmat[:, :], lts[:, ct0:ct0 + 128],
                                               lprevc[:, :], None, ALU.subtract),
            reads=(bdtl, blpc), writes=(brm,))
        act(rmat[:, :], rmat[:, :], AF.Exp, reads=(brm,), writes=(brm,))
        vop(lambda v, ct0=ct0: v.tensor_copy(lprevc[:, :], lts[:, ct0 + 127:ct0 + 128]),
            reads=(bdtl,), writes=(blpc,))
        vop(lambda v: v.tensor_copy(lprevbc[:, :], PZ[0][:, 152:160]),
            reads=(bpz[0],), writes=(blpc,))
        vop(lambda v, ct0=ct0: v.tensor_mul(
                rmsk[:, :], bass.AP(lts, ct0, [[512, 8], [0, 8], [1, 128]]),
                mask8[:, :]),
            reads=(bdtl, cb), writes=(brmsk,))
        act(nltm[:, :], ltm[:, 8:16], AF.Identity, scale=-1.0,
            reads=(bltm,), writes=(bnltm,))
        for half in range(2):
            mm(PZ[1][:, 0:512], ones8[:, :], rmsk[:, 512 * half:512 * (half + 1)],
               True, True, reads=(cb, brmsk), writes=(bpz[1],))
            for hh in range(4):
                h = 4 * half + hh
                act(mpre[:, 128 * h:128 * (h + 1)], PZ[1][:, 128 * hh:128 * (hh + 1)],
                    AF.Identity, bias=nltm[:, h:h + 1],
                    reads=(bpz[1], bnltm), writes=(bmpre,))
        vop(lambda v: v.tensor_add(mpre[:, :], mpre[:, :], maskb8[:, :]),
            reads=(bmpre, cb), writes=(bmpre,))
        act(mexp[:, :], mpre[:, :], AF.Exp, reads=(bmpre,), writes=(bmexp,))
        vop(lambda v: v.tensor_mul(
                bass.AP(gst, 0, [[1024, 128], [128, 8], [1, 128]]),
                bass.AP(mexp, 0, [[1024, 128], [128, 8], [1, 128]]),
                bass.AP(btbc_sb, 0, [[128, 128], [0, 8], [1, 128]])),
            reads=(bmexp, bbtbc), writes=(bgst,))
        for h in range(NHM):
            act(bph[:, 64 * h:64 * (h + 1)], btm[:, :], AF.Identity,
                scale=wtm[:, h:h + 1], reads=(bbtm, bwm), writes=(bbph,))
        vop(lambda v: v.tensor_mul(rmsk[:, :], bass.AP(rmat, 0, [[128, 8], [0, 8], [1, 128]]),
                                   mask8[:, :]),
            reads=(brm, cb), writes=(brmsk,))
        if mid is not None:
            mid()
        for half in range(2):
            mm(PZ[1][0:64, 0:512], ones8[:, 0:64], rmsk[:, 512 * half:512 * (half + 1)],
               True, True, reads=(cb, brmsk), writes=(bpz[1],))
            vop(lambda v, half=half, ct0=ct0: v.tensor_mul(
                    bass.AP(cpb, 512 * half, [[1024, 64], [128, 4], [1, 128]]),
                    bass.AP(cT_b, ct0, [[512, 64], [0, 4], [1, 128]]),
                    bass.AP(PZ[1], 0, [[512, 64], [128, 4], [1, 128]])),
                reads=(bbct, bpz[1]), writes=(bcpb,))
        for h in range(NHM):
            act(dtx[:, 256 * h:256 * (h + 1)], xs_bf[:, 256 * h:256 * (h + 1)],
                AF.Identity, scale=ltm[:, h:h + 1],
                reads=(bxsb, bltm), writes=(bdtx,))

    YSLOT = [(PBIG, 0, bpb), (PBIG, 256, bpb), (PBIG, 512, bpb), (PBIG, 768, bpb),
             (PY, 0, bpy), (PY, 256, bpy), (PX[0], 0, bpx[0]), (PX[0], 256, bpx[0])]

    def emit_y(c):
        ct0 = (c % CPS) * 128
        for h in range(NHM):
            ypp, yc0, ypb = YSLOT[h]
            parts = []
            for q in range(2):
                o = ypp[:, yc0 + 128 * q:yc0 + 128 * (q + 1)]
                parts.append((o, dtx[:, 256 * h + 128 * q:256 * h + 128 * (q + 1)],
                              gst[:, 128 * h:128 * (h + 1)], q == 0, False))
            for q in range(2):
                o = ypp[:, yc0 + 128 * q:yc0 + 128 * (q + 1)]
                parts.append((o, xs_bf[:, 256 * h + 128 * q:256 * h + 128 * (q + 1)],
                              identD[:, 128 * h:128 * (h + 1)], False, False))
            for q in range(2):
                o = ypp[:, yc0 + 128 * q:yc0 + 128 * (q + 1)]
                parts.append((o, bass.AP(hstb, 256 * h + 128 * q, [[2048, 64], [1, 128]]),
                              cpb[:, 128 * h:128 * (h + 1)], False, q == 1))
            mm_raw(parts, reads=[bdtx, bgst, bxsb, cb, bhstb[h], bcpb], writes=(ypb,))
            hps = PS6[0:64, 0:256] if h % 2 == 0 else PX[1][0:64, 0:256]
            hpb = bp6 if h % 2 == 0 else bpx[1]
            mm(hps, bph[:, 64 * h:64 * (h + 1)], dtx[:, 256 * h:256 * (h + 1)],
               True, True, reads=(bbph, bdtx), writes=(hpb,))
            vop(lambda v, h=h, hps=hps: v.scalar_tensor_tensor(
                    bass.AP(hst, 256 * h, [[2048, 64], [1, 256]]),
                    bass.AP(hst, 256 * h, [[2048, 64], [1, 256]]),
                    decbc[0:64, h:h + 1], hps, ALU.mult, ALU.add),
                reads=(hpb, bwm, bhst), writes=(bhst,))
            vop(lambda v, h=h: v.tensor_copy(
                    bass.AP(hstb, 256 * h, [[2048, 64], [1, 256]]),
                    bass.AP(hst, 256 * h, [[2048, 64], [1, 256]])),
                reads=(bhst,), writes=(bhstb[h],))
        for (ypp, ypb, m0, nt) in ((PBIG, bpb, 0, 8), (PY, bpy, 8, 4), (PX[0], bpx[0], 12, 4)):
            vop(lambda v, ypp=ypp, m0=m0, nt=nt, ct0=ct0: v.tensor_mul(
                    bass.AP(yt, 128 * m0, [[2048, 128], [128, nt], [1, 128]]),
                    bass.AP(ypp, 0, [[128 * nt, 128], [128, nt], [1, 128]]),
                    bass.AP(szall, 512 * m0 + ct0, [[8192, 128], [512, nt], [1, 128]])),
                reads=tuple([ypb] + [bsz[m0 + i] for i in range(nt)]),
                writes=tuple(byt[m0 + i] for i in range(nt)))

    for sc in range(NSC if KPH >= 2 else 0):
        t0 = sc * SCLEN
        for mi, m in enumerate(MODE_B):
            wr, wbr = rdb[12 + mi % 4], brdb[12 + mi % 4]
            cw = 128 if m < 33 else 8
            dma("g", bass.AP(wr, 0, [[1024, 128], [128, 8], [1, cw]]),
                w_in[:, 128 * m:128 * m + cw].rearrange("(c p) n -> p c n", p=128),
                writes=(wbr,))
            pz, pzb = PZ[mi % 2], bpz[mi % 2]
            parts = []
            for fi in range(8):
                parts.append((pz[0:cw, 0:512],
                              bass.AP(wr, 128 * fi, [[1024, 128], [1, cw]]),
                              fm[fi][:, t0:t0 + 512], fi == 0, fi == 7))
            mm_raw(parts, reads=[wbr] + bfm, writes=(pzb,))
            if m < 16:
                act(bass.AP(szall, 512 * m, [[8192, 128], [1, 512]]), pz[:, :],
                    AF.Silu, bias=b_in_col[:, m:m + 1],
                    reads=(pzb, vb), writes=(bsz[m],))
            elif m < 32:
                act(xsfm[m - 16][:, :], pz[:, :], AF.Identity, bias=b_in_col[:, m:m + 1],
                    reads=(pzb, vb), writes=(bxsf[m - 16],))
            elif m == 32:
                act(bT_b[:, :], pz[0:64, :], AF.Identity, bias=b_in_col[0:64, 32:33],
                    reads=(pzb, vb), writes=(bbct,))
                act(cT_b[:, :], pz[64:128, :], AF.Identity, bias=b_in_col[64:128, 32:33],
                    reads=(pzb, vb), writes=(bbct,))
            else:
                act(dts[:, :], pz[0:8, :], AF.Identity, bias=comb8[:, :],
                    reads=(pzb, vb), writes=(bdtl,))
            if mi == 1:
                act(ldec[:, :], dts[:, :], AF.Abs, reads=(bdtl,), writes=(bldec,))
                act(ldec[:, :], ldec[:, :], AF.Exp, scale=-1.0, reads=(bldec,), writes=(bldec,))
                act(ldec[:, :], ldec[:, :], AF.Ln, bias=1.0, reads=(bldec,), writes=(bldec,))
                vop(lambda v: v.tensor_scalar(rmsk[0:8, 0:512], dts[:, :], 0.0, None, ALU.max),
                    reads=(bdtl,), writes=(brmsk,))
                vop(lambda v: v.tensor_add(dts[:, :], ldec[:, :], rmsk[0:8, 0:512]),
                    reads=(bldec, brmsk), writes=(bdtl,))
                vop(lambda v: v.tensor_scalar(ldec[:, :], dts[:, :], acol[:, :], None, ALU.mult),
                    reads=(bdtl, vb), writes=(bldec,))
                vop(lambda v: v.memset(rmsk[0:8, 0:512], 0.0), writes=(brmsk,))
                vop(lambda v: v.tensor_tensor_scan(lts[:, :], ldec[:, :], rmsk[0:8, 0:512],
                                                   lcarry[:, :], ALU.add, ALU.add),
                    reads=(bldec, brmsk, blc), writes=(bdtl,))
                vop(lambda v: v.tensor_copy(lcarry[:, :], lts[:, 511:512]),
                    reads=(bdtl,), writes=(blc,))
        for j in (12, 13, 14, 15):
            dma("g", rdb[j][:, :], w_out_d[128 * j:128 * (j + 1), :], writes=(brdb[j],))
        emit_chain(sc * CPS)

        for tc in range(CPS):
            c = sc * CPS + tc
            emit_y(c)
            if tc < CPS - 1:
                emit_chain(c + 1, mid=lambda c=c: emit_wout(c))
            else:
                emit_wout(c)

    P.barrier()

    for c in range(NCH if KPH >= 3 else 0):
        nb, bnb = nbf[c % 2], bstg[c % 2]
        dma("g", nb[:, :], xn_scr[128 * c:128 * (c + 1), :], reads=(bxn_s[c],), writes=(bnb,))
        fm_transposes(nb, bnb, c)

    bwvc = [buf(f"wvc{i}") for i in range(8)]
    bvtm = [buf(f"vtm{i}") for i in range(16)]
    bbias = buf("biasrow")
    for j in range(8 if KPH >= 3 else 0):
        dma("g", wvc[j][:, :], wv[128 * j:128 * (j + 1), :], writes=(bwvc[j],))
    dma("y", biasrow[0:1, :], bv.rearrange("(b a) -> b a", b=1), writes=(bbias,))
    for tt in range(16 if KPH >= 3 else 0):
        vop(lambda v, tt=tt: v.memset(bass.AP(vtm[tt], 64, [[1040, 128], [65, 16]]), 1.0),
            writes=(bvtm[tt],))
        parts = []
        for fi in range(8):
            for nh in range(2):
                parts.append((PBIG[:, 512 * nh:512 * (nh + 1)],
                              fm[fi][:, 128 * tt:128 * (tt + 1)],
                              wvc[fi][:, 512 * nh:512 * (nh + 1)], fi == 0, False))
        for nh in range(2):
            parts.append((PBIG[:, 512 * nh:512 * (nh + 1)], ones1[0:1, :],
                          biasrow[0:1, 512 * nh:512 * (nh + 1)], False, True))
        mm_raw(parts, reads=[cb, bbias] + bfm + bwvc, writes=(bpb,))
        vop(lambda v, tt=tt: v.tensor_copy(
                bass.AP(vtm[tt], 0, [[1040, 128], [65, 16], [1, 64]]), PBIG[:, :]),
            reads=(bpb,), writes=(bvtm[tt],))

    _bq2 = buf("q2s")
    for j in range(4, 8):
        wb_ = buf(f"wvc{j}")
        if wb_.w is not None:
            _bq2.r[wb_.w[0]] = max(_bq2.r.get(wb_.w[0], 0), wb_.w[1])
        for pe_, v_ in wb_.r.items():
            _bq2.r[pe_] = max(_bq2.r.get(pe_, 0), v_)

    bqp = [buf("qp0"), buf("qp1")]
    bkp = [buf("kp0"), buf("kp1")]
    bwqk = [buf("bwqk0"), buf("bwqk1")]
    batnT = [buf(f"atnT{i}") for i in range(8)]
    bq2 = buf("q2s")
    bnegm = buf("negm")
    bnrow = buf("nrow")
    bptt = [buf("ptt0"), buf("ptt1")]
    bptt2 = [buf("ptt2_0"), buf("ptt2_1")]
    for _b2 in bptt2:
        wb_ = buf("biasrow")
        if wb_.w is not None:
            _b2.r[wb_.w[0]] = max(_b2.r.get(wb_.w[0], 0), wb_.w[1])
        for pe_, v_ in wb_.r.items():
            _b2.r[pe_] = max(_b2.r.get(pe_, 0), v_)
    brc = buf("rc65")

    pend_norm = [None]

    def flush_norm():
        if pend_norm[0] is not None:
            pend_norm[0]()
            pend_norm[0] = None

    for p in range(NPAIR if KPH >= 3 else 0):
        qp, kp = qpair[p % 2], kpair[p % 2]
        bq_, bk_ = bqp[p % 2], bkp[p % 2]
        for (wdram, stg_t, stg_b) in ((wq, wqk[0], bwqk[0]), (wk, wqk[1], bwqk[1])):
            dma("g", bass.AP(stg_t, 0, [[1024, 128], [128, 8], [1, 128]]),
                wdram[:, 128 * p:128 * (p + 1)].rearrange("(c p) n -> p c n", p=128),
                writes=(stg_b,))
        for (wstg, bwstg, dst, bdst, bcol) in ((wqk[0], bwqk[0], qp, bq_, 0),
                                               (wqk[1], bwqk[1], kp, bk_, 8)):
            for ns in range(4):
                pz = PBIG[:, 512 * (ns % 2):512 * (ns % 2 + 1)]
                pzb = bpb
                parts = []
                for fi in range(8):
                    parts.append((pz, bass.AP(wstg, 128 * fi, [[1024, 128], [1, 128]]),
                                  fm[fi][:, 512 * ns:512 * (ns + 1)], fi == 0, fi == 7))
                mm_raw(parts, reads=[bwstg] + bfm, writes=(pzb,))
                act(dst[:, 512 * ns:512 * (ns + 1)], pz, AF.Identity,
                    bias=bqk_col[:, bcol + p:bcol + p + 1], reads=(pzb, vb), writes=(bdst,))
                if ns == 0 and wstg is wqk[0]:
                    flush_norm()
        for which, (src, bsrc) in enumerate(((qp, bq_), (kp, bk_))):
            for ns in range(4):
                s = 4 * which + ns
                act(q2s8[:, 512 * s:512 * (s + 1)], src[:, 512 * ns:512 * (ns + 1)],
                    AF.Square, reads=(bsrc,), writes=(bq2,))
        for which in range(2):
            for ns in range(4):
                s = 4 * which + ns
                mbank, mb = (PS6, bp6) if s % 2 == 0 else (PY, bpy)
                mm(mbank[0:2, 0:512], hselb[:, :], q2s8[:, 512 * s:512 * (s + 1)],
                   True, True, reads=(cb, bq2), writes=(mb,))
                if ns == 0:
                    vop(lambda v, which=which, mbank=mbank: v.reduce_max(
                            negm[0:2, which:which + 1], mbank[0:2, 0:512], AX.X),
                        reads=(mb,), writes=(bnegm,))
                else:
                    vop(lambda v, which=which, mbank=mbank: v.reduce_max(
                            stat[0:2, 6 + which:7 + which], mbank[0:2, 0:512], AX.X),
                        reads=(mb,), writes=(bstat,))
                    vop(lambda v, which=which: v.tensor_max(negm[0:2, which:which + 1],
                                                            negm[0:2, which:which + 1],
                                                            stat[0:2, 6 + which:7 + which]),
                        reads=(bnegm, bstat), writes=(bnegm,))
        vop(lambda v: v.tensor_mul(negm[0:2, 0:1], negm[0:2, 0:1], negm[0:2, 1:2]),
            reads=(bnegm,), writes=(bnegm,))
        act(negm[0:2, 0:1], negm[0:2, 0:1], AF.Sqrt, scale=1.0 / 64.0,
            reads=(bnegm,), writes=(bnegm,))
        vop(lambda v: v.tensor_scalar(negm[0:2, 0:1], negm[0:2, 0:1], -1.0, None, ALU.mult),
            reads=(bnegm,), writes=(bnegm,))
        trans(PS6[0:1, 24:26], negm[0:2, 0:1], identf[0:2, 0:2], reads=(bnegm, cb), writes=(bp6,))
        vop(lambda v: v.tensor_copy(nrow[:, :], PS6[0:1, 24:26]), reads=(bp6,), writes=(bnrow,))
        mm(PS6[:, 26:28], ones1[0:1, :], nrow[0:1, :], True, True, reads=(cb, bnrow), writes=(bp6,))
        vop(lambda v: v.tensor_copy(negm[:, :], PS6[:, 26:28]), reads=(bp6,), writes=(bnegm,))

        for qq in range(4):
            q0 = 512 * qq

            def pt_slot(kt, head):
                base = ptt if kt % 4 < 2 else ptt2
                sl = 512 * (kt % 2)
                return base[head][:, sl:sl + 512], (bptt if kt % 4 < 2 else bptt2)[head]

            def st_emit(kt):
                par = kt % 2
                pa, ba = pt_slot(kt, 0)
                pb, bb = pt_slot(kt, 1)
                mm_raw([(PZ[par][:, :], kp[0:64, 128 * kt:128 * (kt + 1)],
                         qp[0:64, q0:q0 + 512], True, True),
                        (PX[par][:, :], kp[64:128, 128 * kt:128 * (kt + 1)],
                         qp[64:128, q0:q0 + 512], True, True)],
                       reads=(bq_, bk_), writes=(bpz[par], bpx[par]))
                act(pa, PZ[par][:, :], AF.Exp, bias=negm[:, 0:1], scale=0.125,
                    reads=(bpz[par], bnegm), writes=(ba,))
                act(pb, PX[par][:, :], AF.Exp, bias=negm[:, 1:2], scale=0.125,
                    reads=(bpx[par], bnegm), writes=(bb,))

            if qq % 2 == 1:
                AVB = [(PS6[0:65, :], PS6[64:65, :], PS6[64:128, :], PS6[0:64, :], bp6),
                       (PY[0:65, :], PY[64:65, :], PY[64:128, :], PY[0:64, :], bpy)]
            else:
                AVB = [(PBIG[0:65, 0:512], PBIG[64:65, 0:512], PBIG[64:128, 0:512],
                        PBIG[0:64, 0:512], bpb),
                       (PBIG[0:65, 512:1024], PBIG[64:65, 512:1024],
                        PBIG[64:128, 512:1024], PBIG[0:64, 512:1024], bpb)]

            def av_emit(kt, AVB=AVB):
                pa, ba = pt_slot(kt, 0)
                pb, bb = pt_slot(kt, 1)
                mm_raw([(AVB[0][0], vtm[kt][:, 65 * (2 * p):65 * (2 * p) + 65],
                         pa, kt == 0, kt == 15),
                        (AVB[1][0], vtm[kt][:, 65 * (2 * p + 1):65 * (2 * p + 1) + 65],
                         pb, kt == 0, kt == 15)],
                       reads=(bvtm[kt], ba, bb), writes=(AVB[0][4], AVB[1][4]))

            st_emit(0)
            st_emit(1)
            for kt in range(2, 16):
                st_emit(kt)
                av_emit(kt - 2)
                if kt == 7:
                    flush_norm()
            av_emit(14)
            av_emit(15)

            rcsl2 = [rc65[64:65, 512 * ((qq + hh) % 2):512 * ((qq + hh) % 2) + 512]
                     for hh in range(2)]
            for hh, (avfull, avsum, avbc, avdat, avb) in enumerate(AVB):
                vop(lambda v, avsum=avsum, rcsl=rcsl2[hh]: v.reciprocal(rcsl, avsum),
                    reads=(avb,), writes=(brc,))

            def norm_emit(AVB=AVB, qq=qq, p=p, q0=q0, rcsl2=rcsl2):
                rcbc = rc65[0:64, 512 * (qq % 2):512 * (qq % 2) + 512]
                for hh, (avfull, avsum, avbc, avdat, avb) in enumerate(AVB):
                    mm_raw([(avbc, ones65[64:65, 0:64], rcsl2[hh], True, True)],
                           reads=(brc, cb), writes=(avb,))
                    vop(lambda v, avbc=avbc, rcbc=rcbc: v.tensor_copy(rcbc, avbc),
                        reads=(avb,), writes=(brc,))
                    vop(lambda v, hh=hh, p=p, q0=q0, avdat=avdat, rcbc=rcbc: v.tensor_mul(
                            attnT[p][64 * hh:64 * (hh + 1), q0:q0 + 512],
                            avdat, rcbc),
                        reads=(avb, brc), writes=(batnT[p],))

            pend_norm[0] = norm_emit

    flush_norm()
    P.barrier()

    rwbc = sub("rwbc", [128, 1024], F32, R16 + 12 * K)
    brwbc = buf("rwbc")
    for j in range(8 if KPH >= 4 else 0):
        dma("g", woc[j][:, :], wo[128 * j:128 * (j + 1), :], writes=(buf(f"woc{j}"),))
    bwoc = [buf(f"woc{j}") for j in range(8)]
    dma("y", biasrow[0:1, :], bo.rearrange("(b a) -> b a", b=1), writes=(bbias,))
    build_bc(gbc_g, gbc_b, l1g, l1b)
    if KPH >= 4:
        dma("y", rowstage[0:1, 0:1024], rw.rearrange("(b a) 1 -> b a", b=1), writes=(brow,))
        for j in range(0, D, 512):
            mm(PBIG[:, j:j + 512], ones1[0:1, :], rowstage[0:1, j:j + 512], True, True,
               reads=(cb, brow), writes=(bpb,))
        vop(lambda v: v.tensor_copy(rwbc[:, :], PBIG[:, :]), reads=(bpb,), writes=(brwbc,))
    bkey = buf("keyrow")
    pend = []
    for c in range(NCH if KPH >= 4 else 0):
        WOP = [PBIG[:, 0:512], PBIG[:, 512:1024]] if c % 2 == 0 else [PZ[0][:, :], PZ[1][:, :]]
        WOB = (bpb,) if c % 2 == 0 else (bpz[0], bpz[1])
        parts = []
        for j in range(8):
            for nh in range(2):
                parts.append((WOP[nh], attnT[j][:, 128 * c:128 * (c + 1)],
                              woc[j][:, 512 * nh:512 * (nh + 1)], j == 0, False))
        for nh in range(2):
            parts.append((WOP[nh], ones1[0:1, :],
                          biasrow[0:1, 512 * nh:512 * (nh + 1)], False, True))
        mm_raw(parts, reads=[cb, bbias] + batnT + bwoc, writes=WOB)
        for fn in pend:
            fn()
        pend = []
        dma("y", stg[1][:, :], xn_scr[128 * c:128 * (c + 1), :], reads=(bxn_s[c],),
            writes=(bstg[1],))
        if c % 2 == 0:
            vop(lambda v: v.scalar_tensor_tensor(stg[2][:, :], PBIG[:, :], 1.0, stg[1][:, :],
                                                 ALU.mult, ALU.add),
                reads=(bpb, bstg[1]), writes=(bstg[2],))
        else:
            vop(lambda v: v.scalar_tensor_tensor(stg[2][:, 0:512], PZ[0][:, :], 1.0,
                                                 stg[1][:, 0:512], ALU.mult, ALU.add),
                reads=(bpz[0], bstg[1]), writes=(bstg[2],))
            vop(lambda v: v.scalar_tensor_tensor(stg[2][:, 512:1024], PZ[1][:, :], 1.0,
                                                 stg[1][:, 512:1024], ALU.mult, ALU.add),
                reads=(bpz[1], bstg[1]), writes=(bstg[2],))
        ln_stats(stg[2][:, :], bstg[2])
        vop(lambda v: v.tensor_scalar(stg[2][:, :], stg[2][:, :], stat[:, 0:1], stat[:, 1:2],
                                      ALU.subtract, ALU.mult),
            reads=(bstg[2], bstat), writes=(bstg[2],))
        vop(lambda v: v.tensor_mul(stg[2][:, :], stg[2][:, :], gbc_g[:, :]),
            reads=(bstg[2], bgbc), writes=(bstg[2],))
        vop(lambda v: v.tensor_add(stg[2][:, :], stg[2][:, :], gbc_b[:, :]),
            reads=(bstg[2], bgbc), writes=(bstg[2],))
        dma("y", x1_scr[128 * c:128 * (c + 1), :], stg[2][:, :], reads=(bstg[2],),
            writes=(bx1_s[c],))
        vop(lambda v: v.tensor_mul(sq_scr[:, :], stg[2][:, :], rwbc[:, :]),
            reads=(bstg[2], brwbc), writes=(bsq,))
        vop(lambda v: v.reduce_sum(stat[:, 5:6], sq_scr[:, :], AX.X),
            reads=(bsq,), writes=(bstat,))
        act(nbf[0][:, :], stg[2][:, :], AF.Copy, reads=(bstg[2],), writes=(bstg[0],))

        def mk_pend(c):
            def fl():
                trans(PS6[0:1, 32:160], stat[:, 5:6], identf[:, :], reads=(bstat, cb),
                      writes=(bp6,))
                vop(lambda v, c=c: v.tensor_copy(keyrow[0:1, 128 * c:128 * (c + 1)],
                                                 PS6[0:1, 32:160]),
                    reads=(bp6,), writes=(bkey,))
                fm_transposes(nbf[0], bstg[0], c)
            return [fl]

        pend = mk_pend(c)
    for fn in pend:
        fn()

    P.barrier()

    bt5a = buf("t512a")
    bt5b = buf("t512b")
    bmT = buf("masksT")
    bmbc = buf("maskbc")
    bhm = buf("hmsk")
    bhs = [buf(f"hsb{m}") for m in range(16)]
    bw6 = [buf(f"wstp6{i}") for i in range(4)]
    for ns in range(4 if KPH >= 5 else 0):
        bt5c = buf("t512c")
        vop(lambda v, ns=ns: v.tensor_scalar(t512a[0:1, :], keyrow[0:1, 512 * ns:512 * (ns + 1)],
                                             rb_t[0:1, 0:1], 64.0, ALU.add, ALU.add),
            reads=(bkey, vb), writes=(bt5a,))
        for dstep in (64.0, 32.0, 16.0, 8.0, 4.0):
            vop(lambda v, d=dstep: v.tensor_scalar(t512b[0:1, :], t512a[0:1, :], d, -d,
                                                   ALU.is_ge, ALU.mult),
                reads=(bt5a,), writes=(bt5b,))
            vop(lambda v: v.tensor_add(t512a[0:1, :], t512a[0:1, :], t512b[0:1, :]),
                reads=(bt5a, bt5b), writes=(bt5a,))
        vop(lambda v: v.tensor_scalar(t512b[0:1, :], t512a[0:1, :], 1.0, None, ALU.is_ge),
            reads=(bt5a,), writes=(bt5b,))
        vop(lambda v: v.tensor_scalar(t512c[0:1, :], t512a[0:1, :], 2.0, None, ALU.is_ge),
            reads=(bt5a,), writes=(bt5c,))
        vop(lambda v: v.tensor_add(t512b[0:1, :], t512b[0:1, :], t512c[0:1, :]),
            reads=(bt5b, bt5c), writes=(bt5b,))
        vop(lambda v: v.tensor_scalar(t512c[0:1, :], t512a[0:1, :], 3.0, None, ALU.is_ge),
            reads=(bt5a,), writes=(bt5c,))
        vop(lambda v, ns=ns: v.tensor_add(keyrow[0:1, 512 * ns:512 * (ns + 1)], t512b[0:1, :],
                                          t512c[0:1, :]),
            reads=(bt5b, bt5c), writes=(bkey,))
    for ns in range(4 if KPH >= 5 else 0):
        mm(PZ[1][0:4, :], ones1[0:1, 0:4], keyrow[0:1, 512 * ns:512 * (ns + 1)], True, True,
           reads=(cb, bkey), writes=(bpz[1],))
        vop(lambda v, ns=ns: v.tensor_scalar(masksT[:, 512 * ns:512 * (ns + 1)], PZ[1][0:4, :],
                                             cvec4[:, :], None, ALU.is_equal),
            reads=(bpz[1], cb), writes=(bmT,))

    for ex in range(NEXP if KPH >= 5 else 0):
        for ns in range(4):
            vop(lambda v, ns=ns, ex=ex: v.tensor_scalar(
                    t512a[0:1, :], keyrow[0:1, 512 * ns:512 * (ns + 1)], float(ex), None,
                    ALU.is_equal),
                reads=(bkey,), writes=(bt5a,))
            mm(PZ[1][:, :], ones1[0:1, :], t512a[0:1, :], True, True,
               reads=(cb, bt5a), writes=(bpz[1],))
            vop(lambda v, ns=ns: v.tensor_copy(maskbc[:, 512 * ns:512 * (ns + 1)], PZ[1][:, :]),
                reads=(bpz[1],), writes=(bmbc,))
        for half in range(2):
            for mj in range(16):
                wr, wbr = wstp6[mj % 4], bw6[mj % 4]
                ffc = 2048 * half + 128 * mj
                dma("g", bass.AP(wr, 0, [[1024, 128], [128, 8], [1, 128]]),
                    w1[ex, :, ffc:ffc + 128].rearrange("(c p) n -> p c n", p=128), writes=(wbr,))
                if mj % 2 == 0:
                    pset = [PBIG[:, 0:512], PBIG[:, 512:1024], PZ[0][:, :], PZ[1][:, :]]
                    pbufs = [bpb, bpb, bpz[0], bpz[1]]
                else:
                    pset = [PX[0][:, :], PX[1][:, :], PS6[:, :], PY[:, :]]
                    pbufs = [bpx[0], bpx[1], bp6, bpy]
                parts = []
                for fi in range(8):
                    for ns in range(4):
                        parts.append((pset[ns], bass.AP(wr, 128 * fi, [[1024, 128], [1, 128]]),
                                      fm[fi][:, 512 * ns:512 * (ns + 1)], fi == 0, fi == 7))
                mm_raw(parts, reads=[wbr] + bfm, writes=tuple(set(pbufs)))
                bcol = b1_col[:, 32 * ex + 16 * half + mj:32 * ex + 16 * half + mj + 1]
                for ns in range(4):
                    act(hmsk[:, 512 * ns:512 * (ns + 1)], pset[ns], AF.Relu, bias=bcol,
                        reads=(pbufs[ns], vb), writes=(bhm,))
                vop(lambda v, mj=mj: v.tensor_mul(hsb[mj][:, :], hmsk[:, :], maskbc[:, :]),
                    reads=(bhm, bmbc), writes=(bhs[mj],))
                dma("g", rdb[mj][:, :],
                    w2[ex, 2048 * half + 128 * mj:2048 * half + 128 * (mj + 1), :],
                    writes=(brdb[mj],))
            for tt in range(16):
                PP = PX if tt % 2 == 0 else PZ
                PB_ = bpx if tt % 2 == 0 else bpz
                parts = []
                for mj in range(16):
                    for s2 in range(2):
                        last = (mj == 15) and not (ex == 0 and half == 0)
                        parts.append((PP[s2][:, :], hsb[mj][:, 128 * tt:128 * (tt + 1)],
                                      rdb[mj][:, 512 * s2:512 * (s2 + 1)], mj == 0, last))
                if ex == 0 and half == 0:
                    for s2 in range(2):
                        parts.append((PP[s2][:, :], masksT[:, 128 * tt:128 * (tt + 1)],
                                      b2_sb[:, 512 * s2:512 * (s2 + 1)], False, True))
                mm_raw(parts, reads=[vb, bmT] + bhs + brdb, writes=(PB_[0], PB_[1]))
                sst, bsst = stg[tt % 2], bstg[tt % 2]
                vop(lambda v, sst=sst, PP=PP: v.tensor_copy(sst[:, 0:512], PP[0][:, :]),
                    reads=(PB_[0],), writes=(bsst,))
                vop(lambda v, sst=sst, PP=PP: v.tensor_copy(sst[:, 512:1024], PP[1][:, :]),
                    reads=(PB_[1],), writes=(bsst,))
                dma("g", x1_scr[128 * tt:128 * (tt + 1), :], sst[:, :], reads=(bsst,),
                    writes=(bx1_s[tt],), accum_op=ALU.add)

    P.barrier()

    build_bc(gbc_g, gbc_b, l2g, l2b)
    for c in range(NCH if KPH >= 6 else 0):
        dma("y", stg[0][:, :], x1_scr[128 * c:128 * (c + 1), :], reads=(bx1_s[c],),
            writes=(bstg[0],))
        dma("y", stg[1][:, :], part_scr[128 * c:128 * (c + 1), :], reads=(bpart_s[c],),
            writes=(bstg[1],))
        ln_stats(stg[0][:, :], bstg[0])
        vop(lambda v: v.tensor_scalar(stg[2][:, :], stg[0][:, :], stat[:, 0:1], stat[:, 1:2],
                                      ALU.subtract, ALU.mult),
            reads=(bstg[0], bstat), writes=(bstg[2],))
        vop(lambda v: v.tensor_mul(stg[2][:, :], stg[2][:, :], gbc_g[:, :]),
            reads=(bstg[2], bgbc), writes=(bstg[2],))
        vop(lambda v: v.tensor_add(stg[2][:, :], stg[2][:, :], gbc_b[:, :]),
            reads=(bstg[2], bgbc), writes=(bstg[2],))
        vop(lambda v: v.tensor_add(stg[2][:, :], stg[2][:, :], stg[1][:, :]),
            reads=(bstg[2], bstg[1]), writes=(bstg[2],))
        dma("y", out_hbm[128 * c:128 * (c + 1), :], stg[2][:, :], reads=(bstg[2],))

    for q in DMAQ:
        for i in range(NLANE):
            lane = f"{q}{i}"
            P._wait(q, lane, P.cnt[lane])

    global _LASTP
    _LASTP = P
    with nc.Block() as block:
        block.tensor(lambda eng: _replay(P, "t", eng, sems))
        block.vector(lambda eng: _replay(P, "v", eng, sems))
        block.scalar(lambda eng: _replay(P, "s", eng, sems))
        block.gpsimd(lambda eng: _replay(P, "g", eng, sems))
        block.sync(lambda eng: _replay(P, "y", eng, sems))

    ctx.close()
    return nc


def kernel(**inputs):
    nc = build_program()
    consts = host_consts()
    x = np.ascontiguousarray(np.asarray(inputs["L0_attn_input_tensor"], dtype=np.float32))
    names = {
        "w_in": "L0_attn_W_in", "b_in": "L0_attn_b_in", "a_log": "L0_attn_A_log",
        "d_skip": "L0_attn_D_skip", "dt_bias": "L0_attn_dt_bias", "w_out": "L0_attn_W_out",
        "b_out": "L0_attn_b_out", "ln_g": "L0_attn_ln_gamma", "ln_b": "L0_attn_ln_beta",
        "wq": "L0_ffn_Wq", "bq": "L0_ffn_bq", "wk": "L0_ffn_Wk", "bk": "L0_ffn_bk",
        "wv": "L0_ffn_Wv", "bv": "L0_ffn_bv", "wo": "L0_ffn_Wo", "bo": "L0_ffn_bo",
        "rw": "L0_ffn_router_w", "rb": "L0_ffn_router_b", "w1": "L0_ffn_W1", "b1": "L0_ffn_b1",
        "w2": "L0_ffn_W2", "b2": "L0_ffn_b2", "l1g": "L0_ffn_ln1_g", "l1b": "L0_ffn_ln1_b",
        "l2g": "L0_ffn_ln2_g", "l2b": "L0_ffn_ln2_b",
    }
    base = {k: np.ascontiguousarray(np.asarray(inputs[v], np.float32)) for k, v in names.items()}
    base.update(consts)
    in_maps = []
    for c in range(8):
        m = dict(base)
        m["x_in"] = np.ascontiguousarray(x[c])
        in_maps.append(m)
    res = run_bass_kernel_spmd(nc, in_maps, core_ids=list(range(8)))
    return np.stack([res.results[c]["out"] for c in range(8)], axis=0)


if __name__ == "__main__":
    print("building program...")
    nc = build_program()
    print("built ok")
